# version 01
# baseline (speedup 1.0000x reference)
import math
import numpy as np
import concourse.bass as bass
import concourse.mybir as mybir
from concourse.alu_op_type import AluOpType as OP
from contextlib import ExitStack

F32, BF16, I32 = mybir.dt.float32, mybir.dt.bfloat16, mybir.dt.int32
AF = mybir.ActivationFunctionType

D, T, NCH, TW = 4096, 2048, 32, 512
NH, NKV, HD = 32, 8, 128
EXPC = 10.0
SM_SCALE = 1.0 / math.sqrt(HD)
EPS = 1e-5


def host_constants():
    import ml_dtypes
    inv_freq = 1.0 / (10000.0 ** (np.arange(0, HD, 2, dtype=np.float64) / HD))
    ang = np.arange(T, dtype=np.float64)[:, None] * inv_freq[None, :]
    cos, sin = np.cos(ang).T, np.sin(ang).T
    kk = np.arange(128)[:, None]
    jj = np.arange(896)[None, :]
    return {
        "cosbase": np.concatenate([cos, cos], 0).astype(ml_dtypes.bfloat16),
        "sinbase": np.concatenate([-sin, sin], 0).astype(ml_dtypes.bfloat16),
        "maskm": (kk + 384 <= jj).astype(np.float32).astype(ml_dtypes.bfloat16),
        "ident": np.eye(128, dtype=np.float32).astype(ml_dtypes.bfloat16),
    }


_unctr = {}
def _un(nm):
    _unctr[nm] = _unctr.get(nm, 0) + 1
    return f"{nm}_{_unctr[nm]}"


def build_program(stop_after="all", debug_taps=()):
    _unctr.clear()
    nc = bass.Bass("TRN2", target_bir_lowering=False)
    taps = set(debug_taps)

    x_in = nc.dram_tensor("x_fm", [D, T], F32, kind="ExternalInput").ap()
    qw = {
        "q": nc.dram_tensor("qw_q", [D, 4096], I32, kind="ExternalInput").ap(),
        "k": nc.dram_tensor("qw_k", [D, 1024], I32, kind="ExternalInput").ap(),
        "v": nc.dram_tensor("qw_v", [D, 1024], I32, kind="ExternalInput").ap(),
        "o": nc.dram_tensor("qw_o", [4096, D], I32, kind="ExternalInput").ap(),
    }
    sc = {
        "q": nc.dram_tensor("sc_q", [32, 4096], F32, kind="ExternalInput").ap(),
        "k": nc.dram_tensor("sc_k", [32, 1024], F32, kind="ExternalInput").ap(),
        "v": nc.dram_tensor("sc_v", [32, 1024], F32, kind="ExternalInput").ap(),
        "o": nc.dram_tensor("sc_o", [32, 4096], F32, kind="ExternalInput").ap(),
    }
    w_in = nc.dram_tensor("w_in", [6144, D], F32, kind="ExternalInput").ap()
    w_out = nc.dram_tensor("w_out", [D, D], F32, kind="ExternalInput").ap()
    fq = nc.dram_tensor("fq", [D, D], I32, kind="ExternalInput").ap()
    fsc = nc.dram_tensor("fsc", [D, 32], F32, kind="ExternalInput").ap()
    fzp = nc.dram_tensor("fzp", [D, 32], F32, kind="ExternalInput").ap()
    fbias = nc.dram_tensor("fbias", [1, D], F32, kind="ExternalInput").ap()
    cosb_h = nc.dram_tensor("cosbase", [128, T], BF16, kind="ExternalInput").ap()
    sinb_h = nc.dram_tensor("sinbase", [128, T], BF16, kind="ExternalInput").ap()
    maskm = nc.dram_tensor("maskm", [128, 896], BF16, kind="ExternalInput").ap()
    ident_h = nc.dram_tensor("ident", [128, 128], BF16, kind="ExternalInput").ap()
    y_out = nc.dram_tensor("y", [D, T], F32, kind="ExternalOutput").ap()

    x_home = nc.dram_tensor("x_home", [NCH, 128, T], BF16, kind="Internal").ap()
    qspill = nc.dram_tensor("qspill", [NH, 128, T], BF16, kind="Internal").ap()
    qcache = nc.dram_tensor("qcache", [NH, 128, D], BF16, kind="Internal").ap()
    kcache = nc.dram_tensor("kcache", [NKV, 128, D], BF16, kind="Internal").ap()
    vcache = nc.dram_tensor("vcache", [128, 2, NCH, 512], BF16, kind="Internal").ap()
    fcache = nc.dram_tensor("fcache", [NCH, 128, D], BF16, kind="Internal").ap()
    bcache = nc.dram_tensor("bcache", [6144, D], BF16, kind="Internal").ap()
    ocache = nc.dram_tensor("ocache", [D, D], BF16, kind="Internal").ap()
    vbcache = nc.dram_tensor("vbcache", [128, 2, NCH, 512], BF16, kind="Internal").ap()
    wsums = nc.dram_tensor("wsums", [1, 16384], BF16, kind="Internal").ap()

    dbg = {}

    def mkdbg(name, shape, dtype=F32):
        if name not in dbg:
            dbg[name] = nc.dram_tensor("dbg_" + name, shape, dtype, kind="ExternalOutput").ap()
        return dbg[name]

    es = ExitStack()
    with es:
        sbt = lambda name, shape, dtype: es.enter_context(nc.sbuf_tensor(name, shape, dtype))
        A = sbt("A", [128, NCH * T], BF16)
        B = sbt("B", [128, 16384], BF16)
        C = sbt("C", [128, 16384], BF16)
        ones_row = sbt("ones_row", [1, 128], BF16)
        ones_col = sbt("ones_col", [128, 1], BF16)
        cb_negC = sbt("cb_negC", [128, 1], F32)
        cb_eps = sbt("cb_eps", [128, 1], F32)
        ident = sbt("ident_s", [128, 128], BF16)
        rstdcols = sbt("rstdcols", [128, 16], F32)
        biascols = sbt("biascols", [128, 32], F32)

        PS8 = [es.enter_context(nc.psum_tensor(f"gps{i}", [128, 512], F32)) for i in range(8)]
        PSB = [p[:].bitcast(BF16) for p in PS8]
        SEM = {}
        cnt = {}
        for k in ["GLD", "SLD", "ALD", "SST", "PE1", "PE2", "DV", "AC",
                  "XLD", "SC0", "SC1", "BD0", "BD1", "QH0", "QH1", "QS0", "QS1",
                  "STG", "FF0", "FF1"]:
            SEM[k] = es.enter_context(nc.semaphore(k))
            cnt[k] = 0

        def dma(engs, key, out, in_):
            engs.dma_start(out, in_).then_inc(SEM[key], 16)
            cnt[key] += 16
            return cnt[key]

        def dma_t(engs, key, out, in_):
            engs.dma_start_transpose(out, in_).then_inc(SEM[key], 16)
            cnt[key] += 16
            return cnt[key]

        def inc(instr, key, n=1):
            instr.then_inc(SEM[key], n)
            cnt[key] += n
            return cnt[key]

        def AC_(c, t0=0, ww=T):
            return A[:, c * T + t0:c * T + t0 + ww]

        C_cos = C[:, 0:2048]
        C_sin = C[:, 2048:4096]
        C_qsm = [C[:, 4096:4608], C[:, 4608:5120]]
        C_rt1 = C[:, 5120:5632]
        C_rt2 = C[:, 5632:6144]
        C_band = [C[:, 6144 + i * 4096:6144 + (i + 1) * 4096]
                  .rearrange("p (c f) -> p c f", f=128) for i in range(2)]
        C_strow = C[0:1, 0:8192].bitcast(F32)
        C_sq = C[:, 8192:10240]
        C_rstdrow = C[0:1, 10240:12288]
        C_meanrow = C[0:1, 14336:16384]
        C_deq = C.rearrange("p (c f) -> p c f", f=512)
        C_rstdb = C[:, 2048:4096]
        B_stage = B.rearrange("p (c f) -> p c f", f=512)
        B_stage4 = B.rearrange("p (j f) -> p j f", f=D)
        B_band = [B[:, i * 4096:(i + 1) * 4096].rearrange("p (c f) -> p c f", f=128)
                  for i in range(2)]
        B_xold = [B[:, 8192 + i * 2048:8192 + (i + 1) * 2048] for i in range(3)]
        B_stgo2 = [B[:, 8192 + i * 2048:8192 + (i + 1) * 2048] for i in range(2)]
        B_stgf2 = [B[:, 12288 + i * 1024:12288 + (i + 1) * 1024].bitcast(F32) for i in range(2)]
        B_stgo3 = [B[:, 8192 + i * 4096:8192 + (i + 1) * 4096].bitcast(F32) for i in range(2)]

        with nc.Block() as blk:
            @blk.vector
            def _(v):
                v.memset(ones_row[:], 1.0)
                v.memset(ones_col[:], 1.0)
                v.memset(cb_negC[:], -EXPC)
                v.memset(cb_eps[:], EPS)
            @blk.sync
            def _(s):
                dma(s, "SLD", ident[:], ident_h[:])
            @blk.gpsimd
            def _(g):
                for qd in range(4):
                    dma(g, "XLD",
                        A[:, qd * 8 * T:(qd + 1) * 8 * T].rearrange("p (c t) -> p c t", t=T),
                        x_in[qd * 1024:(qd + 1) * 1024, :].rearrange("(c p) t -> p c t", p=128))
                dma(g, "GLD", bcache[:], w_in[:])
                dma(g, "GLD", ocache[:], w_out[:])
        GLD_X = 4 * 16
        GLD_PRE = cnt["GLD"]
        SLD_CONST = cnt["SLD"]

        def p0_attn_a():
            with ExitStack() as ph:
                gath = ph.enter_context(nc.sbuf_tensor("p0_gath", [128, D], BF16))
                scbc = [ph.enter_context(nc.sbuf_tensor(f"p0_sc{i}", [128, 512], BF16)) for i in range(2)]
                wsrow = ph.enter_context(nc.sbuf_tensor("p0_wsr", [1, 512], BF16))
                pst = [PSB[0][:, 0:128], PSB[1][:, 0:128]]
                ws_ps = PS8[2][0:1, :]
                bands = [("q", b, "t") for b in range(8)] + \
                        [("k", b, "t") for b in range(2)] + \
                        [("v", b, "v") for b in range(2)]
                ws_off = {("q", 0): 0, ("k", 0): 4096, ("v", 0): 5120}
                for bi, (wname, b, kind) in enumerate(bands):
                    col = b * 512
                    with nc.Block() as blk:
                        @blk.gpsimd
                        def _(g):
                            g.wait_ge(SEM["DV"], cnt["DV"])
                            for cq in range(4):
                                dma(g, "GLD", B_stage[:, cq * 8:(cq + 1) * 8, :],
                                    qw[wname][cq * 1024:(cq + 1) * 1024, col:col + 512]
                                    .rearrange("(c p) f -> p c f", p=128))
                        @blk.sync
                        def _(s):
                            if bi == 0:
                                s.wait_ge(SEM["SLD"], SLD_CONST)
                            s.wait_ge(SEM["GLD"], cnt["GLD"])
                            s.wait_ge(SEM["SST"], cnt["SST"])
                    d0, p0, a0, s0 = cnt["DV"], cnt["PE1"], cnt["AC"], cnt["SST"]
                    ev_sc = {}
                    with nc.Block() as blk:
                        @blk.gpsimd
                        def _(g):
                            for c in range(NCH):
                                if c >= 2:
                                    g.wait_ge(SEM["DV"], d0 + c - 1)
                                ev_sc[c] = dma(g, "SC%d" % (c % 2), scbc[c % 2][:],
                                    sc[wname][c:c + 1, col:col + 512].partition_broadcast(128).squeeze(1))
                        @blk.vector
                        def _(v):
                            for c in range(NCH):
                                v.wait_ge(SEM["SC%d" % (c % 2)], ev_sc[c])
                                st = v.scalar_tensor_tensor(C_deq[:, c, :], B_stage[:, c, :], -8.0,
                                                            scbc[c % 2][:], op0=OP.add, op1=OP.mult)
                                inc(st, "DV")
                        wsbase = ws_off[(wname, 0)] + b * 512
                        d_ws = cnt["DV"]
                        if kind == "t":
                            @blk.tensor
                            def _(t):
                                for c in range(NCH):
                                    t.wait_ge(SEM["DV"], d0 + c + 1)
                                    mm = t.matmul(ws_ps, ones_col[:], C_deq[:, c, :],
                                                  start=(c == 0), stop=(c == NCH - 1))
                                inc(mm, "PE1")
                                for nl in range(4):
                                    for c in range(NCH):
                                        t.wait_ge(SEM["DV"], d0 + c + 1)
                                        kk = nl * 32 + c
                                        if kk >= 2:
                                            t.wait_ge(SEM["AC"], a0 + kk - 1)
                                        tp = t.transpose(pst[c % 2],
                                                         C_deq[:, c, nl * 128:(nl + 1) * 128], ident[:])
                                        inc(tp, "PE1")
                            @blk.scalar
                            def _(a):
                                for nl in range(4):
                                    if nl > 0:
                                        a.wait_ge(SEM["SST"], s0 + 16 + nl * 16)
                                    for c in range(NCH):
                                        a.wait_ge(SEM["PE1"], p0 + 1 + nl * 32 + c + 1)
                                        cp = a.copy(gath[:, c * 128:(c + 1) * 128], pst[c % 2])
                                        inc(cp, "AC")
                            @blk.vector
                            def _(v):
                                v.wait_ge(SEM["PE1"], p0 + 1)
                                ws = v.tensor_copy(wsrow[:], ws_ps)
                                inc(ws, "DV")
                            @blk.sync
                            def _(s):
                                s.wait_ge(SEM["DV"], cnt["DV"])
                                dma(s, "SST", wsums[0:1, wsbase:wsbase + 512], wsrow[:])
                                for nl in range(4):
                                    s.wait_ge(SEM["AC"], a0 + (nl + 1) * 32)
                                    dst = qcache if wname == "q" else kcache
                                    dma(s, "SST", dst[4 * b + nl], gath[:])
                        else:
                            @blk.tensor
                            def _(t):
                                for c in range(NCH):
                                    t.wait_ge(SEM["DV"], d0 + c + 1)
                                    mm = t.matmul(ws_ps, ones_col[:], C_deq[:, c, :],
                                                  start=(c == 0), stop=(c == NCH - 1))
                                inc(mm, "PE1")
                            @blk.vector
                            def _(v2):
                                v2.wait_ge(SEM["PE1"], p0 + 1)
                                ws = v2.tensor_copy(wsrow[:], ws_ps)
                                inc(ws, "DV")
                            @blk.sync
                            def _(s):
                                s.wait_ge(SEM["DV"], cnt["DV"])
                                dma(s, "SST", wsums[0:1, wsbase:wsbase + 512], wsrow[:])
                                for c in range(NCH):
                                    s.wait_ge(SEM["DV"], d0 + c + 1)
                                    dma(s, "SST", vcache[:, b, c, :], C_deq[:, c, :])

        def p0_ffn():
            with ExitStack() as ph:
                fscb = [ph.enter_context(nc.sbuf_tensor(f"p0_fs{i}", [128, 32], F32)) for i in range(2)]
                fzpb = [ph.enter_context(nc.sbuf_tensor(f"p0_fz{i}", [128, 32], F32)) for i in range(2)]
                wcol32 = ph.enter_context(nc.sbuf_tensor("p0_wc32", [128, 1], F32))
                wcol = ph.enter_context(nc.sbuf_tensor("p0_wc", [128, 2], BF16))
                wrow = ph.enter_context(nc.sbuf_tensor("p0_wr", [1, 128], BF16))
                ws_ps = PS8[2][0:2, 0:128]
                Bs = [B[:, 0:4096], B[:, 4096:8192]]
                Cd = [C[:, 0:4096], C[:, 4096:8192]]
                with nc.Block() as blk:
                    @blk.sync
                    def _(s):
                        s.wait_ge(SEM["SST"], cnt["SST"])
                g0, d0, s0 = cnt["GLD"], cnt["DV"], cnt["SST"]
                p00, a00, sst00 = cnt["PE1"], cnt["AC"], cnt["SST"]
                ev_ff = {}
                with nc.Block() as blk:
                    @blk.gpsimd
                    def _(g):
                        for n in range(NCH):
                            if n >= 2:
                                g.wait_ge(SEM["DV"], d0 + 2 * (n - 2) + 1)
                            dma(g, "FF%d" % (n % 2), Bs[n % 2], fq[n * 128:(n + 1) * 128, :])
                            dma(g, "FF%d" % (n % 2), fscb[n % 2][:], fsc[n * 128:(n + 1) * 128, :])
                            ev_ff[n] = dma(g, "FF%d" % (n % 2), fzpb[n % 2][:], fzp[n * 128:(n + 1) * 128, :])
                    @blk.vector
                    def _(v):
                        for n in range(NCH):
                            v.wait_ge(SEM["FF%d" % (n % 2)], ev_ff[n])
                            if n >= 2:
                                v.wait_ge(SEM["SST"], s0 + (n - 1) * 32)
                            ts = None
                            for gg in range(32):
                                ts = v.tensor_scalar(Cd[n % 2][:, gg * 128:(gg + 1) * 128],
                                                     Bs[n % 2][:, gg * 128:(gg + 1) * 128],
                                                     fscb[n % 2][:, gg:gg + 1], fzpb[n % 2][:, gg:gg + 1],
                                                     op0=OP.mult, op1=OP.add)
                            inc(ts, "DV")
                            if n >= 1:
                                v.wait_ge(SEM["AC"], a00 + 3 * (n - 1) + 1)
                            rd = v.tensor_reduce(wcol32[:], Cd[n % 2], axis=mybir.AxisListType.X,
                                                 op=OP.add)
                            inc(rd, "DV")
                    @blk.scalar
                    def _(a):
                        for n in range(NCH):
                            a.wait_ge(SEM["DV"], d0 + 2 * n + 2)
                            if n >= 1:
                                a.wait_ge(SEM["PE1"], p00 + n)
                            cp = a.copy(wcol[:, 0:1], wcol32[:])
                            inc(cp, "AC")
                            z = a.activation(wcol[:, 1:2], wcol32[:], AF.Copy, bias=0.0, scale=0.0)
                            inc(z, "AC")
                            a.wait_ge(SEM["PE1"], p00 + n + 1)
                            if n >= 1:
                                a.wait_ge(SEM["SST"], sst00 + n * 32)
                            cp2 = a.copy(wrow[:], ws_ps[0:1, :])
                            inc(cp2, "AC")
                    @blk.tensor
                    def _(t):
                        for n in range(NCH):
                            t.wait_ge(SEM["AC"], a00 + 3 * n + 2)
                            mm = t.matmul(ws_ps, wcol[:], ident[:], start=True, stop=True)
                            inc(mm, "PE1")
                    @blk.sync
                    def _(s):
                        for n in range(NCH):
                            s.wait_ge(SEM["DV"], d0 + 2 * n + 1)
                            dma(s, "SST", fcache[n], Cd[n % 2])
                            s.wait_ge(SEM["AC"], a00 + 3 * n + 3)
                            dma(s, "SST", wsums[0:1, 6144 + n * 128:6144 + (n + 1) * 128], wrow[:])
                if "p0ffn" in taps:
                    dcol = mkdbg("ffncol", [128, 1])
                    dcol2 = mkdbg("ffncol2", [128, 2])
                    drow = mkdbg("ffnrow", [1, 128])
                    with nc.Block() as blk2:
                        @blk2.gpsimd
                        def _(g):
                            dma(g, "GLD", dcol[:], wcol32[:])
                            dma(g, "GLD", dcol2[:], wcol[:])
                            dma(g, "GLD", drow[:], wrow[:])

        def p0_vb():
            with ExitStack() as ph:
                pst = [PSB[0][:, 0:128], PSB[1][:, 0:128]]
                for dvh in range(2):
                    with nc.Block() as blk:
                        @blk.sync
                        def _(s):
                            s.wait_ge(SEM["GLD"], cnt["GLD"])
                            s.wait_ge(SEM["SST"], cnt["SST"])
                            dma(s, "SLD", B_stage4[:, :, :],
                                bcache[5120 + dvh * 512:5120 + (dvh + 1) * 512, :]
                                .rearrange("(j p) e -> p j e", p=128))
                    p0, a0, s0 = cnt["PE1"], cnt["AC"], cnt["SLD"]
                    with nc.Block() as blk:
                        @blk.tensor
                        def _(t):
                            t.wait_ge(SEM["SLD"], s0)
                            for jj in range(4):
                                for c in range(NCH):
                                    kk = jj * 32 + c
                                    if kk >= 2:
                                        t.wait_ge(SEM["AC"], a0 + kk - 1)
                                    tp = t.transpose(pst[c % 2],
                                                     B_stage4[:, jj, c * 128:(c + 1) * 128], ident[:])
                                    inc(tp, "PE1")
                        @blk.scalar
                        def _(a):
                            for jj in range(4):
                                for c in range(NCH):
                                    a.wait_ge(SEM["PE1"], p0 + jj * 32 + c + 1)
                                    cp = a.copy(C[:, c * 512 + jj * 128:c * 512 + (jj + 1) * 128],
                                                pst[c % 2])
                                    inc(cp, "AC")
                        @blk.sync
                        def _(s):
                            s.wait_ge(SEM["AC"], a0 + 128)
                            dma(s, "SST", vbcache[:, dvh, :, :], C_deq[:, :, :])

        def p0_bsum():
            with ExitStack() as ph:
                wcol32 = ph.enter_context(nc.sbuf_tensor("p0d_wc32", [128, 1], F32))
                wcol = ph.enter_context(nc.sbuf_tensor("p0d_wc", [128, 2], BF16))
                wrow = ph.enter_context(nc.sbuf_tensor("p0d_wr", [1, 128], BF16))
                ws_ps = PS8[2][0:2, 0:128]
                stg = B[:, 0:4096]
                d0, p0, a0 = cnt["DV"], cnt["PE1"], cnt["AC"]
                sst_b0 = cnt["SST"]
                ev_st = {}
                with nc.Block() as blk:
                    @blk.sync
                    def _(s):
                        s.wait_ge(SEM["SST"], cnt["SST"])
                        s.wait_ge(SEM["GLD"], cnt["GLD"])
                        for nb in range(48):
                            if nb >= 1:
                                s.wait_ge(SEM["DV"], d0 + nb)
                            ev_st[nb] = dma(s, "STG", stg, bcache[nb * 128:(nb + 1) * 128, :])
                            s.wait_ge(SEM["AC"], a0 + 3 * nb + 3)
                            dma(s, "SST", wsums[0:1, 10240 + nb * 128:10240 + (nb + 1) * 128],
                                wrow[:])
                    @blk.vector
                    def _(v):
                        for nb in range(48):
                            v.wait_ge(SEM["STG"], ev_st[nb])
                            if nb >= 1:
                                v.wait_ge(SEM["AC"], a0 + 3 * (nb - 1) + 1)
                            rd = v.tensor_reduce(wcol32[:], stg, axis=mybir.AxisListType.X, op=OP.add)
                            inc(rd, "DV")
                    @blk.scalar
                    def _(a):
                        for nb in range(48):
                            a.wait_ge(SEM["DV"], d0 + nb + 1)
                            if nb >= 1:
                                a.wait_ge(SEM["PE1"], p0 + nb)
                            cp = a.copy(wcol[:, 0:1], wcol32[:])
                            inc(cp, "AC")
                            z = a.activation(wcol[:, 1:2], wcol32[:], AF.Copy, bias=0.0, scale=0.0)
                            inc(z, "AC")
                            a.wait_ge(SEM["PE1"], p0 + nb + 1)
                            if nb >= 1:
                                a.wait_ge(SEM["SST"], sst_b0 + nb * 16)
                            cp2 = a.copy(wrow[:], ws_ps[0:1, :])
                            inc(cp2, "AC")
                    @blk.tensor
                    def _(t):
                        for nb in range(48):
                            t.wait_ge(SEM["AC"], a0 + 3 * nb + 2)
                            mm = t.matmul(ws_ps, wcol[:], ident[:], start=True, stop=True)
                            inc(mm, "PE1")

        def phase_stats(first, ffn):
            with ExitStack() as ph:
                sums = [PS8[i][0:1, :] for i in range(4)]
                sqs = [PS8[4 + i][0:1, :] for i in range(4)]
                sqb = [C[:, 8192:10240], C[:, 10240:12288]]
                if not first:
                    with nc.Block() as blk:
                        @blk.sync
                        def _(s):
                            s.wait_ge(SEM["SST"], cnt["SST"])
                            for qd in range(4):
                                dma(s, "SLD",
                                    A[:, qd * 8 * T:(qd + 1) * 8 * T].rearrange("p (c t) -> p c t", t=T),
                                    x_home[qd * 8:(qd + 1) * 8].rearrange("c p t -> p c t"))
                d0, p0 = cnt["DV"], cnt["PE1"]
                sld_x = cnt["SLD"]
                with nc.Block() as blk:
                    @blk.vector
                    def _(v):
                        if first:
                            v.wait_ge(SEM["XLD"], GLD_X)
                        else:
                            v.wait_ge(SEM["SLD"], sld_x)
                        for c in range(NCH):
                            if c >= 2:
                                v.wait_ge(SEM["PE1"], p0 + c - 1)
                            tt = v.tensor_tensor(sqb[c % 2][:], AC_(c), AC_(c), op=OP.mult)
                            inc(tt, "DV")
                    @blk.tensor
                    def _(t):
                        for c in range(NCH):
                            t.wait_ge(SEM["DV"], d0 + c + 1)
                            mm = None
                            for ti in range(4):
                                t.matmul(sums[ti], ones_col[:], AC_(c, ti * TW, TW),
                                         start=(c == 0), stop=(c == NCH - 1))
                                mm = t.matmul(sqs[ti], ones_col[:], sqb[c % 2][:, ti * TW:(ti + 1) * TW],
                                              start=(c == 0), stop=(c == NCH - 1))
                            inc(mm, "PE1")
                pe_end = cnt["PE1"]
                d1 = cnt["DV"]
                with nc.Block() as blk:
                    @blk.vector
                    def _(v):
                        v.wait_ge(SEM["PE1"], pe_end)
                        for ti in range(4):
                            sl = slice(ti * TW, (ti + 1) * TW)
                            r_mean = C_strow[:, ti * 512:ti * 512 + 512]
                            r_var = C_strow[:, 2048 + ti * 512:2048 + ti * 512 + 512]
                            r_tmp = C[0:1, 8192:9216].bitcast(F32)
                            v.tensor_scalar(r_mean, sums[ti], 1.0 / D, None, op0=OP.mult)
                            v.tensor_scalar(r_var, sqs[ti], 1.0 / D, None, op0=OP.mult)
                            v.scalar_tensor_tensor(r_tmp, r_mean, -1.0, r_mean, op0=OP.mult, op1=OP.mult)
                            v.tensor_tensor(r_var, r_var, r_tmp, op=OP.add)
                            tc = v.tensor_scalar(C_meanrow[:, sl], r_mean, -1.0, None, op0=OP.mult)
                            inc(tc, "DV")
                    @blk.scalar
                    def _(a):
                        for ti in range(4):
                            sl = slice(ti * TW, (ti + 1) * TW)
                            a.wait_ge(SEM["DV"], d1 + ti + 1)
                            r_var = C_strow[:, 2048 + ti * 512:2048 + ti * 512 + 512]
                            a.activation(r_var, r_var, AF.Ln, bias=cb_eps[0:1, :], scale=1.0)
                            ac = a.activation(C_rstdrow[:, sl], r_var, AF.Exp, bias=0.0, scale=-0.5)
                            inc(ac, "AC")
                a1 = cnt["AC"]
                with nc.Block() as blk:
                    if not ffn:
                        pcol = None
                        p1 = cnt["PE1"]
                        pcol_ps = PS8[0][:, 0:16]
                        @blk.tensor
                        def _(t):
                            t.wait_ge(SEM["AC"], a1)
                            mm = None
                            for tau in range(16):
                                mm = t.matmul(pcol_ps[:, tau:tau + 1],
                                              C_rstdrow[:, tau * 128:(tau + 1) * 128],
                                              ones_row[0:1, 0:1], start=True, stop=True)
                            inc(mm, "PE1")
                        @blk.vector
                        def _(v):
                            v.wait_ge(SEM["PE1"], p1 + 1)
                            tc = v.tensor_copy(rstdcols[:], pcol_ps)
                            inc(tc, "DV")
                    else:
                        pass


        def phase_tables(ws_loads):
            with ExitStack() as ph:
                pb = [PS8[0], PS8[1]]
                s0 = cnt["SLD"]
                with nc.Block() as blk:
                    @blk.sync
                    def _(s):
                        dma(s, "SLD", C_cos, cosb_h[:])
                        dma(s, "SLD", C_sin, sinb_h[:])
                        for dst, srcsl in ws_loads:
                            dma(s, "SLD", dst, srcsl)
                sld_all = cnt["SLD"]
                p1, d0 = cnt["PE1"], cnt["DV"]
                with nc.Block() as blk:
                    @blk.tensor
                    def _(t):
                        for i in range(8):
                            ti = i % 4
                            if i >= 2:
                                t.wait_ge(SEM["DV"], d0 + i - 1)
                            mm = t.matmul(pb[i % 2][:], ones_row[:],
                                          C_rstdrow[:, ti * TW:(ti + 1) * TW],
                                          start=True, stop=True)
                            inc(mm, "PE1")
                    @blk.vector
                    def _(v):
                        v.wait_ge(SEM["SLD"], sld_all)
                        for i in range(8):
                            ti = i % 4
                            v.wait_ge(SEM["PE1"], p1 + i + 1)
                            if i < 4:
                                tgt = C_cos[:, ti * TW:(ti + 1) * TW]
                            else:
                                tgt = C_sin[:, ti * TW:(ti + 1) * TW]
                            tt = v.tensor_tensor(tgt, pb[i % 2][:], tgt, op=OP.mult)
                            inc(tt, "DV")

        def gemm_T(name, cache_view, n_list, band_bufs, epilogue, dv_per_n,
                   sst_hook=None, rank1=None, pre_dv_wait=None):
            NL = len(n_list)
            p0, d0 = cnt["PE1"], cnt["DV"]
            sst_pre = cnt["SST"]
            ev_bd = {}
            with nc.Block() as blk:
                @blk.scalar
                def _(a):
                    a.wait_ge(SEM["SST"], sst_pre)
                    for i in range(NL):
                        if i >= 2:
                            a.wait_ge(SEM["PE1"], p0 + i - 1)
                        ev_bd[i] = dma_t(a, "BD%d" % (i % 2), band_bufs[i % 2][:], cache_view[n_list[i]])
                @blk.tensor
                def _(t):
                    for i, n in enumerate(n_list):
                        t.wait_ge(SEM["BD%d" % (i % 2)], ev_bd[i])
                        if i >= 2:
                            t.wait_ge(SEM["DV"], d0 + dv_per_n * (i - 1))
                        ps4 = PS8[(i % 2) * 4:(i % 2) * 4 + 4]
                        mm = None
                        for c in range(NCH):
                            for ti in range(4):
                                mm = t.matmul(ps4[ti][:], band_bufs[i % 2][:, c, :],
                                              AC_(c, ti * TW, TW),
                                              start=(c == 0), stop=(c == NCH - 1 and rank1 is None))
                        if rank1 is not None:
                            for ti in range(4):
                                mm = rank1(t, i, n, ps4[ti], ti)
                        inc(mm, "PE1")
                @blk.vector
                def _(v):
                    for i, n in enumerate(n_list):
                        v.wait_ge(SEM["PE1"], p0 + i + 1)
                        if pre_dv_wait is not None:
                            pre_dv_wait(v, i, n)
                        ps4 = PS8[(i % 2) * 4:(i % 2) * 4 + 4]
                        epilogue(v, i, n, ps4)
                if sst_hook is not None:
                    @blk.sync
                    def _(s):
                        for i, n in enumerate(n_list):
                            sst_hook(s, i, n, d0)

        def phase_s4_qk(layer, wsq_row, wsk_row):
            if layer == 0:
                qcv, kcv = qcache, kcache
                q_nl, k_nl = list(range(NH)), list(range(NKV))
            else:
                bv = bcache.rearrange("(n p) e -> n p e", p=128)
                qcv, kcv = bv, bv
                q_nl, k_nl = list(range(NH)), list(range(32, 32 + NKV))

            def rope_ep(dst_of, small):
                def ep(v, i, n, ps4):
                    for ti in range(4):
                        sl = slice(ti * TW, (ti + 1) * TW)
                        dst = dst_of(i, ti)
                        v.tensor_tensor(C_rt2[:], ps4[ti][:], C_cos[:, sl], op=OP.mult)
                        v.tensor_tensor(C_rt1[0:64, :], ps4[ti][64:128, :], C_sin[0:64, sl], op=OP.mult)
                        v.tensor_tensor(C_rt1[64:128, :], ps4[ti][0:64, :], C_sin[64:128, sl], op=OP.mult)
                        tt = v.tensor_tensor(dst, C_rt2[:], C_rt1[:], op=OP.add)
                        inc(tt, "DV")
                return ep

            qs0 = {0: cnt["QS0"], 1: cnt["QS1"]}

            def q_ep(v, i, n, ps4):
                for ti in range(4):
                    gidx = 4 * i + ti
                    sl = slice(ti * TW, (ti + 1) * TW)
                    if gidx >= 2:
                        v.wait_ge(SEM["QS%d" % (gidx % 2)],
                                  qs0[gidx % 2] + ((gidx - 2) // 2 + 1) * 16)
                    v.tensor_tensor(C_rt2[:], ps4[ti][:], C_cos[:, sl], op=OP.mult)
                    v.tensor_tensor(C_rt1[0:64, :], ps4[ti][64:128, :], C_sin[0:64, sl], op=OP.mult)
                    v.tensor_tensor(C_rt1[64:128, :], ps4[ti][0:64, :], C_sin[64:128, sl], op=OP.mult)
                    tt = v.tensor_tensor(C_qsm[gidx % 2], C_rt2[:], C_rt1[:], op=OP.add)
                    inc(tt, "DV")

            def q_sst(s, i, n, d0):
                for ti in range(4):
                    gidx = 4 * i + ti
                    s.wait_ge(SEM["DV"], d0 + gidx + 1)
                    dma(s, "QS%d" % (gidx % 2), qspill[n_q_list[i]][:, ti * TW:(ti + 1) * TW],
                        C_qsm[gidx % 2])

            n_q_list = q_nl

            def q_rank1(t, i, n, psu, ti):
                return t.matmul(psu[:], wsq_row[0:1, i * 128:(i + 1) * 128],
                                C_meanrow[:, ti * TW:(ti + 1) * TW], start=False, stop=True)

            gemm_T(f"s4q{layer}", qcv, q_nl, C_band, q_ep, dv_per_n=4, sst_hook=q_sst,
                   rank1=q_rank1)

            def k_ep(v, i, n, ps4):
                for ti in range(4):
                    sl = slice(ti * TW, (ti + 1) * TW)
                    v.tensor_tensor(C_rt2[:], ps4[ti][:], C_cos[:, sl], op=OP.mult)
                    v.tensor_tensor(C_rt1[0:64, :], ps4[ti][64:128, :], C_sin[0:64, sl], op=OP.mult)
                    v.tensor_tensor(C_rt1[64:128, :], ps4[ti][0:64, :], C_sin[64:128, sl], op=OP.mult)
                    tt = v.tensor_tensor(B[:, i * T + ti * TW:i * T + (ti + 1) * TW],
                                         C_rt2[:], C_rt1[:], op=OP.add)
                    inc(tt, "DV")
            def k_rank1(t, i, n, psu, ti):
                return t.matmul(psu[:], wsk_row[0:1, i * 128:(i + 1) * 128],
                                C_meanrow[:, ti * TW:(ti + 1) * TW], start=False, stop=True)

            gemm_T(f"s4k{layer}", kcv, k_nl, C_band, k_ep, dv_per_n=4, rank1=k_rank1)

        def phase_s4_v(layer, ws_off_v):
            vc = vcache if layer == 0 else vbcache
            with ExitStack() as ph:
                wsv_row = ph.enter_context(nc.sbuf_tensor(_un(f"ws_v{layer}"), [1, 1024], BF16))
                nmean = ph.enter_context(nc.sbuf_tensor(_un("s4v_nm"), [1, 2048], BF16))
                vband = ph.enter_context(nc.sbuf_tensor(_un("s4v_bd"), [128, 8, 512], BF16))
                s0, p0, d0 = cnt["SLD"], cnt["PE1"], cnt["DV"]
                qs_tot0, qs_tot1 = cnt["QS0"], cnt["QS1"]
                with nc.Block() as blk:
                    @blk.sync
                    def _(s):
                        dma(s, "SLD", wsv_row[:], wsums[0:1, ws_off_v:ws_off_v + 1024])
                        kq = 0
                        for dvh in range(2):
                            for tg in range(2):
                                grp = dvh * 2 + tg
                                for iq in range(4):
                                    if kq >= 1:
                                        pv = p0 + (kq // 4) * 5 + (kq % 4)
                                        s.wait_ge(SEM["PE1"], pv)
                                    dma(s, "SLD", vband[:], vc[:, dvh, iq * 8:(iq + 1) * 8, :])
                                    kq += 1
                    @blk.tensor
                    def _(t):
                        kq = 0
                        for dvh in range(2):
                            for tg in range(2):
                                grp = dvh * 2 + tg
                                for iq in range(4):
                                    t.wait_ge(SEM["SLD"], s0 + (kq + 2) * 16)
                                    if iq == 0 and grp >= 1:
                                        t.wait_ge(SEM["DV"], d0 + 1 + grp * 8)
                                    mm = None
                                    for c8 in range(8):
                                        c = iq * 8 + c8
                                        for tau in range(tg * 8, tg * 8 + 8):
                                            mm = t.matmul(PS8[tau % 8][:],
                                                          AC_(c, tau * 128, 128),
                                                          vband[:, c8, :],
                                                          start=(iq == 0 and c8 == 0),
                                                          stop=False)
                                    inc(mm, "PE1")
                                    kq += 1
                                if dvh == 0 and tg == 0:
                                    t.wait_ge(SEM["DV"], d0 + 1)
                                for tau in range(tg * 8, tg * 8 + 8):
                                    mm = t.matmul(PS8[tau % 8][:],
                                                  nmean[:, tau * 128:(tau + 1) * 128],
                                                  wsv_row[0:1, dvh * 512:(dvh + 1) * 512],
                                                  start=False, stop=True)
                                inc(mm, "PE1")
                    @blk.vector
                    def _(v):
                        v.wait_ge(SEM["QS0"], qs_tot0)
                        v.wait_ge(SEM["QS1"], qs_tot1)
                        nm = v.tensor_copy(nmean[:], C_meanrow[:])
                        inc(nm, "DV")
                        for dvh in range(2):
                            for tg in range(2):
                                grp = dvh * 2 + tg
                                v.wait_ge(SEM["PE1"], p0 + (grp + 1) * 5)
                                for tau in range(tg * 8, tg * 8 + 8):
                                    ts = v.tensor_scalar(
                                        C[:, tau * 1024 + dvh * 512:tau * 1024 + dvh * 512 + 512],
                                        PS8[tau % 8][:], rstdcols[:, tau:tau + 1], None, op0=OP.mult)
                                    inc(ts, "DV")

        def phase_s6(layer):
            with ExitStack() as ph:
                qh = [ph.enter_context(nc.sbuf_tensor(_un(f"s6_q{i}"), [128, T], BF16)) for i in range(2)]
                pt = [ph.enter_context(nc.sbuf_tensor(_un(f"s6_p{i}"), [128, 512], BF16)) for i in range(2)]
                rb_sb = ph.enter_context(nc.sbuf_tensor(_un("s6_rb"), [128, 512], BF16))
                rrow_bf = ph.enter_context(nc.sbuf_tensor(_un("s6_rrb"), [1, 512], BF16))
                maskt = ph.enter_context(nc.sbuf_tensor(_un("s6_mask"), [128, 896], BF16))
                sc_ps = [PS8[0][:], PS8[1][:]]
                cx_ps = [PS8[2][:], PS8[3][:]]
                dn_ps = [PS8[4][0:1, :], PS8[5][0:1, :]]
                rb_ps = PS8[6][:]
                rr_ps = PS8[7][0:1, :]

                ev_scores, ev_exp, ev_mask, ev_pt = {}, {}, {}, {}
                ev_grp, ev_recip, ev_rb, ev_rbe, ev_cxe, ev_qld = {}, {}, {}, {}, {}, {}
                pe1, pe2, dv, ac, sld = cnt["PE1"], cnt["PE2"], cnt["DV"], cnt["AC"], cnt["SLD"]
                for h in range(NH):
                    sld += 16
                    ev_qld[h] = sld
                    for tau in range(4):
                        nk = 4 * tau + 4
                        for kc in range(nk):
                            pe1 += 1; ev_scores[(h, tau, kc)] = pe1
                            ac += 1; ev_exp[(h, tau, kc)] = ac
                            if kc >= 4 * tau:
                                dv += 1; ev_mask[(h, tau, kc)] = dv
                            pe2 += 1; ev_pt[(h, tau, kc)] = pe2
                        ev_grp[(h, tau)] = pe2
                        dv += 1; ev_recip[(h, tau)] = dv
                        pe2 += 1; ev_rb[(h, tau)] = pe2
                        ac += 1; ev_rbe[(h, tau)] = ac
                        dv += 1; ev_cxe[(h, tau)] = dv
                GV = {h * 4 + tau: (h, tau) for h in range(NH) for tau in range(4)}
                qs_tot0, qs_tot1 = cnt["QS0"], cnt["QS1"]
                ev_qh = {}
                ev_mask_ld = [0]
                with nc.Block() as blk:
                    @blk.sync
                    def _(s):
                        s.wait_ge(SEM["QS0"], qs_tot0)
                        s.wait_ge(SEM["QS1"], qs_tot1)
                        ev_mask_ld[0] = dma(s, "SLD", maskt[:], maskm[:])
                        for h in range(NH):
                            if h >= 2:
                                s.wait_ge(SEM["PE1"], ev_scores[(h - 2, 3, 15)])
                            ev_qh[h] = dma(s, "QH%d" % (h % 2), qh[h % 2][:], qspill[h])
                    @blk.tensor
                    def _(t):
                        for h in range(NH):
                            j = h // 4
                            t.wait_ge(SEM["QH%d" % (h % 2)], ev_qh[h])
                            for tau in range(4):
                                g = h * 4 + tau
                                nk = 4 * tau + 4
                                if g >= 2:
                                    t.wait_ge(SEM["DV"], ev_recip[GV[g - 2]])
                                    t.wait_ge(SEM["DV"], ev_cxe[GV[g - 2]])
                                if g >= 1:
                                    t.wait_ge(SEM["AC"], ev_rbe[GV[g - 1]])

                                def mm_scores(kc):
                                    if (h, tau, kc - 2) in ev_exp:
                                        t.wait_ge(SEM["AC"], ev_exp[(h, tau, kc - 2)])
                                    mm = t.matmul(sc_ps[kc % 2],
                                                  B[:, j * T + kc * 128:j * T + kc * 128 + 128],
                                                  qh[h % 2][:, tau * TW:(tau + 1) * TW],
                                                  start=True, stop=True)
                                    inc(mm, "PE1")

                                def mm_ctxden(kc):
                                    t.wait_ge(SEM["AC"], ev_exp[(h, tau, kc)])
                                    if (h, tau, kc) in ev_mask:
                                        t.wait_ge(SEM["DV"], ev_mask[(h, tau, kc)])
                                    t.matmul(cx_ps[g % 2],
                                             C[:, kc * 1024 + j * 128:kc * 1024 + (j + 1) * 128],
                                             pt[kc % 2][:],
                                             start=(kc == 0), stop=(kc == nk - 1))
                                    mm = t.matmul(dn_ps[g % 2], ones_col[:], pt[kc % 2][:],
                                                  start=(kc == 0), stop=(kc == nk - 1))
                                    inc(mm, "PE2")

                                mm_scores(0)
                                for kc in range(nk):
                                    if kc + 1 < nk:
                                        mm_scores(kc + 1)
                                    mm_ctxden(kc)
                                t.wait_ge(SEM["DV"], ev_recip[(h, tau)])
                                mm = t.matmul(rb_ps, ones_row[:], rrow_bf[:], start=True, stop=True)
                                inc(mm, "PE2")
                    @blk.scalar
                    def _(a):
                        for h in range(NH):
                            for tau in range(4):
                                g = h * 4 + tau
                                nk = 4 * tau + 4
                                for kc in range(nk):
                                    a.wait_ge(SEM["PE1"], ev_scores[(h, tau, kc)])
                                    if (h, tau, kc - 2) in ev_pt:
                                        a.wait_ge(SEM["PE2"], ev_pt[(h, tau, kc - 2)])
                                    ex = a.activation(pt[kc % 2][:], sc_ps[kc % 2], AF.Exp,
                                                      bias=cb_negC[:], scale=SM_SCALE)
                                    inc(ex, "AC")
                                a.wait_ge(SEM["PE2"], ev_rb[(h, tau)])
                                if g >= 1:
                                    a.wait_ge(SEM["DV"], ev_cxe[GV[g - 1]])
                                cp = a.copy(rb_sb[:], rb_ps)
                                inc(cp, "AC")
                    @blk.vector
                    def _(v):
                        v.wait_ge(SEM["SLD"], ev_mask_ld[0])
                        for h in range(NH):
                            for tau in range(4):
                                g = h * 4 + tau
                                nk = 4 * tau + 4
                                for kc in range(4 * tau, nk):
                                    off = kc * 128 - tau * 512
                                    v.wait_ge(SEM["AC"], ev_exp[(h, tau, kc)])
                                    mk = v.tensor_tensor(pt[kc % 2][:], pt[kc % 2][:],
                                                         maskt[:, 384 - off:384 - off + 512],
                                                         op=OP.mult)
                                    inc(mk, "DV")
                                v.wait_ge(SEM["PE2"], ev_grp[(h, tau)])
                                v.reciprocal(rr_ps, dn_ps[g % 2])
                                rc = v.tensor_copy(rrow_bf[:], rr_ps)
                                inc(rc, "DV")
                                v.wait_ge(SEM["AC"], ev_rbe[(h, tau)])
                                ce = v.tensor_tensor(
                                    A[:, h * T + tau * TW:h * T + (tau + 1) * TW],
                                    cx_ps[g % 2], rb_sb[:], op=OP.mult)
                                inc(ce, "DV")
                assert cnt["PE1"] == ev_scores[(NH - 1, 3, 15)]
                assert cnt["PE2"] == ev_rb[(NH - 1, 3)]
                assert cnt["DV"] == ev_cxe[(NH - 1, 3)]
                assert cnt["AC"] == ev_rbe[(NH - 1, 3)]

        def phase_xhome_init():
            d0 = cnt["DV"]
            with nc.Block() as blk:
                @blk.sync
                def _(s):
                    for i in range(NCH):
                        dma(s, "SST", x_home[i], AC_(i))

        def phase_s8a_with_loads():
            with ExitStack() as ph:
                scbc = [ph.enter_context(nc.sbuf_tensor(_un(f"s8_sc{i}"), [128, 512], BF16)) for i in range(2)]
                xold = [ph.enter_context(nc.sbuf_tensor(_un(f"s8_xo{i}"), [128, T], BF16)) for i in range(2)]
                g0, d0, p0, s0 = cnt["GLD"], cnt["DV"], cnt["PE1"], cnt["SLD"]
                ev_stg = {}
                ev_sc8 = {}

                def ev_stt(b, c): return d0 + b * 36 + c + 1
                def ev_epi(i): return d0 + (i // 4) * 36 + 32 + (i % 4) + 1
                def ev_pe(b, nl): return p0 + b * 4 + nl + 1

                with nc.Block() as blk:
                    @blk.gpsimd
                    def _(g):
                        for b in range(8):
                            if b >= 1:
                                g.wait_ge(SEM["DV"], ev_stt(b - 1, 31))
                            for cq in range(4):
                                ev_stg[b] = dma(g, "STG", B_stage[:, cq * 8:(cq + 1) * 8, :],
                                    qw["o"][cq * 1024:(cq + 1) * 1024, b * 512:(b + 1) * 512]
                                    .rearrange("(c p) f -> p c f", p=128))
                            for c in range(NCH):
                                kk = b * 32 + c - 2
                                if kk >= 0:
                                    g.wait_ge(SEM["DV"], ev_stt(kk // 32, kk % 32))
                                ev_sc8[(b, c)] = dma(g, "SC%d" % (c % 2), scbc[c % 2][:],
                                    sc["o"][c:c + 1, b * 512:(b + 1) * 512]
                                    .partition_broadcast(128).squeeze(1))
                    @blk.sync
                    def _(s):
                        for i in range(NCH):
                            dma(s, "SLD", xold[i % 2][:], x_home[i])
                            s.wait_ge(SEM["DV"], ev_epi(i))
                            dma(s, "SST", x_home[i], xold[i % 2][:])
                    @blk.vector
                    def _(v):
                        for b in range(8):
                            for c in range(NCH):
                                v.wait_ge(SEM["STG"], ev_stg[b])
                                v.wait_ge(SEM["SC%d" % (c % 2)], ev_sc8[(b, c)])
                                if b >= 1 and c == 0:
                                    v.wait_ge(SEM["PE1"], ev_pe(b - 1, 3))
                                st = v.scalar_tensor_tensor(C_deq[:, c, :], B_stage[:, c, :], -8.0,
                                                            scbc[c % 2][:], op0=OP.add, op1=OP.mult)
                                inc(st, "DV")
                            for nl in range(4):
                                i = b * 4 + nl
                                v.wait_ge(SEM["PE1"], ev_pe(b, nl))
                                v.wait_ge(SEM["SLD"], s0 + (i + 1) * 16)
                                last = None
                                for ti in range(4):
                                    sl = slice(ti * TW, (ti + 1) * TW)
                                    last = v.tensor_tensor(xold[i % 2][:, sl],
                                                           PS8[(i % 2) * 4 + ti][:],
                                                           xold[i % 2][:, sl], op=OP.add)
                                inc(last, "DV")
                    @blk.tensor
                    def _(t):
                        for b in range(8):
                            for nl in range(4):
                                i = b * 4 + nl
                                if nl == 0:
                                    t.wait_ge(SEM["DV"], ev_stt(b, 31))
                                if i >= 2:
                                    t.wait_ge(SEM["DV"], ev_epi(i - 2))
                                mm = None
                                for c in range(NCH):
                                    for ti in range(4):
                                        mm = t.matmul(PS8[(i % 2) * 4 + ti][:],
                                                      C_deq[:, c, nl * 128:(nl + 1) * 128],
                                                      AC_(c, ti * TW, TW),
                                                      start=(c == 0), stop=(c == NCH - 1))
                                inc(mm, "PE1")

        def phase_s8b():
            oc = ocache.rearrange("(n p) e -> n p e", p=128)
            d0, s0 = cnt["DV"], cnt["SLD"]

            def ep(v, i, n, ps4):
                v.wait_ge(SEM["SLD"], s0 + (i + 1) * 16)
                last = None
                for ti in range(4):
                    sl = slice(ti * TW, (ti + 1) * TW)
                    last = v.tensor_tensor(B_xold[i % 3][:, sl], ps4[ti][:],
                                           B_xold[i % 3][:, sl], op=OP.add)
                inc(last, "DV")

            def sst(s, i, n, d0_):
                dma(s, "SLD", B_xold[i % 3][:], x_home[i])
                s.wait_ge(SEM["DV"], d0 + i + 1)
                dma(s, "SST", x_home[i], B_xold[i % 3][:])

            gemm_T("s8b", oc, list(range(NCH)), B_band, ep, dv_per_n=1, sst_hook=sst)

        def phase_ffn(final):
            with ExitStack() as ph:
                wsf = ph.enter_context(nc.sbuf_tensor(_un("ff_wsf"), [1, 4096], BF16))
                if final:
                    stgf = [ph.enter_context(nc.sbuf_tensor(_un(f"f3_sg{i}"), [128, 512], F32))
                            for i in range(2)]
                    stgo = B_stgo3
                else:
                    stgf = B_stgf2
                    stgo = B_stgo2
                pb = [PS8[0], PS8[1]]
                g0 = cnt["GLD"]
                with nc.Block() as blk:
                    @blk.gpsimd
                    def _(g):
                        for n in range(NCH):
                            dma(g, "GLD", biascols[:, n:n + 1],
                                fbias[0:1, n * 128:(n + 1) * 128]
                                .rearrange("o (p f) -> (o p) f", f=1))
                        dma(g, "GLD", wsf[:], wsums[0:1, 6144:10240])
                p0, d0 = cnt["PE1"], cnt["DV"]
                with nc.Block() as blk:
                    @blk.tensor
                    def _(t):
                        for ti in range(4):
                            if ti >= 2:
                                t.wait_ge(SEM["DV"], d0 + ti - 1)
                            mm = t.matmul(pb[ti % 2][:], ones_row[:],
                                          C_rstdrow[:, ti * TW:(ti + 1) * TW], start=True, stop=True)
                            inc(mm, "PE1")
                    @blk.vector
                    def _(v):
                        for ti in range(4):
                            v.wait_ge(SEM["PE1"], p0 + ti + 1)
                            tc = v.tensor_copy(C_rstdb[:, ti * TW:(ti + 1) * TW], pb[ti % 2][:])
                            inc(tc, "DV")
                st0 = cnt["SST"]

                def rank1(t, i, n, psu, ti):
                    return t.matmul(psu[:], wsf[0:1, n * 128:(n + 1) * 128],
                                    C_meanrow[:, ti * TW:(ti + 1) * TW], start=False, stop=True)

                def ep(v, i, n, ps4):
                    v.wait_ge(SEM["GLD"], g0 + 33 * 16)
                    if i >= 2:
                        v.wait_ge(SEM["SST"], st0 + (i - 1) * 16)
                    last = None
                    for ti in range(4):
                        sl = slice(ti * TW, (ti + 1) * TW)
                        v.tensor_tensor(stgf[ti % 2][:], ps4[ti][:], C_rstdb[:, sl], op=OP.mult)
                        v.tensor_scalar(stgf[ti % 2][:], stgf[ti % 2][:],
                                        biascols[:, n:n + 1], None, op0=OP.add)
                        last = v.tensor_tensor(stgo[i % 2][:, sl], stgf[ti % 2][:],
                                               AC_(n, ti * TW, TW), op=OP.add)
                    inc(last, "DV")

                d2 = cnt["DV"]

                def sst2(s, i, n, d0_):
                    s.wait_ge(SEM["DV"], d2 + i + 1)
                    if final:
                        dma(s, "SST", y_out[n * 128:(n + 1) * 128, :], stgo[i % 2][:])
                    else:
                        dma(s, "SST", x_home[n], stgo[i % 2][:])

                gemm_T("ff" + ("3" if final else "2"), fcache, list(range(NCH)),
                       B_band, ep, dv_per_n=1, sst_hook=sst2, rank1=rank1)

        def dump_fm(name):
            d = mkdbg(name, [D, T])
            with nc.Block() as blk:
                @blk.gpsimd
                def _(g):
                    for qd in range(4):
                        dma(g, "GLD",
                            d[qd * 1024:(qd + 1) * 1024, :].rearrange("(c p) t -> p c t", p=128),
                            A[:, qd * 8 * T:(qd + 1) * 8 * T].rearrange("p (c t) -> p c t", t=T))

        def dump_sbuf(name, ap, shape):
            d = mkdbg(name, list(shape))
            with nc.Block() as blk:
                @blk.gpsimd
                def _(g):
                    dma(g, "GLD", d[:], ap)

        def dump_dram(name, src, shape):
            d = mkdbg(name, list(shape), BF16)
            with nc.Block() as blk:
                @blk.sync
                def _(s):
                    s.wait_ge(SEM["SST"], cnt["SST"])
                    dma(s, "SST", d[:], src[:])

        def finish():
            with nc.Block() as blk:
                @blk.sync
                def _(s):
                    s.wait_ge(SEM["SST"], cnt["SST"])
                    s.wait_ge(SEM["GLD"], cnt["GLD"])
                    s.wait_ge(SEM["SLD"], cnt["SLD"])
                    s.wait_ge(SEM["ALD"], cnt["ALD"])

        p0_attn_a()
        p0_ffn()
        p0_vb()
        p0_bsum()
        if stop_after == "p0":
            if "caches" in taps:
                dump_dram("ws", wsums, [1, 16384])
                dump_dram("qc", qcache, [NH, 128, D])
                dump_dram("kc", kcache, [NKV, 128, D])
                dump_dram("vc", vcache, [128, 2, NCH, 512])
                dump_dram("fc", fcache, [NCH, 128, D])
                dump_dram("vbc", vbcache, [128, 2, NCH, 512])
            finish()
            return nc, dbg

        WSOFF = {0: (0, 4096, 5120), 1: (10240, 14336, 15360)}

        def attn_layer(layer):
            with ExitStack() as lh:
                oq, ok, ov = WSOFF[layer]
                phase_stats(first=(layer == 0), ffn=False)
                if layer == 0:
                    phase_xhome_init()
                with ExitStack() as lh2:
                    wsq = lh2.enter_context(nc.sbuf_tensor(_un(f"ws_q{layer}"), [1, 4096], BF16))
                    wsk = lh2.enter_context(nc.sbuf_tensor(_un(f"ws_k{layer}"), [1, 1024], BF16))
                    phase_tables([(wsq[:], wsums[0:1, oq:oq + 4096]),
                                  (wsk[:], wsums[0:1, ok:ok + 1024])])
                    phase_s4_qk(layer, wsq, wsk)
                phase_s4_v(layer, ov)
                if layer == 0 and stop_after == "qkv":
                    return True
                phase_s6(layer)
                if layer == 0 and stop_after == "ctx":
                    return True
                if layer == 0:
                    phase_s8a_with_loads()
                else:
                    phase_s8b()
            return False

        stop = attn_layer(0)
        if stop_after == "qkv":
            if "qkv" in taps:
                dump_dram("q", qspill, [NH, 128, T])
                dump_sbuf("k", B[:], [128, 8 * T])
                dump_sbuf("v", C[:], [128, 16 * 1024])
            finish()
            return nc, dbg
        if stop_after == "ctx":
            if "ctx" in taps:
                dump_fm("ctx")
            finish()
            return nc, dbg
        if stop_after == "sub1":
            if "x1" in taps:
                dump_dram("x1", x_home, [NCH, 128, T])
            finish()
            return nc, dbg

        phase_stats(first=False, ffn=True)
        phase_ffn(final=False)
        if stop_after == "sub2":
            if "x2" in taps:
                dump_dram("x2", x_home, [NCH, 128, T])
            finish()
            return nc, dbg

        attn_layer(1)
        if stop_after == "sub3":
            if "x3" in taps:
                dump_dram("x3", x_home, [NCH, 128, T])
            finish()
            return nc, dbg

        phase_stats(first=False, ffn=True)
        phase_ffn(final=True)
        finish()
        return nc, dbg


from concourse.bass_utils import run_bass_kernel_spmd


def kernel(**inputs):
    x = np.asarray(inputs["L0_attn_a_hidden_states"])
    consts = host_constants()
    shared = {
        "qw_q": np.ascontiguousarray(np.asarray(inputs["L0_attn_a_qw_q"], dtype=np.int32)),
        "qw_k": np.ascontiguousarray(np.asarray(inputs["L0_attn_a_qw_k"], dtype=np.int32)),
        "qw_v": np.ascontiguousarray(np.asarray(inputs["L0_attn_a_qw_v"], dtype=np.int32)),
        "qw_o": np.ascontiguousarray(np.asarray(inputs["L0_attn_a_qw_o"], dtype=np.int32)),
        "sc_q": np.ascontiguousarray(np.asarray(inputs["L0_attn_a_scales_q"], dtype=np.float32)),
        "sc_k": np.ascontiguousarray(np.asarray(inputs["L0_attn_a_scales_k"], dtype=np.float32)),
        "sc_v": np.ascontiguousarray(np.asarray(inputs["L0_attn_a_scales_v"], dtype=np.float32)),
        "sc_o": np.ascontiguousarray(np.asarray(inputs["L0_attn_a_scales_o"], dtype=np.float32)),
        "w_in": np.ascontiguousarray(np.asarray(inputs["L0_attn_b_w_in"], dtype=np.float32)),
        "w_out": np.ascontiguousarray(np.asarray(inputs["L0_attn_b_w_out"], dtype=np.float32)),
        "fq": np.ascontiguousarray(np.asarray(inputs["L0_ffn_qweight"], dtype=np.int32).reshape(D, D)),
        "fsc": np.ascontiguousarray(np.asarray(inputs["L0_ffn_scales"], dtype=np.float32).reshape(D, 32)),
        "fzp": np.ascontiguousarray(np.asarray(inputs["L0_ffn_scaled_zeros"], dtype=np.float32).reshape(D, 32)),
        "fbias": np.ascontiguousarray(np.asarray(inputs["L0_ffn_bias"], dtype=np.float32).reshape(1, D)),
        "cosbase": consts["cosbase"],
        "sinbase": consts["sinbase"],
        "maskm": consts["maskm"],
        "ident": consts["ident"],
    }
    nc, _dbg = build_program()
    in_maps = []
    for b in range(8):
        m = dict(shared)
        m["x_fm"] = np.ascontiguousarray(np.asarray(x[b], dtype=np.float32).T)
        in_maps.append(m)
    res = run_bass_kernel_spmd(nc, in_maps, core_ids=list(range(8)))
    out = np.stack([np.ascontiguousarray(np.asarray(r["y"]).T) for r in res.results], axis=0)
    return out


# version 02
# speedup vs baseline: 1.0604x; 1.0604x over previous
import math
import numpy as np
import concourse.bass as bass
import concourse.mybir as mybir
from concourse.alu_op_type import AluOpType as OP
from contextlib import ExitStack

F32, BF16, I32 = mybir.dt.float32, mybir.dt.bfloat16, mybir.dt.int32
AF = mybir.ActivationFunctionType

D, T, NCH, TW = 4096, 2048, 32, 512
NH, NKV, HD = 32, 8, 128
EXPC = 10.0
SM_SCALE = 1.0 / math.sqrt(HD)
EPS = 1e-5


def host_constants():
    import ml_dtypes
    inv_freq = 1.0 / (10000.0 ** (np.arange(0, HD, 2, dtype=np.float64) / HD))
    ang = np.arange(T, dtype=np.float64)[:, None] * inv_freq[None, :]
    cos, sin = np.cos(ang).T, np.sin(ang).T
    kk = np.arange(128)[:, None]
    jj = np.arange(896)[None, :]
    return {
        "cosbase": np.concatenate([cos, cos], 0).astype(ml_dtypes.bfloat16),
        "sinbase": np.concatenate([-sin, sin], 0).astype(ml_dtypes.bfloat16),
        "maskm": (kk + 384 <= jj).astype(np.float32).astype(ml_dtypes.bfloat16),
        "ident": np.eye(128, dtype=np.float32).astype(ml_dtypes.bfloat16),
    }


_unctr = {}
def _un(nm):
    _unctr[nm] = _unctr.get(nm, 0) + 1
    return f"{nm}_{_unctr[nm]}"


def build_program(stop_after="all", debug_taps=()):
    _unctr.clear()
    nc = bass.Bass("TRN2", target_bir_lowering=False)
    taps = set(debug_taps)

    x_in = nc.dram_tensor("x_fm", [D, T], F32, kind="ExternalInput").ap()
    qw = {
        "q": nc.dram_tensor("qw_q", [D, 4096], I32, kind="ExternalInput").ap(),
        "k": nc.dram_tensor("qw_k", [D, 1024], I32, kind="ExternalInput").ap(),
        "v": nc.dram_tensor("qw_v", [D, 1024], I32, kind="ExternalInput").ap(),
        "o": nc.dram_tensor("qw_o", [4096, D], I32, kind="ExternalInput").ap(),
    }
    sc = {
        "q": nc.dram_tensor("sc_q", [32, 4096], F32, kind="ExternalInput").ap(),
        "k": nc.dram_tensor("sc_k", [32, 1024], F32, kind="ExternalInput").ap(),
        "v": nc.dram_tensor("sc_v", [32, 1024], F32, kind="ExternalInput").ap(),
        "o": nc.dram_tensor("sc_o", [32, 4096], F32, kind="ExternalInput").ap(),
    }
    w_in = nc.dram_tensor("w_in", [6144, D], F32, kind="ExternalInput").ap()
    w_out = nc.dram_tensor("w_out", [D, D], F32, kind="ExternalInput").ap()
    fq = nc.dram_tensor("fq", [D, D], I32, kind="ExternalInput").ap()
    fsc = nc.dram_tensor("fsc", [D, 32], F32, kind="ExternalInput").ap()
    fzp = nc.dram_tensor("fzp", [D, 32], F32, kind="ExternalInput").ap()
    fbias = nc.dram_tensor("fbias", [1, D], F32, kind="ExternalInput").ap()
    cosb_h = nc.dram_tensor("cosbase", [128, T], BF16, kind="ExternalInput").ap()
    sinb_h = nc.dram_tensor("sinbase", [128, T], BF16, kind="ExternalInput").ap()
    maskm = nc.dram_tensor("maskm", [128, 896], BF16, kind="ExternalInput").ap()
    ident_h = nc.dram_tensor("ident", [128, 128], BF16, kind="ExternalInput").ap()
    y_out = nc.dram_tensor("y", [D, T], F32, kind="ExternalOutput").ap()

    x_home = nc.dram_tensor("x_home", [NCH, 128, T], BF16, kind="Internal").ap()
    qspill = nc.dram_tensor("qspill", [NH, 128, T], BF16, kind="Internal").ap()
    qcache = nc.dram_tensor("qcache", [NH, 128, D], BF16, kind="Internal").ap()
    kcache = nc.dram_tensor("kcache", [NKV, 128, D], BF16, kind="Internal").ap()
    vcache = nc.dram_tensor("vcache", [128, 2, NCH, 512], BF16, kind="Internal").ap()
    fcache = nc.dram_tensor("fcache", [NCH, 128, D], BF16, kind="Internal").ap()
    bcache = nc.dram_tensor("bcache", [6144, D], BF16, kind="Internal").ap()
    ocache = nc.dram_tensor("ocache", [D, D], BF16, kind="Internal").ap()
    vbcache = nc.dram_tensor("vbcache", [128, 2, NCH, 512], BF16, kind="Internal").ap()
    wsums = nc.dram_tensor("wsums", [1, 16384], BF16, kind="Internal").ap()
    ocacheT = nc.dram_tensor("ocacheT", [NCH, 128, D], BF16, kind="Internal").ap()

    dbg = {}

    def mkdbg(name, shape, dtype=F32):
        if name not in dbg:
            dbg[name] = nc.dram_tensor("dbg_" + name, shape, dtype, kind="ExternalOutput").ap()
        return dbg[name]

    es = ExitStack()
    with es:
        sbt = lambda name, shape, dtype: es.enter_context(nc.sbuf_tensor(name, shape, dtype))
        A = sbt("A", [128, NCH * T], BF16)
        B = sbt("B", [128, 16384], BF16)
        C = sbt("C", [128, 16384], BF16)
        ones_row = sbt("ones_row", [1, 128], BF16)
        ones_col = sbt("ones_col", [128, 1], BF16)
        cb_negC = sbt("cb_negC", [128, 1], F32)
        cb_eps = sbt("cb_eps", [128, 1], F32)
        ident = sbt("ident_s", [128, 128], BF16)
        ident32 = sbt("ident32", [32, 32], F32)
        rstdcols = sbt("rstdcols", [128, 16], F32)
        biascols = sbt("biascols", [128, 32], F32)

        PS8 = [es.enter_context(nc.psum_tensor(f"gps{i}", [128, 512], F32)) for i in range(8)]
        PSB = [p[:].bitcast(BF16) for p in PS8]
        SEM = {}
        cnt = {}
        for k in ["GLD", "SLD", "ALD", "SST", "PE1", "PE2", "DV", "AC",
                  "XLD", "SC0", "SC1", "BD0", "BD1", "QH0", "QH1", "QS0", "QS1",
                  "STG", "FF0", "FF1", "WCC"]:
            SEM[k] = es.enter_context(nc.semaphore(k))
            cnt[k] = 0

        def dma(engs, key, out, in_):
            engs.dma_start(out, in_).then_inc(SEM[key], 16)
            cnt[key] += 16
            return cnt[key]

        def dma_t(engs, key, out, in_):
            engs.dma_start_transpose(out, in_).then_inc(SEM[key], 16)
            cnt[key] += 16
            return cnt[key]

        def inc(instr, key, n=1):
            instr.then_inc(SEM[key], n)
            cnt[key] += n
            return cnt[key]

        def AC_(c, t0=0, ww=T):
            return A[:, c * T + t0:c * T + t0 + ww]

        C_cos = C[:, 0:2048]
        C_sin = C[:, 2048:4096]
        C_qsm = [C[:, 4096:4608], C[:, 4608:5120]]
        C_rt1 = C[:, 5120:5632]
        C_rt2 = C[:, 5632:6144]
        C_band = [C[:, 6144 + i * 4096:6144 + (i + 1) * 4096]
                  .rearrange("p (c f) -> p c f", f=128) for i in range(2)]
        C_strow = C[0:1, 0:8192].bitcast(F32)
        C_sq = C[:, 8192:10240]
        C_rstdrow = C[0:1, 10240:12288]
        C_meanrow = C[0:1, 14336:16384]
        C_deq = C.rearrange("p (c f) -> p c f", f=512)
        C_rstdb = C[:, 2048:4096]
        B_stage = B.rearrange("p (c f) -> p c f", f=512)
        B_stage4 = B.rearrange("p (j f) -> p j f", f=D)
        B_band = [B[:, i * 4096:(i + 1) * 4096].rearrange("p (c f) -> p c f", f=128)
                  for i in range(2)]
        B_xold = [B[:, 8192 + i * 2048:8192 + (i + 1) * 2048] for i in range(3)]
        B_stgo2 = [B[:, 8192 + i * 2048:8192 + (i + 1) * 2048] for i in range(2)]
        B_stgf2 = [B[:, 12288 + i * 1024:12288 + (i + 1) * 1024].bitcast(F32) for i in range(2)]
        B_stgo3 = [B[:, 8192 + i * 4096:8192 + (i + 1) * 4096].bitcast(F32) for i in range(2)]

        with nc.Block() as blk:
            @blk.vector
            def _(v):
                v.memset(ones_row[:], 1.0)
                v.memset(ones_col[:], 1.0)
                v.memset(cb_negC[:], -EXPC)
                v.memset(cb_eps[:], EPS)
                v.wait_ge(SEM["SLD"], 16)
                v.tensor_copy(ident32[:], ident[0:32, 0:32])
            @blk.sync
            def _(s):
                dma(s, "SLD", ident[:], ident_h[:])
            @blk.gpsimd
            def _(g):
                for qd in range(4):
                    dma(g, "XLD",
                        A[:, qd * 8 * T:(qd + 1) * 8 * T].rearrange("p (c t) -> p c t", t=T),
                        x_in[qd * 1024:(qd + 1) * 1024, :].rearrange("(c p) t -> p c t", p=128))
                dma(g, "WCC", bcache[:], w_in[:])
                dma(g, "WCC", ocache[:], w_out[:])
        GLD_X = 4 * 16
        GLD_PRE = cnt["GLD"]
        SLD_CONST = cnt["SLD"]

        def p0_attn_a():
            with ExitStack() as ph:
                gath = ph.enter_context(nc.sbuf_tensor("p0_gath", [128, D], BF16))
                scratch = ph.enter_context(nc.sbuf_tensor("p0_scr", [128, 1024], F32))
                scl = scratch[0:32, 0:512]
                scbc = [scratch[:, 0:512], scratch[:, 512:1024]]
                scT = [ph.enter_context(nc.sbuf_tensor(f"p0_scT{i}", [128, 32], F32)) for i in range(4)]
                n8T = [ph.enter_context(nc.sbuf_tensor(f"p0_n8T{i}", [128, 32], F32)) for i in range(4)]
                wsrow = ph.enter_context(nc.sbuf_tensor("p0_wsr", [1, 512], BF16))
                pst = [PSB[0][:, 0:128], PSB[1][:, 0:128]]
                psS = PS8[2][:, 0:32]
                ws_ps = PS8[3][0:1, :]
                bands = [("q", b, "t") for b in range(8)] + \
                        [("k", b, "t") for b in range(2)] + \
                        [("o", b, "t") for b in range(8)] + \
                        [("v", b, "v") for b in range(2)]
                ws_off = {"q": 0, "k": 4096, "v": 5120}
                dstmap = {"q": qcache, "k": kcache, "o": ocacheT}
                for bi, (wname, b, kind) in enumerate(bands):
                    col = b * 512
                    if kind == "t":
                        with nc.Block() as blk:
                            @blk.gpsimd
                            def _(g):
                                g.wait_ge(SEM["PE1"], cnt["PE1"])
                                for cq in range(4):
                                    dma(g, "GLD", B_stage[:, cq * 8:(cq + 1) * 8, :],
                                        qw[wname][cq * 1024:(cq + 1) * 1024, col:col + 512]
                                        .rearrange("(c p) f -> p c f", p=128))
                            @blk.sync
                            def _(s):
                                if bi == 0:
                                    s.wait_ge(SEM["SLD"], 16)
                                dma(s, "SLD", scl, sc[wname][:, col:col + 512])
                                s.wait_ge(SEM["GLD"], cnt["GLD"])
                                s.wait_ge(SEM["SLD"], cnt["SLD"])
                                s.wait_ge(SEM["SST"], cnt["SST"])
                        d0, p0, a0, s0, dv0 = cnt["DV"], cnt["PE1"], cnt["AC"], cnt["SST"], cnt["DV"]
                        with nc.Block() as blk:
                            @blk.tensor
                            def _(t):
                                for nl in range(4):
                                    if nl >= 1:
                                        t.wait_ge(SEM["DV"], d0 + 2 * nl)
                                    tp = t.transpose(psS, scl[:, nl * 128:(nl + 1) * 128],
                                                     ident32[:])
                                    inc(tp, "PE1")
                                for nl in range(4):
                                    for c in range(NCH):
                                        kk = nl * 32 + c
                                        if kk >= 2:
                                            t.wait_ge(SEM["AC"], a0 + kk - 1)
                                        tp = t.transpose(pst[c % 2],
                                                         B_stage[:, c, nl * 128:(nl + 1) * 128],
                                                         ident[:])
                                        inc(tp, "PE1")
                            @blk.vector
                            def _(v):
                                for nl in range(4):
                                    v.wait_ge(SEM["PE1"], p0 + nl + 1)
                                    tc = v.tensor_copy(scT[nl][:], psS)
                                    inc(tc, "DV")
                                    ts = v.tensor_scalar(n8T[nl][:], psS, -8.0, None,
                                                         op0=OP.mult)
                                    inc(ts, "DV")
                            @blk.scalar
                            def _(a):
                                for nl in range(4):
                                    a.wait_ge(SEM["DV"], d0 + 2 * nl + 2)
                                    if nl > 0:
                                        a.wait_ge(SEM["SST"], s0 + nl * 16)
                                    for c in range(NCH):
                                        a.wait_ge(SEM["PE1"], p0 + 4 + nl * 32 + c + 1)
                                        cp = a.activation(gath[:, c * 128:(c + 1) * 128],
                                                          pst[c % 2], AF.Identity,
                                                          bias=n8T[nl][:, c:c + 1],
                                                          scale=scT[nl][:, c:c + 1])
                                        inc(cp, "AC")
                            @blk.sync
                            def _(s):
                                for nl in range(4):
                                    s.wait_ge(SEM["AC"], a0 + (nl + 1) * 32)
                                    dma(s, "SST", dstmap[wname][4 * b + nl], gath[:])
                    else:
                        with nc.Block() as blk:
                            @blk.gpsimd
                            def _(g):
                                g.wait_ge(SEM["PE1"], cnt["PE1"])
                                g.wait_ge(SEM["DV"], cnt["DV"])
                                for cq in range(4):
                                    dma(g, "GLD", B_stage[:, cq * 8:(cq + 1) * 8, :],
                                        qw[wname][cq * 1024:(cq + 1) * 1024, col:col + 512]
                                        .rearrange("(c p) f -> p c f", p=128))
                            @blk.sync
                            def _(s):
                                s.wait_ge(SEM["GLD"], cnt["GLD"])
                                s.wait_ge(SEM["SST"], cnt["SST"])
                        d0, p0, a0, s0 = cnt["DV"], cnt["PE1"], cnt["AC"], cnt["SST"]
                        ev_sc = {}
                        with nc.Block() as blk:
                            @blk.sync
                            def _(s):
                                for c in range(NCH):
                                    if c >= 2:
                                        s.wait_ge(SEM["DV"], d0 + c - 1)
                                    ev_sc[c] = dma(s, "SC%d" % (c % 2), scbc[c % 2][:],
                                                   sc[wname][c:c + 1, col:col + 512]
                                                   .partition_broadcast(128).squeeze(1))
                                for c in range(NCH):
                                    s.wait_ge(SEM["DV"], d0 + c + 1)
                                    dma(s, "SST", vcache[:, b, c, :], C_deq[:, c, :])
                            @blk.vector
                            def _(v):
                                for c in range(NCH):
                                    v.wait_ge(SEM["SC%d" % (c % 2)], ev_sc[c])
                                    st = v.scalar_tensor_tensor(C_deq[:, c, :], B_stage[:, c, :],
                                                                -8.0, scbc[c % 2][:],
                                                                op0=OP.add, op1=OP.mult)
                                    inc(st, "DV")
                        d1, p1 = cnt["DV"], cnt["PE1"]
                        with nc.Block() as blk:
                            @blk.tensor
                            def _(t):
                                mm = None
                                for c in range(NCH):
                                    mm = t.matmul(ws_ps, ones_col[:], C_deq[:, c, :],
                                                  start=(c == 0), stop=(c == NCH - 1))
                                inc(mm, "PE1")
                            @blk.vector
                            def _(v):
                                v.wait_ge(SEM["PE1"], p1 + 1)
                                ws = v.tensor_copy(wsrow[:], ws_ps)
                                inc(ws, "DV")
                            @blk.sync
                            def _(s):
                                s.wait_ge(SEM["DV"], cnt["DV"])
                                dma(s, "SST", wsums[0:1, 5120 + b * 512:5120 + (b + 1) * 512],
                                    wsrow[:])

        def p0_ffn():
            with ExitStack() as ph:
                fscb = [ph.enter_context(nc.sbuf_tensor(f"p0_fs{i}", [128, 32], F32)) for i in range(2)]
                fzpb = [ph.enter_context(nc.sbuf_tensor(f"p0_fz{i}", [128, 32], F32)) for i in range(2)]
                wcol32 = ph.enter_context(nc.sbuf_tensor("p0_wc32", [128, 1], F32))
                wcol = ph.enter_context(nc.sbuf_tensor("p0_wc", [128, 2], BF16))
                wrow = ph.enter_context(nc.sbuf_tensor("p0_wr", [1, 128], BF16))
                ws_ps = PS8[2][0:2, 0:128]
                Bs = [B[:, 0:4096], B[:, 4096:8192]]
                Cd = [C[:, 0:4096], C[:, 4096:8192]]
                with nc.Block() as blk:
                    @blk.sync
                    def _(s):
                        s.wait_ge(SEM["SST"], cnt["SST"])
                g0, d0, s0 = cnt["GLD"], cnt["DV"], cnt["SST"]
                p00, a00, sst00 = cnt["PE1"], cnt["AC"], cnt["SST"]
                ev_ff = {}
                with nc.Block() as blk:
                    @blk.gpsimd
                    def _(g):
                        for n in range(NCH):
                            if n >= 2:
                                g.wait_ge(SEM["DV"], d0 + 2 * (n - 2) + 1)
                            dma(g, "FF%d" % (n % 2), Bs[n % 2], fq[n * 128:(n + 1) * 128, :])
                            dma(g, "FF%d" % (n % 2), fscb[n % 2][:], fsc[n * 128:(n + 1) * 128, :])
                            ev_ff[n] = dma(g, "FF%d" % (n % 2), fzpb[n % 2][:], fzp[n * 128:(n + 1) * 128, :])
                    @blk.vector
                    def _(v):
                        for n in range(NCH):
                            v.wait_ge(SEM["FF%d" % (n % 2)], ev_ff[n])
                            if n >= 2:
                                v.wait_ge(SEM["SST"], s0 + (n - 1) * 32)
                            ts = None
                            for gg in range(32):
                                ts = v.tensor_scalar(Cd[n % 2][:, gg * 128:(gg + 1) * 128],
                                                     Bs[n % 2][:, gg * 128:(gg + 1) * 128],
                                                     fscb[n % 2][:, gg:gg + 1], fzpb[n % 2][:, gg:gg + 1],
                                                     op0=OP.mult, op1=OP.add)
                            inc(ts, "DV")
                            if n >= 1:
                                v.wait_ge(SEM["AC"], a00 + 3 * (n - 1) + 1)
                            rd = v.tensor_reduce(wcol32[:], Cd[n % 2], axis=mybir.AxisListType.X,
                                                 op=OP.add)
                            inc(rd, "DV")
                    @blk.scalar
                    def _(a):
                        for n in range(NCH):
                            a.wait_ge(SEM["DV"], d0 + 2 * n + 2)
                            if n >= 1:
                                a.wait_ge(SEM["PE1"], p00 + n)
                            cp = a.copy(wcol[:, 0:1], wcol32[:])
                            inc(cp, "AC")
                            z = a.activation(wcol[:, 1:2], wcol32[:], AF.Copy, bias=0.0, scale=0.0)
                            inc(z, "AC")
                            a.wait_ge(SEM["PE1"], p00 + n + 1)
                            if n >= 1:
                                a.wait_ge(SEM["SST"], sst00 + n * 32)
                            cp2 = a.copy(wrow[:], ws_ps[0:1, :])
                            inc(cp2, "AC")
                    @blk.tensor
                    def _(t):
                        for n in range(NCH):
                            t.wait_ge(SEM["AC"], a00 + 3 * n + 2)
                            mm = t.matmul(ws_ps, wcol[:], ident[:], start=True, stop=True)
                            inc(mm, "PE1")
                    @blk.sync
                    def _(s):
                        for n in range(NCH):
                            s.wait_ge(SEM["DV"], d0 + 2 * n + 1)
                            dma(s, "SST", fcache[n], Cd[n % 2])
                            s.wait_ge(SEM["AC"], a00 + 3 * n + 3)
                            dma(s, "SST", wsums[0:1, 6144 + n * 128:6144 + (n + 1) * 128], wrow[:])
                if "p0ffn" in taps:
                    dcol = mkdbg("ffncol", [128, 1])
                    dcol2 = mkdbg("ffncol2", [128, 2])
                    drow = mkdbg("ffnrow", [1, 128])
                    with nc.Block() as blk2:
                        @blk2.gpsimd
                        def _(g):
                            dma(g, "GLD", dcol[:], wcol32[:])
                            dma(g, "GLD", dcol2[:], wcol[:])
                            dma(g, "GLD", drow[:], wrow[:])

        def p0_vb():
            with ExitStack() as ph:
                pst = [PSB[0][:, 0:128], PSB[1][:, 0:128]]
                for dvh in range(2):
                    with nc.Block() as blk:
                        @blk.sync
                        def _(s):
                            s.wait_ge(SEM["GLD"], cnt["GLD"])
                            s.wait_ge(SEM["SST"], cnt["SST"])
                            dma(s, "SLD", B_stage4[:, :, :],
                                bcache[5120 + dvh * 512:5120 + (dvh + 1) * 512, :]
                                .rearrange("(j p) e -> p j e", p=128))
                    p0, a0, s0 = cnt["PE1"], cnt["AC"], cnt["SLD"]
                    with nc.Block() as blk:
                        @blk.tensor
                        def _(t):
                            t.wait_ge(SEM["SLD"], s0)
                            for jj in range(4):
                                for c in range(NCH):
                                    kk = jj * 32 + c
                                    if kk >= 2:
                                        t.wait_ge(SEM["AC"], a0 + kk - 1)
                                    tp = t.transpose(pst[c % 2],
                                                     B_stage4[:, jj, c * 128:(c + 1) * 128], ident[:])
                                    inc(tp, "PE1")
                        @blk.scalar
                        def _(a):
                            for jj in range(4):
                                for c in range(NCH):
                                    a.wait_ge(SEM["PE1"], p0 + jj * 32 + c + 1)
                                    cp = a.copy(C[:, c * 512 + jj * 128:c * 512 + (jj + 1) * 128],
                                                pst[c % 2])
                                    inc(cp, "AC")
                        @blk.sync
                        def _(s):
                            s.wait_ge(SEM["AC"], a0 + 128)
                            dma(s, "SST", vbcache[:, dvh, :, :], C_deq[:, :, :])

        _cs_ctr = [0]
        def p0_colsum(srcrows, NB, ws_base):
            _cs_ctr[0] += 1
            with ExitStack() as ph:
                wcol32 = ph.enter_context(nc.sbuf_tensor(f"p0d_wc32_{_cs_ctr[0]}", [128, 1], F32))
                wcol = ph.enter_context(nc.sbuf_tensor(f"p0d_wc_{_cs_ctr[0]}", [128, 2], BF16))
                wrow = ph.enter_context(nc.sbuf_tensor(f"p0d_wr_{_cs_ctr[0]}", [1, 128], BF16))
                ws_ps = PS8[2][0:2, 0:128]
                stg = B[:, 0:4096]
                d0, p0, a0 = cnt["DV"], cnt["PE1"], cnt["AC"]
                sst_b0 = cnt["SST"]
                ev_st = {}
                with nc.Block() as blk:
                    @blk.sync
                    def _(s):
                        s.wait_ge(SEM["SST"], cnt["SST"])
                        s.wait_ge(SEM["WCC"], 32)
                        for nb in range(NB):
                            if nb >= 1:
                                s.wait_ge(SEM["DV"], d0 + nb)
                            ev_st[nb] = dma(s, "STG", stg, srcrows(nb))
                            s.wait_ge(SEM["AC"], a0 + 3 * nb + 3)
                            dma(s, "SST", wsums[0:1, ws_base + nb * 128:ws_base + (nb + 1) * 128],
                                wrow[:])
                    @blk.vector
                    def _(v):
                        for nb in range(NB):
                            v.wait_ge(SEM["STG"], ev_st[nb])
                            if nb >= 1:
                                v.wait_ge(SEM["AC"], a0 + 3 * (nb - 1) + 1)
                            rd = v.tensor_reduce(wcol32[:], stg, axis=mybir.AxisListType.X, op=OP.add)
                            inc(rd, "DV")
                    @blk.scalar
                    def _(a):
                        for nb in range(NB):
                            a.wait_ge(SEM["DV"], d0 + nb + 1)
                            if nb >= 1:
                                a.wait_ge(SEM["PE1"], p0 + nb)
                            cp = a.copy(wcol[:, 0:1], wcol32[:])
                            inc(cp, "AC")
                            z = a.activation(wcol[:, 1:2], wcol32[:], AF.Copy, bias=0.0, scale=0.0)
                            inc(z, "AC")
                            a.wait_ge(SEM["PE1"], p0 + nb + 1)
                            if nb >= 1:
                                a.wait_ge(SEM["SST"], sst_b0 + nb * 16)
                            cp2 = a.copy(wrow[:], ws_ps[0:1, :])
                            inc(cp2, "AC")
                    @blk.tensor
                    def _(t):
                        for nb in range(NB):
                            t.wait_ge(SEM["AC"], a0 + 3 * nb + 2)
                            mm = t.matmul(ws_ps, wcol[:], ident[:], start=True, stop=True)
                            inc(mm, "PE1")

        def phase_stats(first, ffn):
            with ExitStack() as ph:
                sums = [PS8[i][0:1, :] for i in range(4)]
                sqs = [PS8[4 + i][0:1, :] for i in range(4)]
                sqb = [C[:, 8192:10240], C[:, 10240:12288]]
                if not first:
                    with nc.Block() as blk:
                        @blk.sync
                        def _(s):
                            s.wait_ge(SEM["SST"], cnt["SST"])
                            for qd in range(4):
                                dma(s, "SLD",
                                    A[:, qd * 8 * T:(qd + 1) * 8 * T].rearrange("p (c t) -> p c t", t=T),
                                    x_home[qd * 8:(qd + 1) * 8].rearrange("c p t -> p c t"))
                d0, p0 = cnt["DV"], cnt["PE1"]
                sld_x = cnt["SLD"]
                with nc.Block() as blk:
                    @blk.vector
                    def _(v):
                        if first:
                            v.wait_ge(SEM["XLD"], GLD_X)
                        else:
                            v.wait_ge(SEM["SLD"], sld_x)
                        for c in range(NCH):
                            if c >= 2:
                                v.wait_ge(SEM["PE1"], p0 + c - 1)
                            tt = v.tensor_tensor(sqb[c % 2][:], AC_(c), AC_(c), op=OP.mult)
                            inc(tt, "DV")
                    @blk.tensor
                    def _(t):
                        for c in range(NCH):
                            t.wait_ge(SEM["DV"], d0 + c + 1)
                            mm = None
                            for ti in range(4):
                                t.matmul(sums[ti], ones_col[:], AC_(c, ti * TW, TW),
                                         start=(c == 0), stop=(c == NCH - 1))
                                mm = t.matmul(sqs[ti], ones_col[:], sqb[c % 2][:, ti * TW:(ti + 1) * TW],
                                              start=(c == 0), stop=(c == NCH - 1))
                            inc(mm, "PE1")
                pe_end = cnt["PE1"]
                d1 = cnt["DV"]
                with nc.Block() as blk:
                    @blk.vector
                    def _(v):
                        v.wait_ge(SEM["PE1"], pe_end)
                        for ti in range(4):
                            sl = slice(ti * TW, (ti + 1) * TW)
                            r_mean = C_strow[:, ti * 512:ti * 512 + 512]
                            r_var = C_strow[:, 2048 + ti * 512:2048 + ti * 512 + 512]
                            r_tmp = C[0:1, 8192:9216].bitcast(F32)
                            v.tensor_scalar(r_mean, sums[ti], 1.0 / D, None, op0=OP.mult)
                            v.tensor_scalar(r_var, sqs[ti], 1.0 / D, None, op0=OP.mult)
                            v.scalar_tensor_tensor(r_tmp, r_mean, -1.0, r_mean, op0=OP.mult, op1=OP.mult)
                            v.tensor_tensor(r_var, r_var, r_tmp, op=OP.add)
                            tc = v.tensor_scalar(C_meanrow[:, sl], r_mean, -1.0, None, op0=OP.mult)
                            inc(tc, "DV")
                    @blk.scalar
                    def _(a):
                        for ti in range(4):
                            sl = slice(ti * TW, (ti + 1) * TW)
                            a.wait_ge(SEM["DV"], d1 + ti + 1)
                            r_var = C_strow[:, 2048 + ti * 512:2048 + ti * 512 + 512]
                            a.activation(r_var, r_var, AF.Ln, bias=cb_eps[0:1, :], scale=1.0)
                            ac = a.activation(C_rstdrow[:, sl], r_var, AF.Exp, bias=0.0, scale=-0.5)
                            inc(ac, "AC")
                a1 = cnt["AC"]
                with nc.Block() as blk:
                    if not ffn:
                        pcol = None
                        p1 = cnt["PE1"]
                        pcol_ps = PS8[0][:, 0:16]
                        @blk.tensor
                        def _(t):
                            t.wait_ge(SEM["AC"], a1)
                            mm = None
                            for tau in range(16):
                                mm = t.matmul(pcol_ps[:, tau:tau + 1],
                                              C_rstdrow[:, tau * 128:(tau + 1) * 128],
                                              ones_row[0:1, 0:1], start=True, stop=True)
                            inc(mm, "PE1")
                        @blk.vector
                        def _(v):
                            v.wait_ge(SEM["PE1"], p1 + 1)
                            tc = v.tensor_copy(rstdcols[:], pcol_ps)
                            inc(tc, "DV")
                    else:
                        pass


        def phase_tables(ws_loads):
            with ExitStack() as ph:
                pb = [PS8[0], PS8[1]]
                s0 = cnt["SLD"]
                with nc.Block() as blk:
                    @blk.sync
                    def _(s):
                        dma(s, "SLD", C_cos, cosb_h[:])
                        dma(s, "SLD", C_sin, sinb_h[:])
                        for dst, srcsl in ws_loads:
                            dma(s, "SLD", dst, srcsl)
                sld_all = cnt["SLD"]
                p1, d0 = cnt["PE1"], cnt["DV"]
                with nc.Block() as blk:
                    @blk.tensor
                    def _(t):
                        for i in range(8):
                            ti = i % 4
                            if i >= 2:
                                t.wait_ge(SEM["DV"], d0 + i - 1)
                            mm = t.matmul(pb[i % 2][:], ones_row[:],
                                          C_rstdrow[:, ti * TW:(ti + 1) * TW],
                                          start=True, stop=True)
                            inc(mm, "PE1")
                    @blk.vector
                    def _(v):
                        v.wait_ge(SEM["SLD"], sld_all)
                        for i in range(8):
                            ti = i % 4
                            v.wait_ge(SEM["PE1"], p1 + i + 1)
                            if i < 4:
                                tgt = C_cos[:, ti * TW:(ti + 1) * TW]
                            else:
                                tgt = C_sin[:, ti * TW:(ti + 1) * TW]
                            tt = v.tensor_tensor(tgt, pb[i % 2][:], tgt, op=OP.mult)
                            inc(tt, "DV")

        def gemm_T(name, cache_view, n_list, band_bufs, epilogue, dv_per_n,
                   sst_hook=None, rank1=None, pre_dv_wait=None):
            NL = len(n_list)
            p0, d0 = cnt["PE1"], cnt["DV"]
            sst_pre = cnt["SST"]
            ev_bd = {}
            with nc.Block() as blk:
                @blk.scalar
                def _(a):
                    a.wait_ge(SEM["SST"], sst_pre)
                    a.wait_ge(SEM["WCC"], 32)
                    for i in range(NL):
                        if i >= 2:
                            a.wait_ge(SEM["PE1"], p0 + i - 1)
                        ev_bd[i] = dma_t(a, "BD%d" % (i % 2), band_bufs[i % 2][:], cache_view[n_list[i]])
                @blk.tensor
                def _(t):
                    for i, n in enumerate(n_list):
                        t.wait_ge(SEM["BD%d" % (i % 2)], ev_bd[i])
                        if i >= 2:
                            t.wait_ge(SEM["DV"], d0 + dv_per_n * (i - 1))
                        ps4 = PS8[(i % 2) * 4:(i % 2) * 4 + 4]
                        mm = None
                        for c in range(NCH):
                            for ti in range(4):
                                mm = t.matmul(ps4[ti][:], band_bufs[i % 2][:, c, :],
                                              AC_(c, ti * TW, TW),
                                              start=(c == 0), stop=(c == NCH - 1 and rank1 is None))
                        if rank1 is not None:
                            for ti in range(4):
                                mm = rank1(t, i, n, ps4[ti], ti)
                        inc(mm, "PE1")
                @blk.vector
                def _(v):
                    for i, n in enumerate(n_list):
                        v.wait_ge(SEM["PE1"], p0 + i + 1)
                        if pre_dv_wait is not None:
                            pre_dv_wait(v, i, n)
                        ps4 = PS8[(i % 2) * 4:(i % 2) * 4 + 4]
                        epilogue(v, i, n, ps4)
                if sst_hook is not None:
                    @blk.sync
                    def _(s):
                        for i, n in enumerate(n_list):
                            sst_hook(s, i, n, d0)

        def phase_s4_qk(layer, wsq_row, wsk_row):
            if layer == 0:
                qcv, kcv = qcache, kcache
                q_nl, k_nl = list(range(NH)), list(range(NKV))
            else:
                bv = bcache.rearrange("(n p) e -> n p e", p=128)
                qcv, kcv = bv, bv
                q_nl, k_nl = list(range(NH)), list(range(32, 32 + NKV))

            def rope_ep(dst_of, small):
                def ep(v, i, n, ps4):
                    for ti in range(4):
                        sl = slice(ti * TW, (ti + 1) * TW)
                        dst = dst_of(i, ti)
                        v.tensor_tensor(C_rt2[:], ps4[ti][:], C_cos[:, sl], op=OP.mult)
                        v.tensor_tensor(C_rt1[0:64, :], ps4[ti][64:128, :], C_sin[0:64, sl], op=OP.mult)
                        v.tensor_tensor(C_rt1[64:128, :], ps4[ti][0:64, :], C_sin[64:128, sl], op=OP.mult)
                        tt = v.tensor_tensor(dst, C_rt2[:], C_rt1[:], op=OP.add)
                        inc(tt, "DV")
                return ep

            qs0 = {0: cnt["QS0"], 1: cnt["QS1"]}

            def q_ep(v, i, n, ps4):
                for ti in range(4):
                    gidx = 4 * i + ti
                    sl = slice(ti * TW, (ti + 1) * TW)
                    if gidx >= 2:
                        v.wait_ge(SEM["QS%d" % (gidx % 2)],
                                  qs0[gidx % 2] + ((gidx - 2) // 2 + 1) * 16)
                    v.tensor_tensor(C_rt2[:], ps4[ti][:], C_cos[:, sl], op=OP.mult)
                    v.tensor_tensor(C_rt1[0:64, :], ps4[ti][64:128, :], C_sin[0:64, sl], op=OP.mult)
                    v.tensor_tensor(C_rt1[64:128, :], ps4[ti][0:64, :], C_sin[64:128, sl], op=OP.mult)
                    tt = v.tensor_tensor(C_qsm[gidx % 2], C_rt2[:], C_rt1[:], op=OP.add)
                    inc(tt, "DV")

            def q_sst(s, i, n, d0):
                for ti in range(4):
                    gidx = 4 * i + ti
                    s.wait_ge(SEM["DV"], d0 + gidx + 1)
                    dma(s, "QS%d" % (gidx % 2), qspill[n_q_list[i]][:, ti * TW:(ti + 1) * TW],
                        C_qsm[gidx % 2])

            n_q_list = q_nl

            def q_rank1(t, i, n, psu, ti):
                return t.matmul(psu[:], wsq_row[0:1, i * 128:(i + 1) * 128],
                                C_meanrow[:, ti * TW:(ti + 1) * TW], start=False, stop=True)

            gemm_T(f"s4q{layer}", qcv, q_nl, C_band, q_ep, dv_per_n=4, sst_hook=q_sst,
                   rank1=q_rank1)

            def k_ep(v, i, n, ps4):
                for ti in range(4):
                    sl = slice(ti * TW, (ti + 1) * TW)
                    v.tensor_tensor(C_rt2[:], ps4[ti][:], C_cos[:, sl], op=OP.mult)
                    v.tensor_tensor(C_rt1[0:64, :], ps4[ti][64:128, :], C_sin[0:64, sl], op=OP.mult)
                    v.tensor_tensor(C_rt1[64:128, :], ps4[ti][0:64, :], C_sin[64:128, sl], op=OP.mult)
                    tt = v.tensor_tensor(B[:, i * T + ti * TW:i * T + (ti + 1) * TW],
                                         C_rt2[:], C_rt1[:], op=OP.add)
                    inc(tt, "DV")
            def k_rank1(t, i, n, psu, ti):
                return t.matmul(psu[:], wsk_row[0:1, i * 128:(i + 1) * 128],
                                C_meanrow[:, ti * TW:(ti + 1) * TW], start=False, stop=True)

            gemm_T(f"s4k{layer}", kcv, k_nl, C_band, k_ep, dv_per_n=4, rank1=k_rank1)

        def phase_s4_v(layer, ws_off_v):
            vc = vcache if layer == 0 else vbcache
            with ExitStack() as ph:
                wsv_row = ph.enter_context(nc.sbuf_tensor(_un(f"ws_v{layer}"), [1, 1024], BF16))
                nmean = ph.enter_context(nc.sbuf_tensor(_un("s4v_nm"), [1, 2048], BF16))
                vband = ph.enter_context(nc.sbuf_tensor(_un("s4v_bd"), [128, 8, 512], BF16))
                s0, p0, d0 = cnt["SLD"], cnt["PE1"], cnt["DV"]
                qs_tot0, qs_tot1 = cnt["QS0"], cnt["QS1"]
                with nc.Block() as blk:
                    @blk.sync
                    def _(s):
                        dma(s, "SLD", wsv_row[:], wsums[0:1, ws_off_v:ws_off_v + 1024])
                        kq = 0
                        for dvh in range(2):
                            for tg in range(2):
                                grp = dvh * 2 + tg
                                for iq in range(4):
                                    if kq >= 1:
                                        pv = p0 + (kq // 4) * 5 + (kq % 4)
                                        s.wait_ge(SEM["PE1"], pv)
                                    dma(s, "SLD", vband[:], vc[:, dvh, iq * 8:(iq + 1) * 8, :])
                                    kq += 1
                    @blk.tensor
                    def _(t):
                        kq = 0
                        for dvh in range(2):
                            for tg in range(2):
                                grp = dvh * 2 + tg
                                for iq in range(4):
                                    t.wait_ge(SEM["SLD"], s0 + (kq + 2) * 16)
                                    if iq == 0 and grp >= 1:
                                        t.wait_ge(SEM["DV"], d0 + 1 + grp * 8)
                                    mm = None
                                    for c8 in range(8):
                                        c = iq * 8 + c8
                                        for tau in range(tg * 8, tg * 8 + 8):
                                            mm = t.matmul(PS8[tau % 8][:],
                                                          AC_(c, tau * 128, 128),
                                                          vband[:, c8, :],
                                                          start=(iq == 0 and c8 == 0),
                                                          stop=False)
                                    inc(mm, "PE1")
                                    kq += 1
                                if dvh == 0 and tg == 0:
                                    t.wait_ge(SEM["DV"], d0 + 1)
                                for tau in range(tg * 8, tg * 8 + 8):
                                    mm = t.matmul(PS8[tau % 8][:],
                                                  nmean[:, tau * 128:(tau + 1) * 128],
                                                  wsv_row[0:1, dvh * 512:(dvh + 1) * 512],
                                                  start=False, stop=True)
                                inc(mm, "PE1")
                    @blk.vector
                    def _(v):
                        v.wait_ge(SEM["QS0"], qs_tot0)
                        v.wait_ge(SEM["QS1"], qs_tot1)
                        nm = v.tensor_copy(nmean[:], C_meanrow[:])
                        inc(nm, "DV")
                        for dvh in range(2):
                            for tg in range(2):
                                grp = dvh * 2 + tg
                                v.wait_ge(SEM["PE1"], p0 + (grp + 1) * 5)
                                for tau in range(tg * 8, tg * 8 + 8):
                                    ts = v.tensor_scalar(
                                        C[:, tau * 1024 + dvh * 512:tau * 1024 + dvh * 512 + 512],
                                        PS8[tau % 8][:], rstdcols[:, tau:tau + 1], None, op0=OP.mult)
                                    inc(ts, "DV")

        def phase_s6(layer):
            with ExitStack() as ph:
                qh = [ph.enter_context(nc.sbuf_tensor(_un(f"s6_q{i}"), [128, T], BF16)) for i in range(2)]
                pt = [ph.enter_context(nc.sbuf_tensor(_un(f"s6_p{i}"), [128, 512], BF16)) for i in range(2)]
                rb_sb = ph.enter_context(nc.sbuf_tensor(_un("s6_rb"), [128, 512], BF16))
                rrow_bf = ph.enter_context(nc.sbuf_tensor(_un("s6_rrb"), [1, 512], BF16))
                maskt = ph.enter_context(nc.sbuf_tensor(_un("s6_mask"), [128, 896], BF16))
                sc_ps = [PS8[0][:], PS8[1][:]]
                cx_ps = [PS8[2][:], PS8[3][:]]
                dn_ps = [PS8[4][0:1, :], PS8[5][0:1, :]]
                rb_ps = PS8[6][:]
                rr_ps = PS8[7][0:1, :]

                ev_scores, ev_exp, ev_mask, ev_pt = {}, {}, {}, {}
                ev_grp, ev_recip, ev_rb, ev_rbe, ev_cxe, ev_qld = {}, {}, {}, {}, {}, {}
                pe1, pe2, dv, ac, sld = cnt["PE1"], cnt["PE2"], cnt["DV"], cnt["AC"], cnt["SLD"]
                for h in range(NH):
                    sld += 16
                    ev_qld[h] = sld
                    for tau in range(4):
                        nk = 4 * tau + 4
                        for kc in range(nk):
                            pe1 += 1; ev_scores[(h, tau, kc)] = pe1
                            ac += 1; ev_exp[(h, tau, kc)] = ac
                            if kc >= 4 * tau:
                                dv += 1; ev_mask[(h, tau, kc)] = dv
                            pe2 += 1; ev_pt[(h, tau, kc)] = pe2
                        ev_grp[(h, tau)] = pe2
                        dv += 1; ev_recip[(h, tau)] = dv
                        pe2 += 1; ev_rb[(h, tau)] = pe2
                        ac += 1; ev_rbe[(h, tau)] = ac
                        dv += 1; ev_cxe[(h, tau)] = dv
                GV = {h * 4 + tau: (h, tau) for h in range(NH) for tau in range(4)}
                qs_tot0, qs_tot1 = cnt["QS0"], cnt["QS1"]
                ev_qh = {}
                ev_mask_ld = [0]
                with nc.Block() as blk:
                    @blk.sync
                    def _(s):
                        s.wait_ge(SEM["QS0"], qs_tot0)
                        s.wait_ge(SEM["QS1"], qs_tot1)
                        ev_mask_ld[0] = dma(s, "SLD", maskt[:], maskm[:])
                        for h in range(NH):
                            if h >= 2:
                                s.wait_ge(SEM["PE1"], ev_scores[(h - 2, 3, 15)])
                            ev_qh[h] = dma(s, "QH%d" % (h % 2), qh[h % 2][:], qspill[h])
                    @blk.tensor
                    def _(t):
                        for h in range(NH):
                            j = h // 4
                            t.wait_ge(SEM["QH%d" % (h % 2)], ev_qh[h])
                            for tau in range(4):
                                g = h * 4 + tau
                                nk = 4 * tau + 4
                                if g >= 2:
                                    t.wait_ge(SEM["DV"], ev_recip[GV[g - 2]])
                                    t.wait_ge(SEM["DV"], ev_cxe[GV[g - 2]])
                                if g >= 1:
                                    t.wait_ge(SEM["AC"], ev_rbe[GV[g - 1]])

                                def mm_scores(kc):
                                    if (h, tau, kc - 2) in ev_exp:
                                        t.wait_ge(SEM["AC"], ev_exp[(h, tau, kc - 2)])
                                    mm = t.matmul(sc_ps[kc % 2],
                                                  B[:, j * T + kc * 128:j * T + kc * 128 + 128],
                                                  qh[h % 2][:, tau * TW:(tau + 1) * TW],
                                                  start=True, stop=True)
                                    inc(mm, "PE1")

                                def mm_ctxden(kc):
                                    t.wait_ge(SEM["AC"], ev_exp[(h, tau, kc)])
                                    if (h, tau, kc) in ev_mask:
                                        t.wait_ge(SEM["DV"], ev_mask[(h, tau, kc)])
                                    t.matmul(cx_ps[g % 2],
                                             C[:, kc * 1024 + j * 128:kc * 1024 + (j + 1) * 128],
                                             pt[kc % 2][:],
                                             start=(kc == 0), stop=(kc == nk - 1))
                                    mm = t.matmul(dn_ps[g % 2], ones_col[:], pt[kc % 2][:],
                                                  start=(kc == 0), stop=(kc == nk - 1))
                                    inc(mm, "PE2")

                                mm_scores(0)
                                for kc in range(nk):
                                    if kc + 1 < nk:
                                        mm_scores(kc + 1)
                                    mm_ctxden(kc)
                                t.wait_ge(SEM["DV"], ev_recip[(h, tau)])
                                mm = t.matmul(rb_ps, ones_row[:], rrow_bf[:], start=True, stop=True)
                                inc(mm, "PE2")
                    @blk.scalar
                    def _(a):
                        for h in range(NH):
                            for tau in range(4):
                                g = h * 4 + tau
                                nk = 4 * tau + 4
                                for kc in range(nk):
                                    a.wait_ge(SEM["PE1"], ev_scores[(h, tau, kc)])
                                    if (h, tau, kc - 2) in ev_pt:
                                        a.wait_ge(SEM["PE2"], ev_pt[(h, tau, kc - 2)])
                                    ex = a.activation(pt[kc % 2][:], sc_ps[kc % 2], AF.Exp,
                                                      bias=cb_negC[:], scale=SM_SCALE)
                                    inc(ex, "AC")
                                a.wait_ge(SEM["PE2"], ev_rb[(h, tau)])
                                if g >= 1:
                                    a.wait_ge(SEM["DV"], ev_cxe[GV[g - 1]])
                                cp = a.copy(rb_sb[:], rb_ps)
                                inc(cp, "AC")
                    @blk.vector
                    def _(v):
                        v.wait_ge(SEM["SLD"], ev_mask_ld[0])
                        for h in range(NH):
                            for tau in range(4):
                                g = h * 4 + tau
                                nk = 4 * tau + 4
                                for kc in range(4 * tau, nk):
                                    off = kc * 128 - tau * 512
                                    v.wait_ge(SEM["AC"], ev_exp[(h, tau, kc)])
                                    mk = v.tensor_tensor(pt[kc % 2][:], pt[kc % 2][:],
                                                         maskt[:, 384 - off:384 - off + 512],
                                                         op=OP.mult)
                                    inc(mk, "DV")
                                v.wait_ge(SEM["PE2"], ev_grp[(h, tau)])
                                v.reciprocal(rr_ps, dn_ps[g % 2])
                                rc = v.tensor_copy(rrow_bf[:], rr_ps)
                                inc(rc, "DV")
                                v.wait_ge(SEM["AC"], ev_rbe[(h, tau)])
                                ce = v.tensor_tensor(
                                    A[:, h * T + tau * TW:h * T + (tau + 1) * TW],
                                    cx_ps[g % 2], rb_sb[:], op=OP.mult)
                                inc(ce, "DV")
                assert cnt["PE1"] == ev_scores[(NH - 1, 3, 15)]
                assert cnt["PE2"] == ev_rb[(NH - 1, 3)]
                assert cnt["DV"] == ev_cxe[(NH - 1, 3)]
                assert cnt["AC"] == ev_rbe[(NH - 1, 3)]

        def phase_xhome_init():
            with nc.Block() as blk:
                @blk.sync
                def _(s):
                    for i in range(NCH):
                        dma(s, "SST", x_home[i], AC_(i))

        def phase_s8x(oc):
            d0, s0 = cnt["DV"], cnt["SLD"]

            def ep(v, i, n, ps4):
                v.wait_ge(SEM["SLD"], s0 + (i + 1) * 16)
                last = None
                for ti in range(4):
                    sl = slice(ti * TW, (ti + 1) * TW)
                    last = v.tensor_tensor(B_xold[i % 3][:, sl], ps4[ti][:],
                                           B_xold[i % 3][:, sl], op=OP.add)
                inc(last, "DV")

            def sst(s, i, n, d0_):
                dma(s, "SLD", B_xold[i % 3][:], x_home[i])
                s.wait_ge(SEM["DV"], d0 + i + 1)
                dma(s, "SST", x_home[i], B_xold[i % 3][:])

            gemm_T(_un("s8x"), oc, list(range(NCH)), B_band, ep, dv_per_n=1, sst_hook=sst)

        def phase_ffn(final):
            with ExitStack() as ph:
                wsf = ph.enter_context(nc.sbuf_tensor(_un("ff_wsf"), [1, 4096], BF16))
                if final:
                    stgf = [ph.enter_context(nc.sbuf_tensor(_un(f"f3_sg{i}"), [128, 512], F32))
                            for i in range(2)]
                    stgo = B_stgo3
                else:
                    stgf = B_stgf2
                    stgo = B_stgo2
                pb = [PS8[0], PS8[1]]
                g0 = cnt["GLD"]
                with nc.Block() as blk:
                    @blk.gpsimd
                    def _(g):
                        for n in range(NCH):
                            dma(g, "GLD", biascols[:, n:n + 1],
                                fbias[0:1, n * 128:(n + 1) * 128]
                                .rearrange("o (p f) -> (o p) f", f=1))
                        dma(g, "GLD", wsf[:], wsums[0:1, 6144:10240])
                p0, d0 = cnt["PE1"], cnt["DV"]
                with nc.Block() as blk:
                    @blk.tensor
                    def _(t):
                        for ti in range(4):
                            if ti >= 2:
                                t.wait_ge(SEM["DV"], d0 + ti - 1)
                            mm = t.matmul(pb[ti % 2][:], ones_row[:],
                                          C_rstdrow[:, ti * TW:(ti + 1) * TW], start=True, stop=True)
                            inc(mm, "PE1")
                    @blk.vector
                    def _(v):
                        for ti in range(4):
                            v.wait_ge(SEM["PE1"], p0 + ti + 1)
                            tc = v.tensor_copy(C_rstdb[:, ti * TW:(ti + 1) * TW], pb[ti % 2][:])
                            inc(tc, "DV")
                st0 = cnt["SST"]

                def rank1(t, i, n, psu, ti):
                    return t.matmul(psu[:], wsf[0:1, n * 128:(n + 1) * 128],
                                    C_meanrow[:, ti * TW:(ti + 1) * TW], start=False, stop=True)

                def ep(v, i, n, ps4):
                    v.wait_ge(SEM["GLD"], g0 + 33 * 16)
                    if i >= 2:
                        v.wait_ge(SEM["SST"], st0 + (i - 1) * 16)
                    last = None
                    for ti in range(4):
                        sl = slice(ti * TW, (ti + 1) * TW)
                        v.tensor_tensor(stgf[ti % 2][:], ps4[ti][:], C_rstdb[:, sl], op=OP.mult)
                        v.tensor_scalar(stgf[ti % 2][:], stgf[ti % 2][:],
                                        biascols[:, n:n + 1], None, op0=OP.add)
                        last = v.tensor_tensor(stgo[i % 2][:, sl], stgf[ti % 2][:],
                                               AC_(n, ti * TW, TW), op=OP.add)
                    inc(last, "DV")

                d2 = cnt["DV"]

                def sst2(s, i, n, d0_):
                    s.wait_ge(SEM["DV"], d2 + i + 1)
                    if final:
                        dma(s, "SST", y_out[n * 128:(n + 1) * 128, :], stgo[i % 2][:])
                    else:
                        dma(s, "SST", x_home[n], stgo[i % 2][:])

                gemm_T("ff" + ("3" if final else "2"), fcache, list(range(NCH)),
                       B_band, ep, dv_per_n=1, sst_hook=sst2, rank1=rank1)

        def dump_fm(name):
            d = mkdbg(name, [D, T])
            with nc.Block() as blk:
                @blk.gpsimd
                def _(g):
                    for qd in range(4):
                        dma(g, "GLD",
                            d[qd * 1024:(qd + 1) * 1024, :].rearrange("(c p) t -> p c t", p=128),
                            A[:, qd * 8 * T:(qd + 1) * 8 * T].rearrange("p (c t) -> p c t", t=T))

        def dump_sbuf(name, ap, shape):
            d = mkdbg(name, list(shape))
            with nc.Block() as blk:
                @blk.gpsimd
                def _(g):
                    dma(g, "GLD", d[:], ap)

        def dump_dram(name, src, shape):
            d = mkdbg(name, list(shape), BF16)
            with nc.Block() as blk:
                @blk.sync
                def _(s):
                    s.wait_ge(SEM["SST"], cnt["SST"])
                    dma(s, "SST", d[:], src[:])

        def finish():
            with nc.Block() as blk:
                @blk.sync
                def _(s):
                    s.wait_ge(SEM["SST"], cnt["SST"])
                    s.wait_ge(SEM["GLD"], cnt["GLD"])
                    s.wait_ge(SEM["SLD"], cnt["SLD"])
                    s.wait_ge(SEM["ALD"], cnt["ALD"])

        p0_attn_a()
        p0_ffn()
        p0_vb()
        p0_colsum(lambda nb: bcache[nb * 128:(nb + 1) * 128, :], 48, 10240)
        p0_colsum(lambda nb: qcache[nb], 32, 0)
        p0_colsum(lambda nb: kcache[nb], 8, 4096)
        if stop_after == "p0":
            if "caches" in taps:
                dump_dram("ws", wsums, [1, 16384])
                dump_dram("oc", ocacheT, [NCH, 128, D])
                dump_dram("qc", qcache, [NH, 128, D])
                dump_dram("kc", kcache, [NKV, 128, D])
                dump_dram("vc", vcache, [128, 2, NCH, 512])
                dump_dram("fc", fcache, [NCH, 128, D])
                dump_dram("vbc", vbcache, [128, 2, NCH, 512])
            finish()
            return nc, dbg

        WSOFF = {0: (0, 4096, 5120), 1: (10240, 14336, 15360)}

        def attn_layer(layer):
            with ExitStack() as lh:
                oq, ok, ov = WSOFF[layer]
                phase_stats(first=(layer == 0), ffn=False)
                if layer == 0:
                    phase_xhome_init()
                with ExitStack() as lh2:
                    wsq = lh2.enter_context(nc.sbuf_tensor(_un(f"ws_q{layer}"), [1, 4096], BF16))
                    wsk = lh2.enter_context(nc.sbuf_tensor(_un(f"ws_k{layer}"), [1, 1024], BF16))
                    phase_tables([(wsq[:], wsums[0:1, oq:oq + 4096]),
                                  (wsk[:], wsums[0:1, ok:ok + 1024])])
                    phase_s4_qk(layer, wsq, wsk)
                phase_s4_v(layer, ov)
                if layer == 0 and stop_after == "qkv":
                    return True
                phase_s6(layer)
                if layer == 0 and stop_after == "ctx":
                    return True
                phase_s8x(ocacheT if layer == 0 else
                          ocache.rearrange("(n p) e -> n p e", p=128))
            return False

        stop = attn_layer(0)
        if stop_after == "qkv":
            if "qkv" in taps:
                dump_dram("q", qspill, [NH, 128, T])
                dump_sbuf("k", B[:], [128, 8 * T])
                dump_sbuf("v", C[:], [128, 16 * 1024])
            finish()
            return nc, dbg
        if stop_after == "ctx":
            if "ctx" in taps:
                dump_fm("ctx")
            finish()
            return nc, dbg
        if stop_after == "sub1":
            if "x1" in taps:
                dump_dram("x1", x_home, [NCH, 128, T])
            finish()
            return nc, dbg

        phase_stats(first=False, ffn=True)
        phase_ffn(final=False)
        if stop_after == "sub2":
            if "x2" in taps:
                dump_dram("x2", x_home, [NCH, 128, T])
            finish()
            return nc, dbg

        attn_layer(1)
        if stop_after == "sub3":
            if "x3" in taps:
                dump_dram("x3", x_home, [NCH, 128, T])
            finish()
            return nc, dbg

        phase_stats(first=False, ffn=True)
        phase_ffn(final=True)
        finish()
        return nc, dbg


from concourse.bass_utils import run_bass_kernel_spmd


def kernel(**inputs):
    x = np.asarray(inputs["L0_attn_a_hidden_states"])
    consts = host_constants()
    shared = {
        "qw_q": np.ascontiguousarray(np.asarray(inputs["L0_attn_a_qw_q"], dtype=np.int32)),
        "qw_k": np.ascontiguousarray(np.asarray(inputs["L0_attn_a_qw_k"], dtype=np.int32)),
        "qw_v": np.ascontiguousarray(np.asarray(inputs["L0_attn_a_qw_v"], dtype=np.int32)),
        "qw_o": np.ascontiguousarray(np.asarray(inputs["L0_attn_a_qw_o"], dtype=np.int32)),
        "sc_q": np.ascontiguousarray(np.asarray(inputs["L0_attn_a_scales_q"], dtype=np.float32)),
        "sc_k": np.ascontiguousarray(np.asarray(inputs["L0_attn_a_scales_k"], dtype=np.float32)),
        "sc_v": np.ascontiguousarray(np.asarray(inputs["L0_attn_a_scales_v"], dtype=np.float32)),
        "sc_o": np.ascontiguousarray(np.asarray(inputs["L0_attn_a_scales_o"], dtype=np.float32)),
        "w_in": np.ascontiguousarray(np.asarray(inputs["L0_attn_b_w_in"], dtype=np.float32)),
        "w_out": np.ascontiguousarray(np.asarray(inputs["L0_attn_b_w_out"], dtype=np.float32)),
        "fq": np.ascontiguousarray(np.asarray(inputs["L0_ffn_qweight"], dtype=np.int32).reshape(D, D)),
        "fsc": np.ascontiguousarray(np.asarray(inputs["L0_ffn_scales"], dtype=np.float32).reshape(D, 32)),
        "fzp": np.ascontiguousarray(np.asarray(inputs["L0_ffn_scaled_zeros"], dtype=np.float32).reshape(D, 32)),
        "fbias": np.ascontiguousarray(np.asarray(inputs["L0_ffn_bias"], dtype=np.float32).reshape(1, D)),
        "cosbase": consts["cosbase"],
        "sinbase": consts["sinbase"],
        "maskm": consts["maskm"],
        "ident": consts["ident"],
    }
    nc, _dbg = build_program()
    in_maps = []
    for b in range(8):
        m = dict(shared)
        m["x_fm"] = np.ascontiguousarray(np.asarray(x[b], dtype=np.float32).T)
        in_maps.append(m)
    res = run_bass_kernel_spmd(nc, in_maps, core_ids=list(range(8)))
    out = np.stack([np.ascontiguousarray(np.asarray(r["y"]).T) for r in res.results], axis=0)
    return out


# version 03
# speedup vs baseline: 1.2055x; 1.1369x over previous
import math
import numpy as np
import concourse.bass as bass
import concourse.mybir as mybir
from concourse.alu_op_type import AluOpType as OP
from contextlib import ExitStack

F32, BF16, I32 = mybir.dt.float32, mybir.dt.bfloat16, mybir.dt.int32
AF = mybir.ActivationFunctionType

D, T, NCH, TW = 4096, 2048, 32, 512
NH, NKV, HD = 32, 8, 128
EXPC = 10.0
SM_SCALE = 1.0 / math.sqrt(HD)
EPS = 1e-5


def host_constants():
    import ml_dtypes
    inv_freq = 1.0 / (10000.0 ** (np.arange(0, HD, 2, dtype=np.float64) / HD))
    ang = np.arange(T, dtype=np.float64)[:, None] * inv_freq[None, :]
    cos, sin = np.cos(ang).T, np.sin(ang).T
    kk = np.arange(128)[:, None]
    jj = np.arange(896)[None, :]
    return {
        "cosbase": np.concatenate([cos, cos], 0).astype(ml_dtypes.bfloat16),
        "sinbase": np.concatenate([-sin, sin], 0).astype(ml_dtypes.bfloat16),
        "maskm": (kk + 384 <= jj).astype(np.float32).astype(ml_dtypes.bfloat16),
        "ident": np.eye(128, dtype=np.float32).astype(ml_dtypes.bfloat16),
    }


_unctr = {}
def _un(nm):
    _unctr[nm] = _unctr.get(nm, 0) + 1
    return f"{nm}_{_unctr[nm]}"


def build_program(stop_after="all", debug_taps=()):
    _unctr.clear()
    nc = bass.Bass("TRN2", target_bir_lowering=False)
    taps = set(debug_taps)

    x_in = nc.dram_tensor("x_fm", [D, T], F32, kind="ExternalInput").ap()
    qw = {
        "q": nc.dram_tensor("qw_q", [D, 4096], I32, kind="ExternalInput").ap(),
        "k": nc.dram_tensor("qw_k", [D, 1024], I32, kind="ExternalInput").ap(),
        "v": nc.dram_tensor("qw_v", [D, 1024], I32, kind="ExternalInput").ap(),
        "o": nc.dram_tensor("qw_o", [4096, D], I32, kind="ExternalInput").ap(),
    }
    sc = {
        "q": nc.dram_tensor("sc_q", [32, 4096], F32, kind="ExternalInput").ap(),
        "k": nc.dram_tensor("sc_k", [32, 1024], F32, kind="ExternalInput").ap(),
        "v": nc.dram_tensor("sc_v", [32, 1024], F32, kind="ExternalInput").ap(),
        "o": nc.dram_tensor("sc_o", [32, 4096], F32, kind="ExternalInput").ap(),
    }
    w_in = nc.dram_tensor("w_in", [6144, D], F32, kind="ExternalInput").ap()
    w_out = nc.dram_tensor("w_out", [D, D], F32, kind="ExternalInput").ap()
    fq = nc.dram_tensor("fq", [D, D], I32, kind="ExternalInput").ap()
    fsc = nc.dram_tensor("fsc", [D, 32], F32, kind="ExternalInput").ap()
    fzp = nc.dram_tensor("fzp", [D, 32], F32, kind="ExternalInput").ap()
    fbias = nc.dram_tensor("fbias", [1, D], F32, kind="ExternalInput").ap()
    cosb_h = nc.dram_tensor("cosbase", [128, T], BF16, kind="ExternalInput").ap()
    sinb_h = nc.dram_tensor("sinbase", [128, T], BF16, kind="ExternalInput").ap()
    maskm = nc.dram_tensor("maskm", [128, 896], BF16, kind="ExternalInput").ap()
    ident_h = nc.dram_tensor("ident", [128, 128], BF16, kind="ExternalInput").ap()
    y_out = nc.dram_tensor("y", [D, T], F32, kind="ExternalOutput").ap()

    x_home = nc.dram_tensor("x_home", [NCH, 128, T], BF16, kind="Internal").ap()
    qspill = nc.dram_tensor("qspill", [NH, 128, T], BF16, kind="Internal").ap()
    qcache = nc.dram_tensor("qcache", [NH, 128, D], BF16, kind="Internal").ap()
    kcache = nc.dram_tensor("kcache", [NKV, 128, D], BF16, kind="Internal").ap()
    vcache = nc.dram_tensor("vcache", [128, 2, NCH, 512], BF16, kind="Internal").ap()
    fcache = nc.dram_tensor("fcache", [NCH, 128, D], BF16, kind="Internal").ap()
    bcache = nc.dram_tensor("bcache", [6144, D], BF16, kind="Internal").ap()
    ocache = nc.dram_tensor("ocache", [D, D], BF16, kind="Internal").ap()
    vbcache = nc.dram_tensor("vbcache", [128, 2, NCH, 512], BF16, kind="Internal").ap()
    wsums = nc.dram_tensor("wsums", [1, 16384], BF16, kind="Internal").ap()
    ocacheT = nc.dram_tensor("ocacheT", [NCH, 128, D], BF16, kind="Internal").ap()

    dbg = {}

    def mkdbg(name, shape, dtype=F32):
        if name not in dbg:
            dbg[name] = nc.dram_tensor("dbg_" + name, shape, dtype, kind="ExternalOutput").ap()
        return dbg[name]

    es = ExitStack()
    with es:
        sbt = lambda name, shape, dtype: es.enter_context(nc.sbuf_tensor(name, shape, dtype))
        A = sbt("A", [128, NCH * T], BF16)
        B = sbt("B", [128, 16384], BF16)
        C = sbt("C", [128, 16384], BF16)
        ones_row = sbt("ones_row", [1, 128], BF16)
        ones_col = sbt("ones_col", [128, 1], BF16)
        cb_negC = sbt("cb_negC", [128, 1], F32)
        cb_eps = sbt("cb_eps", [128, 1], F32)
        ident = sbt("ident_s", [128, 128], BF16)
        ident32 = sbt("ident32", [32, 32], F32)
        rstdcols = sbt("rstdcols", [128, 16], F32)
        biascols = sbt("biascols", [128, 32], F32)

        PS8 = [es.enter_context(nc.psum_tensor(f"gps{i}", [128, 512], F32)) for i in range(8)]
        PSB = [p[:].bitcast(BF16) for p in PS8]
        SEM = {}
        cnt = {}
        for k in ["GLD", "SLD", "ALD", "SST", "PE1", "PE2", "DV", "AC",
                  "XLD", "SC0", "SC1", "BD0", "BD1", "QH0", "QH1", "QS0", "QS1",
                  "STG", "FF0", "FF1", "WCC"]:
            SEM[k] = es.enter_context(nc.semaphore(k))
            cnt[k] = 0

        def dma(engs, key, out, in_):
            engs.dma_start(out, in_).then_inc(SEM[key], 16)
            cnt[key] += 16
            return cnt[key]

        def dma_t(engs, key, out, in_):
            engs.dma_start_transpose(out, in_).then_inc(SEM[key], 16)
            cnt[key] += 16
            return cnt[key]

        def inc(instr, key, n=1):
            instr.then_inc(SEM[key], n)
            cnt[key] += n
            return cnt[key]

        def AC_(c, t0=0, ww=T):
            return A[:, c * T + t0:c * T + t0 + ww]

        C_cos = C[:, 0:2048]
        C_sin = C[:, 2048:4096]
        C_qsm = [C[:, 4096:4608], C[:, 4608:5120]]
        C_rt1 = C[:, 5120:5632]
        C_rt2 = C[:, 5632:6144]
        C_band = [C[:, 6144 + i * 4096:6144 + (i + 1) * 4096]
                  .rearrange("p (c f) -> p c f", f=128) for i in range(2)]
        C_strow = C[0:1, 0:8192].bitcast(F32)
        C_sq = C[:, 8192:10240]
        C_rstdrow = C[0:1, 10240:12288]
        C_meanrow = C[0:1, 14336:16384]
        C_deq = C.rearrange("p (c f) -> p c f", f=512)
        C_rstdb = C[:, 2048:4096]
        B_stage = B.rearrange("p (c f) -> p c f", f=512)
        B_stage4 = B.rearrange("p (j f) -> p j f", f=D)
        B_band = [B[:, i * 4096:(i + 1) * 4096].rearrange("p (c f) -> p c f", f=128)
                  for i in range(2)]
        B_xold = [B[:, 8192 + i * 2048:8192 + (i + 1) * 2048] for i in range(3)]
        B_stgo2 = [B[:, 8192 + i * 2048:8192 + (i + 1) * 2048] for i in range(2)]
        B_stgf2 = [B[:, 12288 + i * 1024:12288 + (i + 1) * 1024].bitcast(F32) for i in range(2)]
        B_stgo3 = [B[:, 8192 + i * 4096:8192 + (i + 1) * 4096].bitcast(F32) for i in range(2)]

        with nc.Block() as blk:
            @blk.vector
            def _(v):
                v.memset(ones_row[:], 1.0)
                v.memset(ones_col[:], 1.0)
                v.memset(cb_negC[:], -EXPC)
                v.memset(cb_eps[:], EPS)
                v.wait_ge(SEM["SLD"], 16)
                v.tensor_copy(ident32[:], ident[0:32, 0:32])
            @blk.sync
            def _(s):
                dma(s, "SLD", ident[:], ident_h[:])
            @blk.gpsimd
            def _(g):
                for qd in range(4):
                    dma(g, "XLD",
                        A[:, qd * 8 * T:(qd + 1) * 8 * T].rearrange("p (c t) -> p c t", t=T),
                        x_in[qd * 1024:(qd + 1) * 1024, :].rearrange("(c p) t -> p c t", p=128))
                dma(g, "WCC", bcache[:], w_in[:])
                dma(g, "WCC", ocache[:], w_out[:])
        GLD_X = 4 * 16
        GLD_PRE = cnt["GLD"]
        SLD_CONST = cnt["SLD"]

        def p0_attn_a():
            with ExitStack() as ph:
                gath = ph.enter_context(nc.sbuf_tensor("p0_gath", [128, D], BF16))
                scratch = ph.enter_context(nc.sbuf_tensor("p0_scr", [128, 1024], F32))
                scl = scratch[0:32, 0:512]
                scbc = [scratch[:, 0:512], scratch[:, 512:1024]]
                scT = [ph.enter_context(nc.sbuf_tensor(f"p0_scT{i}", [128, 32], F32)) for i in range(4)]
                n8T = [ph.enter_context(nc.sbuf_tensor(f"p0_n8T{i}", [128, 32], F32)) for i in range(4)]
                wsrow = ph.enter_context(nc.sbuf_tensor("p0_wsr", [1, 512], BF16))
                pst = [PSB[0][:, 0:128], PSB[1][:, 0:128]]
                psS = PS8[2][:, 0:32]
                ws_ps = PS8[3][0:1, :]
                bands = [("q", b, "t") for b in range(8)] + \
                        [("k", b, "t") for b in range(2)] + \
                        [("o", b, "t") for b in range(8)] + \
                        [("v", b, "v") for b in range(2)]
                ws_off = {"q": 0, "k": 4096, "v": 5120}
                dstmap = {"q": qcache, "k": kcache, "o": ocacheT}
                tband = [x for x in bands if x[2] == "t"]
                ev_stage = {}
                ev_scl = {}
                with nc.Block() as blk:
                    @blk.gpsimd
                    def _(g):
                        wname, b, _k = tband[0]
                        for cq in range(4):
                            ev_stage[0] = dma(g, "GLD", B_stage[:, cq * 8:(cq + 1) * 8, :],
                                              qw[wname][cq * 1024:(cq + 1) * 1024,
                                                        b * 512:b * 512 + 512]
                                              .rearrange("(c p) f -> p c f", p=128))
                    @blk.sync
                    def _(s):
                        s.wait_ge(SEM["SLD"], 16)
                        wname, b, _k = tband[0]
                        ev_scl[0] = dma(s, "SLD", scl, sc[wname][:, b * 512:b * 512 + 512])
                for bi, (wname, b, kind) in enumerate(tband):
                    col = b * 512
                    d0, p0, a0, s0 = cnt["DV"], cnt["PE1"], cnt["AC"], cnt["SST"]
                    with nc.Block() as blk:
                        @blk.gpsimd
                        def _(g):
                            if bi + 1 < len(tband):
                                wn2, b2, _k2 = tband[bi + 1]
                                g.wait_ge(SEM["PE1"], p0 + 4 + 128)
                                for cq in range(4):
                                    ev_stage[bi + 1] = dma(
                                        g, "GLD", B_stage[:, cq * 8:(cq + 1) * 8, :],
                                        qw[wn2][cq * 1024:(cq + 1) * 1024,
                                                b2 * 512:b2 * 512 + 512]
                                        .rearrange("(c p) f -> p c f", p=128))
                        @blk.sync
                        def _(s):
                            if bi + 1 < len(tband):
                                wn2, b2, _k2 = tband[bi + 1]
                                s.wait_ge(SEM["PE1"], p0 + 4)
                                ev_scl[bi + 1] = dma(s, "SLD", scl,
                                                     sc[wn2][:, b2 * 512:b2 * 512 + 512])
                            for nl in range(4):
                                s.wait_ge(SEM["AC"], a0 + (nl + 1) * 32)
                                dma(s, "SST", dstmap[wname][4 * b + nl], gath[:])
                        @blk.tensor
                        def _(t):
                            t.wait_ge(SEM["SLD"], ev_scl[bi])
                            t.wait_ge(SEM["GLD"], ev_stage[bi])
                            for nl in range(4):
                                if nl >= 1:
                                    t.wait_ge(SEM["DV"], d0 + 2 * nl)
                                tp = t.transpose(psS, scl[:, nl * 128:(nl + 1) * 128],
                                                 ident32[:])
                                inc(tp, "PE1")
                            for nl in range(4):
                                for c in range(NCH):
                                    kk = nl * 32 + c
                                    if kk >= 2:
                                        t.wait_ge(SEM["AC"], a0 + kk - 1)
                                    tp = t.transpose(pst[c % 2],
                                                     B_stage[:, c, nl * 128:(nl + 1) * 128],
                                                     ident[:])
                                    inc(tp, "PE1")
                        @blk.vector
                        def _(v):
                            for nl in range(4):
                                v.wait_ge(SEM["PE1"], p0 + nl + 1)
                                tc = v.tensor_copy(scT[nl][:], psS)
                                inc(tc, "DV")
                                ts = v.tensor_scalar(n8T[nl][:], psS, -8.0, None,
                                                     op0=OP.mult)
                                inc(ts, "DV")
                        @blk.scalar
                        def _(a):
                            for nl in range(4):
                                a.wait_ge(SEM["DV"], d0 + 2 * nl + 2)
                                a.wait_ge(SEM["SST"], s0 + nl * 16)
                                for c in range(NCH):
                                    a.wait_ge(SEM["PE1"], p0 + 4 + nl * 32 + c + 1)
                                    cp = a.activation(gath[:, c * 128:(c + 1) * 128],
                                                      pst[c % 2], AF.Identity,
                                                      bias=n8T[nl][:, c:c + 1],
                                                      scale=scT[nl][:, c:c + 1])
                                    inc(cp, "AC")
                for bi, (wname, b, kind) in enumerate([x for x in bands if x[2] == "v"]):
                    col = b * 512
                    if True:
                        with nc.Block() as blk:
                            @blk.gpsimd
                            def _(g):
                                g.wait_ge(SEM["PE1"], cnt["PE1"])
                                g.wait_ge(SEM["DV"], cnt["DV"])
                                for cq in range(4):
                                    dma(g, "GLD", B_stage[:, cq * 8:(cq + 1) * 8, :],
                                        qw[wname][cq * 1024:(cq + 1) * 1024, col:col + 512]
                                        .rearrange("(c p) f -> p c f", p=128))
                            @blk.sync
                            def _(s):
                                s.wait_ge(SEM["GLD"], cnt["GLD"])
                                s.wait_ge(SEM["SST"], cnt["SST"])
                        d0, p0, a0, s0 = cnt["DV"], cnt["PE1"], cnt["AC"], cnt["SST"]
                        ev_sc = {}
                        with nc.Block() as blk:
                            @blk.sync
                            def _(s):
                                for c in range(NCH):
                                    if c >= 2:
                                        s.wait_ge(SEM["DV"], d0 + c - 1)
                                    ev_sc[c] = dma(s, "SC%d" % (c % 2), scbc[c % 2][:],
                                                   sc[wname][c:c + 1, col:col + 512]
                                                   .partition_broadcast(128).squeeze(1))
                                for c in range(NCH):
                                    s.wait_ge(SEM["DV"], d0 + c + 1)
                                    dma(s, "SST", vcache[:, b, c, :], C_deq[:, c, :])
                            @blk.vector
                            def _(v):
                                for c in range(NCH):
                                    v.wait_ge(SEM["SC%d" % (c % 2)], ev_sc[c])
                                    st = v.scalar_tensor_tensor(C_deq[:, c, :], B_stage[:, c, :],
                                                                -8.0, scbc[c % 2][:],
                                                                op0=OP.add, op1=OP.mult)
                                    inc(st, "DV")
                        d1, p1 = cnt["DV"], cnt["PE1"]
                        with nc.Block() as blk:
                            @blk.tensor
                            def _(t):
                                mm = None
                                for c in range(NCH):
                                    mm = t.matmul(ws_ps, ones_col[:], C_deq[:, c, :],
                                                  start=(c == 0), stop=(c == NCH - 1))
                                inc(mm, "PE1")
                            @blk.vector
                            def _(v):
                                v.wait_ge(SEM["PE1"], p1 + 1)
                                ws = v.tensor_copy(wsrow[:], ws_ps)
                                inc(ws, "DV")
                            @blk.sync
                            def _(s):
                                s.wait_ge(SEM["DV"], cnt["DV"])
                                dma(s, "SST", wsums[0:1, 5120 + b * 512:5120 + (b + 1) * 512],
                                    wsrow[:])

        def p0_ffn():
            with ExitStack() as ph:
                fscb = [ph.enter_context(nc.sbuf_tensor(f"p0_fs{i}", [128, 32], F32)) for i in range(2)]
                fzpb = [ph.enter_context(nc.sbuf_tensor(f"p0_fz{i}", [128, 32], F32)) for i in range(2)]
                wcol32 = ph.enter_context(nc.sbuf_tensor("p0_wc32", [128, 1], F32))
                wcol = ph.enter_context(nc.sbuf_tensor("p0_wc", [128, 2], BF16))
                wrow = ph.enter_context(nc.sbuf_tensor("p0_wr", [1, 128], BF16))
                ws_ps = PS8[2][0:2, 0:128]
                Bs = [B[:, 0:4096], B[:, 4096:8192]]
                Cd = [C[:, 0:4096], C[:, 4096:8192]]
                with nc.Block() as blk:
                    @blk.sync
                    def _(s):
                        s.wait_ge(SEM["SST"], cnt["SST"])
                g0, d0, s0 = cnt["GLD"], cnt["DV"], cnt["SST"]
                p00, a00, sst00 = cnt["PE1"], cnt["AC"], cnt["SST"]
                ev_ff = {}
                with nc.Block() as blk:
                    @blk.gpsimd
                    def _(g):
                        for n in range(NCH):
                            if n >= 2:
                                g.wait_ge(SEM["DV"], d0 + 2 * (n - 2) + 1)
                            dma(g, "FF%d" % (n % 2), Bs[n % 2], fq[n * 128:(n + 1) * 128, :])
                            dma(g, "FF%d" % (n % 2), fscb[n % 2][:], fsc[n * 128:(n + 1) * 128, :])
                            ev_ff[n] = dma(g, "FF%d" % (n % 2), fzpb[n % 2][:], fzp[n * 128:(n + 1) * 128, :])
                    @blk.vector
                    def _(v):
                        for n in range(NCH):
                            v.wait_ge(SEM["FF%d" % (n % 2)], ev_ff[n])
                            if n >= 2:
                                v.wait_ge(SEM["SST"], s0 + (n - 1) * 32)
                            ts = None
                            for gg in range(32):
                                ts = v.tensor_scalar(Cd[n % 2][:, gg * 128:(gg + 1) * 128],
                                                     Bs[n % 2][:, gg * 128:(gg + 1) * 128],
                                                     fscb[n % 2][:, gg:gg + 1], fzpb[n % 2][:, gg:gg + 1],
                                                     op0=OP.mult, op1=OP.add)
                            inc(ts, "DV")
                            if n >= 1:
                                v.wait_ge(SEM["AC"], a00 + 3 * (n - 1) + 1)
                            rd = v.tensor_reduce(wcol32[:], Cd[n % 2], axis=mybir.AxisListType.X,
                                                 op=OP.add)
                            inc(rd, "DV")
                    @blk.scalar
                    def _(a):
                        for n in range(NCH):
                            a.wait_ge(SEM["DV"], d0 + 2 * n + 2)
                            if n >= 1:
                                a.wait_ge(SEM["PE1"], p00 + n)
                            cp = a.copy(wcol[:, 0:1], wcol32[:])
                            inc(cp, "AC")
                            z = a.activation(wcol[:, 1:2], wcol32[:], AF.Copy, bias=0.0, scale=0.0)
                            inc(z, "AC")
                            a.wait_ge(SEM["PE1"], p00 + n + 1)
                            if n >= 1:
                                a.wait_ge(SEM["SST"], sst00 + n * 32)
                            cp2 = a.copy(wrow[:], ws_ps[0:1, :])
                            inc(cp2, "AC")
                    @blk.tensor
                    def _(t):
                        for n in range(NCH):
                            t.wait_ge(SEM["AC"], a00 + 3 * n + 2)
                            mm = t.matmul(ws_ps, wcol[:], ident[:], start=True, stop=True)
                            inc(mm, "PE1")
                    @blk.sync
                    def _(s):
                        for n in range(NCH):
                            s.wait_ge(SEM["DV"], d0 + 2 * n + 1)
                            dma(s, "SST", fcache[n], Cd[n % 2])
                            s.wait_ge(SEM["AC"], a00 + 3 * n + 3)
                            dma(s, "SST", wsums[0:1, 6144 + n * 128:6144 + (n + 1) * 128], wrow[:])
                if "p0ffn" in taps:
                    dcol = mkdbg("ffncol", [128, 1])
                    dcol2 = mkdbg("ffncol2", [128, 2])
                    drow = mkdbg("ffnrow", [1, 128])
                    with nc.Block() as blk2:
                        @blk2.gpsimd
                        def _(g):
                            dma(g, "GLD", dcol[:], wcol32[:])
                            dma(g, "GLD", dcol2[:], wcol[:])
                            dma(g, "GLD", drow[:], wrow[:])

        def p0_vb():
            with ExitStack() as ph:
                pst = [PSB[0][:, 0:128], PSB[1][:, 0:128]]
                for dvh in range(2):
                    with nc.Block() as blk:
                        @blk.sync
                        def _(s):
                            s.wait_ge(SEM["GLD"], cnt["GLD"])
                            s.wait_ge(SEM["SST"], cnt["SST"])
                            dma(s, "SLD", B_stage4[:, :, :],
                                bcache[5120 + dvh * 512:5120 + (dvh + 1) * 512, :]
                                .rearrange("(j p) e -> p j e", p=128))
                    p0, a0, s0 = cnt["PE1"], cnt["AC"], cnt["SLD"]
                    with nc.Block() as blk:
                        @blk.tensor
                        def _(t):
                            t.wait_ge(SEM["SLD"], s0)
                            for jj in range(4):
                                for c in range(NCH):
                                    kk = jj * 32 + c
                                    if kk >= 2:
                                        t.wait_ge(SEM["AC"], a0 + kk - 1)
                                    tp = t.transpose(pst[c % 2],
                                                     B_stage4[:, jj, c * 128:(c + 1) * 128], ident[:])
                                    inc(tp, "PE1")
                        @blk.scalar
                        def _(a):
                            for jj in range(4):
                                for c in range(NCH):
                                    a.wait_ge(SEM["PE1"], p0 + jj * 32 + c + 1)
                                    cp = a.copy(C[:, c * 512 + jj * 128:c * 512 + (jj + 1) * 128],
                                                pst[c % 2])
                                    inc(cp, "AC")
                        @blk.sync
                        def _(s):
                            s.wait_ge(SEM["AC"], a0 + 128)
                            dma(s, "SST", vbcache[:, dvh, :, :], C_deq[:, :, :])

        _cs_ctr = [0]
        def p0_colsum(srcrows, NB, ws_base):
            _cs_ctr[0] += 1
            with ExitStack() as ph:
                wcol32 = ph.enter_context(nc.sbuf_tensor(f"p0d_wc32_{_cs_ctr[0]}", [128, 1], F32))
                wcol = ph.enter_context(nc.sbuf_tensor(f"p0d_wc_{_cs_ctr[0]}", [128, 2], BF16))
                wrow = ph.enter_context(nc.sbuf_tensor(f"p0d_wr_{_cs_ctr[0]}", [1, 128], BF16))
                ws_ps = PS8[2][0:2, 0:128]
                stg2 = [B[:, 0:4096], B[:, 4096:8192]]
                slotn = ["STG", "FF0"]
                d0, p0, a0 = cnt["DV"], cnt["PE1"], cnt["AC"]
                sst_b0 = cnt["SST"]
                ev_st = {}
                with nc.Block() as blk:
                    @blk.sync
                    def _(s):
                        s.wait_ge(SEM["SST"], cnt["SST"])
                        s.wait_ge(SEM["WCC"], 32)
                        for nb in range(NB):
                            if nb >= 2:
                                s.wait_ge(SEM["DV"], d0 + nb - 1)
                            ev_st[nb] = dma(s, slotn[nb % 2], stg2[nb % 2], srcrows(nb))
                            if nb >= 1:
                                s.wait_ge(SEM["AC"], a0 + 3 * nb)
                                dma(s, "SST",
                                    wsums[0:1, ws_base + (nb - 1) * 128:ws_base + nb * 128],
                                    wrow[:])
                        s.wait_ge(SEM["AC"], a0 + 3 * NB)
                        dma(s, "SST", wsums[0:1, ws_base + (NB - 1) * 128:ws_base + NB * 128],
                            wrow[:])
                    @blk.vector
                    def _(v):
                        for nb in range(NB):
                            v.wait_ge(SEM[slotn[nb % 2]], ev_st[nb])
                            if nb >= 1:
                                v.wait_ge(SEM["AC"], a0 + 3 * (nb - 1) + 1)
                            rd = v.tensor_reduce(wcol32[:], stg2[nb % 2], axis=mybir.AxisListType.X,
                                                 op=OP.add)
                            inc(rd, "DV")
                    @blk.scalar
                    def _(a):
                        for nb in range(NB):
                            a.wait_ge(SEM["DV"], d0 + nb + 1)
                            if nb >= 1:
                                a.wait_ge(SEM["PE1"], p0 + nb)
                            cp = a.copy(wcol[:, 0:1], wcol32[:])
                            inc(cp, "AC")
                            z = a.activation(wcol[:, 1:2], wcol32[:], AF.Copy, bias=0.0, scale=0.0)
                            inc(z, "AC")
                            a.wait_ge(SEM["PE1"], p0 + nb + 1)
                            if nb >= 1:
                                a.wait_ge(SEM["SST"], sst_b0 + nb * 16)
                            cp2 = a.copy(wrow[:], ws_ps[0:1, :])
                            inc(cp2, "AC")
                    @blk.tensor
                    def _(t):
                        for nb in range(NB):
                            t.wait_ge(SEM["AC"], a0 + 3 * nb + 2)
                            mm = t.matmul(ws_ps, wcol[:], ident[:], start=True, stop=True)
                            inc(mm, "PE1")

        def phase_stats(first, ffn):
            with ExitStack() as ph:
                sums = [PS8[i][0:1, :] for i in range(4)]
                sqs = [PS8[4 + i][0:1, :] for i in range(4)]
                sqb = [C[:, 8192:10240], C[:, 10240:12288]]
                if not first:
                    with nc.Block() as blk:
                        @blk.sync
                        def _(s):
                            s.wait_ge(SEM["SST"], cnt["SST"])
                            for qd in range(4):
                                dma(s, "SLD",
                                    A[:, qd * 8 * T:(qd + 1) * 8 * T].rearrange("p (c t) -> p c t", t=T),
                                    x_home[qd * 8:(qd + 1) * 8].rearrange("c p t -> p c t"))
                d0, p0 = cnt["DV"], cnt["PE1"]
                sld_x = cnt["SLD"]
                with nc.Block() as blk:
                    @blk.vector
                    def _(v):
                        if first:
                            v.wait_ge(SEM["XLD"], GLD_X)
                        else:
                            v.wait_ge(SEM["SLD"], sld_x)
                        for c in range(NCH):
                            if c >= 2:
                                v.wait_ge(SEM["PE1"], p0 + c - 1)
                            tt = v.tensor_tensor(sqb[c % 2][:], AC_(c), AC_(c), op=OP.mult)
                            inc(tt, "DV")
                    @blk.tensor
                    def _(t):
                        for c in range(NCH):
                            t.wait_ge(SEM["DV"], d0 + c + 1)
                            mm = None
                            for ti in range(4):
                                t.matmul(sums[ti], ones_col[:], AC_(c, ti * TW, TW),
                                         start=(c == 0), stop=(c == NCH - 1))
                                mm = t.matmul(sqs[ti], ones_col[:], sqb[c % 2][:, ti * TW:(ti + 1) * TW],
                                              start=(c == 0), stop=(c == NCH - 1))
                            inc(mm, "PE1")
                pe_end = cnt["PE1"]
                d1 = cnt["DV"]
                with nc.Block() as blk:
                    @blk.vector
                    def _(v):
                        v.wait_ge(SEM["PE1"], pe_end)
                        for ti in range(4):
                            sl = slice(ti * TW, (ti + 1) * TW)
                            r_mean = C_strow[:, ti * 512:ti * 512 + 512]
                            r_var = C_strow[:, 2048 + ti * 512:2048 + ti * 512 + 512]
                            r_tmp = C[0:1, 8192:9216].bitcast(F32)
                            v.tensor_scalar(r_mean, sums[ti], 1.0 / D, None, op0=OP.mult)
                            v.tensor_scalar(r_var, sqs[ti], 1.0 / D, None, op0=OP.mult)
                            v.scalar_tensor_tensor(r_tmp, r_mean, -1.0, r_mean, op0=OP.mult, op1=OP.mult)
                            v.tensor_tensor(r_var, r_var, r_tmp, op=OP.add)
                            tc = v.tensor_scalar(C_meanrow[:, sl], r_mean, -1.0, None, op0=OP.mult)
                            inc(tc, "DV")
                    @blk.scalar
                    def _(a):
                        for ti in range(4):
                            sl = slice(ti * TW, (ti + 1) * TW)
                            a.wait_ge(SEM["DV"], d1 + ti + 1)
                            r_var = C_strow[:, 2048 + ti * 512:2048 + ti * 512 + 512]
                            a.activation(r_var, r_var, AF.Ln, bias=cb_eps[0:1, :], scale=1.0)
                            ac = a.activation(C_rstdrow[:, sl], r_var, AF.Exp, bias=0.0, scale=-0.5)
                            inc(ac, "AC")
                a1 = cnt["AC"]
                with nc.Block() as blk:
                    if not ffn:
                        pcol = None
                        p1 = cnt["PE1"]
                        pcol_ps = PS8[0][:, 0:16]
                        @blk.tensor
                        def _(t):
                            t.wait_ge(SEM["AC"], a1)
                            mm = None
                            for tau in range(16):
                                mm = t.matmul(pcol_ps[:, tau:tau + 1],
                                              C_rstdrow[:, tau * 128:(tau + 1) * 128],
                                              ones_row[0:1, 0:1], start=True, stop=True)
                            inc(mm, "PE1")
                        @blk.vector
                        def _(v):
                            v.wait_ge(SEM["PE1"], p1 + 1)
                            tc = v.tensor_copy(rstdcols[:], pcol_ps)
                            inc(tc, "DV")
                    else:
                        pass


        def phase_tables(ws_loads):
            with ExitStack() as ph:
                pb = [PS8[0], PS8[1]]
                s0 = cnt["SLD"]
                with nc.Block() as blk:
                    @blk.sync
                    def _(s):
                        dma(s, "SLD", C_cos, cosb_h[:])
                        dma(s, "SLD", C_sin, sinb_h[:])
                        for dst, srcsl in ws_loads:
                            dma(s, "SLD", dst, srcsl)
                sld_all = cnt["SLD"]
                p1, d0 = cnt["PE1"], cnt["DV"]
                with nc.Block() as blk:
                    @blk.tensor
                    def _(t):
                        for i in range(8):
                            ti = i % 4
                            if i >= 2:
                                t.wait_ge(SEM["DV"], d0 + i - 1)
                            mm = t.matmul(pb[i % 2][:], ones_row[:],
                                          C_rstdrow[:, ti * TW:(ti + 1) * TW],
                                          start=True, stop=True)
                            inc(mm, "PE1")
                    @blk.vector
                    def _(v):
                        v.wait_ge(SEM["SLD"], sld_all)
                        for i in range(8):
                            ti = i % 4
                            v.wait_ge(SEM["PE1"], p1 + i + 1)
                            if i < 4:
                                tgt = C_cos[:, ti * TW:(ti + 1) * TW]
                            else:
                                tgt = C_sin[:, ti * TW:(ti + 1) * TW]
                            tt = v.tensor_tensor(tgt, pb[i % 2][:], tgt, op=OP.mult)
                            inc(tt, "DV")

        def gemm_T(name, cache_view, n_list, band_bufs, epilogue, dv_per_n,
                   sst_hook=None, rank1=None, pre_dv_wait=None):
            NL = len(n_list)
            p0, d0 = cnt["PE1"], cnt["DV"]
            sst_pre = cnt["SST"]
            ev_bd = {}
            with nc.Block() as blk:
                @blk.scalar
                def _(a):
                    a.wait_ge(SEM["SST"], sst_pre)
                    a.wait_ge(SEM["WCC"], 32)
                    for i in range(NL):
                        if i >= 2:
                            a.wait_ge(SEM["PE1"], p0 + i - 1)
                        ev_bd[i] = dma_t(a, "BD%d" % (i % 2), band_bufs[i % 2][:], cache_view[n_list[i]])
                @blk.tensor
                def _(t):
                    for i, n in enumerate(n_list):
                        t.wait_ge(SEM["BD%d" % (i % 2)], ev_bd[i])
                        if i >= 2:
                            t.wait_ge(SEM["DV"], d0 + dv_per_n * (i - 1))
                        ps4 = PS8[(i % 2) * 4:(i % 2) * 4 + 4]
                        mm = None
                        for c in range(NCH):
                            for ti in range(4):
                                mm = t.matmul(ps4[ti][:], band_bufs[i % 2][:, c, :],
                                              AC_(c, ti * TW, TW),
                                              start=(c == 0), stop=(c == NCH - 1 and rank1 is None))
                        if rank1 is not None:
                            for ti in range(4):
                                mm = rank1(t, i, n, ps4[ti], ti)
                        inc(mm, "PE1")
                @blk.vector
                def _(v):
                    for i, n in enumerate(n_list):
                        v.wait_ge(SEM["PE1"], p0 + i + 1)
                        if pre_dv_wait is not None:
                            pre_dv_wait(v, i, n)
                        ps4 = PS8[(i % 2) * 4:(i % 2) * 4 + 4]
                        epilogue(v, i, n, ps4)
                if sst_hook is not None:
                    @blk.sync
                    def _(s):
                        for i, n in enumerate(n_list):
                            sst_hook(s, i, n, d0)

        def phase_s4_qk(layer, wsq_row, wsk_row):
            if layer == 0:
                qcv, kcv = qcache, kcache
                q_nl, k_nl = list(range(NH)), list(range(NKV))
            else:
                bv = bcache.rearrange("(n p) e -> n p e", p=128)
                qcv, kcv = bv, bv
                q_nl, k_nl = list(range(NH)), list(range(32, 32 + NKV))

            def rope_ep(dst_of, small):
                def ep(v, i, n, ps4):
                    for ti in range(4):
                        sl = slice(ti * TW, (ti + 1) * TW)
                        dst = dst_of(i, ti)
                        v.tensor_tensor(C_rt2[:], ps4[ti][:], C_cos[:, sl], op=OP.mult)
                        v.tensor_tensor(C_rt1[0:64, :], ps4[ti][64:128, :], C_sin[0:64, sl], op=OP.mult)
                        v.tensor_tensor(C_rt1[64:128, :], ps4[ti][0:64, :], C_sin[64:128, sl], op=OP.mult)
                        tt = v.tensor_tensor(dst, C_rt2[:], C_rt1[:], op=OP.add)
                        inc(tt, "DV")
                return ep

            qs0 = {0: cnt["QS0"], 1: cnt["QS1"]}

            def q_ep(v, i, n, ps4):
                for ti in range(4):
                    gidx = 4 * i + ti
                    sl = slice(ti * TW, (ti + 1) * TW)
                    if gidx >= 2:
                        v.wait_ge(SEM["QS%d" % (gidx % 2)],
                                  qs0[gidx % 2] + ((gidx - 2) // 2 + 1) * 16)
                    v.tensor_tensor(C_rt2[:], ps4[ti][:], C_cos[:, sl], op=OP.mult)
                    v.tensor_tensor(C_rt1[0:64, :], ps4[ti][64:128, :], C_sin[0:64, sl], op=OP.mult)
                    v.tensor_tensor(C_rt1[64:128, :], ps4[ti][0:64, :], C_sin[64:128, sl], op=OP.mult)
                    tt = v.tensor_tensor(C_qsm[gidx % 2], C_rt2[:], C_rt1[:], op=OP.add)
                    inc(tt, "DV")

            def q_sst(s, i, n, d0):
                for ti in range(4):
                    gidx = 4 * i + ti
                    s.wait_ge(SEM["DV"], d0 + gidx + 1)
                    dma(s, "QS%d" % (gidx % 2), qspill[n_q_list[i]][:, ti * TW:(ti + 1) * TW],
                        C_qsm[gidx % 2])

            n_q_list = q_nl

            def q_rank1(t, i, n, psu, ti):
                return t.matmul(psu[:], wsq_row[0:1, i * 128:(i + 1) * 128],
                                C_meanrow[:, ti * TW:(ti + 1) * TW], start=False, stop=True)

            gemm_T(f"s4q{layer}", qcv, q_nl, C_band, q_ep, dv_per_n=4, sst_hook=q_sst,
                   rank1=q_rank1)

            def k_ep(v, i, n, ps4):
                for ti in range(4):
                    sl = slice(ti * TW, (ti + 1) * TW)
                    v.tensor_tensor(C_rt2[:], ps4[ti][:], C_cos[:, sl], op=OP.mult)
                    v.tensor_tensor(C_rt1[0:64, :], ps4[ti][64:128, :], C_sin[0:64, sl], op=OP.mult)
                    v.tensor_tensor(C_rt1[64:128, :], ps4[ti][0:64, :], C_sin[64:128, sl], op=OP.mult)
                    tt = v.tensor_tensor(B[:, i * T + ti * TW:i * T + (ti + 1) * TW],
                                         C_rt2[:], C_rt1[:], op=OP.add)
                    inc(tt, "DV")
            def k_rank1(t, i, n, psu, ti):
                return t.matmul(psu[:], wsk_row[0:1, i * 128:(i + 1) * 128],
                                C_meanrow[:, ti * TW:(ti + 1) * TW], start=False, stop=True)

            gemm_T(f"s4k{layer}", kcv, k_nl, C_band, k_ep, dv_per_n=4, rank1=k_rank1)

        def phase_s4_v(layer, ws_off_v):
            vc = vcache if layer == 0 else vbcache
            with ExitStack() as ph:
                wsv_row = ph.enter_context(nc.sbuf_tensor(_un(f"ws_v{layer}"), [1, 1024], BF16))
                nmean = ph.enter_context(nc.sbuf_tensor(_un("s4v_nm"), [1, 2048], BF16))
                vband = ph.enter_context(nc.sbuf_tensor(_un("s4v_bd"), [128, 8, 512], BF16))
                s0, p0, d0 = cnt["SLD"], cnt["PE1"], cnt["DV"]
                qs_tot0, qs_tot1 = cnt["QS0"], cnt["QS1"]
                with nc.Block() as blk:
                    @blk.sync
                    def _(s):
                        dma(s, "SLD", wsv_row[:], wsums[0:1, ws_off_v:ws_off_v + 1024])
                        kq = 0
                        for dvh in range(2):
                            for tg in range(2):
                                grp = dvh * 2 + tg
                                for iq in range(4):
                                    if kq >= 1:
                                        pv = p0 + (kq // 4) * 5 + (kq % 4)
                                        s.wait_ge(SEM["PE1"], pv)
                                    dma(s, "SLD", vband[:], vc[:, dvh, iq * 8:(iq + 1) * 8, :])
                                    kq += 1
                    @blk.tensor
                    def _(t):
                        kq = 0
                        for dvh in range(2):
                            for tg in range(2):
                                grp = dvh * 2 + tg
                                for iq in range(4):
                                    t.wait_ge(SEM["SLD"], s0 + (kq + 2) * 16)
                                    if iq == 0 and grp >= 1:
                                        t.wait_ge(SEM["DV"], d0 + 1 + grp * 8)
                                    mm = None
                                    for c8 in range(8):
                                        c = iq * 8 + c8
                                        for tau in range(tg * 8, tg * 8 + 8):
                                            mm = t.matmul(PS8[tau % 8][:],
                                                          AC_(c, tau * 128, 128),
                                                          vband[:, c8, :],
                                                          start=(iq == 0 and c8 == 0),
                                                          stop=False)
                                    inc(mm, "PE1")
                                    kq += 1
                                if dvh == 0 and tg == 0:
                                    t.wait_ge(SEM["DV"], d0 + 1)
                                for tau in range(tg * 8, tg * 8 + 8):
                                    mm = t.matmul(PS8[tau % 8][:],
                                                  nmean[:, tau * 128:(tau + 1) * 128],
                                                  wsv_row[0:1, dvh * 512:(dvh + 1) * 512],
                                                  start=False, stop=True)
                                inc(mm, "PE1")
                    @blk.vector
                    def _(v):
                        v.wait_ge(SEM["QS0"], qs_tot0)
                        v.wait_ge(SEM["QS1"], qs_tot1)
                        nm = v.tensor_copy(nmean[:], C_meanrow[:])
                        inc(nm, "DV")
                        for dvh in range(2):
                            for tg in range(2):
                                grp = dvh * 2 + tg
                                v.wait_ge(SEM["PE1"], p0 + (grp + 1) * 5)
                                for tau in range(tg * 8, tg * 8 + 8):
                                    ts = v.tensor_scalar(
                                        C[:, tau * 1024 + dvh * 512:tau * 1024 + dvh * 512 + 512],
                                        PS8[tau % 8][:], rstdcols[:, tau:tau + 1], None, op0=OP.mult)
                                    inc(ts, "DV")

        def phase_s6(layer):
            with ExitStack() as ph:
                qh = [ph.enter_context(nc.sbuf_tensor(_un(f"s6_q{i}"), [128, T], BF16)) for i in range(2)]
                pt = [ph.enter_context(nc.sbuf_tensor(_un(f"s6_p{i}"), [128, 512], BF16)) for i in range(3)]
                rb_sb = ph.enter_context(nc.sbuf_tensor(_un("s6_rb"), [128, 512], BF16))
                rrow_bf = ph.enter_context(nc.sbuf_tensor(_un("s6_rrb"), [1, 512], BF16))
                maskt = ph.enter_context(nc.sbuf_tensor(_un("s6_mask"), [128, 896], BF16))
                sc_ps = [PS8[0][:], PS8[1][:], PS8[7][:]]
                cx_ps = [PS8[2][:], PS8[3][:]]
                dn_ps = [PS8[4][0:1, :], PS8[5][0:1, :]]
                rb_ps = PS8[6][:]

                ev_scores, ev_exp, ev_mask, ev_pt = {}, {}, {}, {}
                ev_grp, ev_recip, ev_rb, ev_rbe, ev_cxe, ev_qld = {}, {}, {}, {}, {}, {}
                pe1, pe2, dv, ac, sld = cnt["PE1"], cnt["PE2"], cnt["DV"], cnt["AC"], cnt["SLD"]
                for h in range(NH):
                    sld += 16
                    ev_qld[h] = sld
                    for tau in range(4):
                        nk = 4 * tau + 4
                        for kc in range(nk):
                            pe1 += 1; ev_scores[(h, tau, kc)] = pe1
                            ac += 1; ev_exp[(h, tau, kc)] = ac
                            if kc >= 4 * tau:
                                dv += 1; ev_mask[(h, tau, kc)] = dv
                            pe2 += 1; ev_pt[(h, tau, kc)] = pe2
                        ev_grp[(h, tau)] = pe2
                        dv += 1; ev_recip[(h, tau)] = dv
                        pe2 += 1; ev_rb[(h, tau)] = pe2
                        ac += 1; ev_rbe[(h, tau)] = ac
                        dv += 1; ev_cxe[(h, tau)] = dv
                GV = {h * 4 + tau: (h, tau) for h in range(NH) for tau in range(4)}
                qs_tot0, qs_tot1 = cnt["QS0"], cnt["QS1"]
                ev_qh = {}
                ev_mask_ld = [0]
                with nc.Block() as blk:
                    @blk.sync
                    def _(s):
                        s.wait_ge(SEM["QS0"], qs_tot0)
                        s.wait_ge(SEM["QS1"], qs_tot1)
                        ev_mask_ld[0] = dma(s, "SLD", maskt[:], maskm[:])
                        for h in range(NH):
                            if h >= 2:
                                s.wait_ge(SEM["PE1"], ev_scores[(h - 2, 3, 15)])
                            ev_qh[h] = dma(s, "QH%d" % (h % 2), qh[h % 2][:], qspill[h])
                    @blk.tensor
                    def _(t):
                        for h in range(NH):
                            j = h // 4
                            t.wait_ge(SEM["QH%d" % (h % 2)], ev_qh[h])
                            for tau in range(4):
                                g = h * 4 + tau
                                nk = 4 * tau + 4
                                if g >= 2:
                                    t.wait_ge(SEM["DV"], ev_recip[GV[g - 2]])
                                    t.wait_ge(SEM["DV"], ev_cxe[GV[g - 2]])
                                if g >= 1:
                                    t.wait_ge(SEM["AC"], ev_rbe[GV[g - 1]])

                                def mm_scores(kc):
                                    if (h, tau, kc - 3) in ev_exp:
                                        t.wait_ge(SEM["AC"], ev_exp[(h, tau, kc - 3)])
                                    mm = t.matmul(sc_ps[kc % 3],
                                                  B[:, j * T + kc * 128:j * T + kc * 128 + 128],
                                                  qh[h % 2][:, tau * TW:(tau + 1) * TW],
                                                  start=True, stop=True)
                                    inc(mm, "PE1")

                                def mm_ctxden(kc):
                                    t.wait_ge(SEM["AC"], ev_exp[(h, tau, kc)])
                                    if (h, tau, kc) in ev_mask:
                                        t.wait_ge(SEM["DV"], ev_mask[(h, tau, kc)])
                                    t.matmul(cx_ps[g % 2],
                                             C[:, kc * 1024 + j * 128:kc * 1024 + (j + 1) * 128],
                                             pt[kc % 3][:],
                                             start=(kc == 0), stop=(kc == nk - 1))
                                    mm = t.matmul(dn_ps[g % 2], ones_col[:], pt[kc % 3][:],
                                                  start=(kc == 0), stop=(kc == nk - 1))
                                    inc(mm, "PE2")

                                mm_scores(0)
                                if nk > 1:
                                    mm_scores(1)
                                for kc in range(nk):
                                    if kc + 2 < nk:
                                        mm_scores(kc + 2)
                                    mm_ctxden(kc)
                                t.wait_ge(SEM["DV"], ev_recip[(h, tau)])
                                mm = t.matmul(rb_ps, ones_row[:], rrow_bf[:], start=True, stop=True)
                                inc(mm, "PE2")
                    @blk.scalar
                    def _(a):
                        for h in range(NH):
                            for tau in range(4):
                                g = h * 4 + tau
                                nk = 4 * tau + 4
                                for kc in range(nk):
                                    a.wait_ge(SEM["PE1"], ev_scores[(h, tau, kc)])
                                    if (h, tau, kc - 3) in ev_pt:
                                        a.wait_ge(SEM["PE2"], ev_pt[(h, tau, kc - 3)])
                                    ex = a.activation(pt[kc % 3][:], sc_ps[kc % 3], AF.Exp,
                                                      bias=cb_negC[:], scale=SM_SCALE)
                                    inc(ex, "AC")
                                a.wait_ge(SEM["PE2"], ev_rb[(h, tau)])
                                if g >= 1:
                                    a.wait_ge(SEM["DV"], ev_cxe[GV[g - 1]])
                                cp = a.copy(rb_sb[:], rb_ps)
                                inc(cp, "AC")
                    @blk.vector
                    def _(v):
                        v.wait_ge(SEM["SLD"], ev_mask_ld[0])
                        for h in range(NH):
                            for tau in range(4):
                                g = h * 4 + tau
                                nk = 4 * tau + 4
                                for kc in range(4 * tau, nk):
                                    off = kc * 128 - tau * 512
                                    v.wait_ge(SEM["AC"], ev_exp[(h, tau, kc)])
                                    mk = v.tensor_tensor(pt[kc % 3][:], pt[kc % 3][:],
                                                         maskt[:, 384 - off:384 - off + 512],
                                                         op=OP.mult)
                                    inc(mk, "DV")
                                v.wait_ge(SEM["PE2"], ev_grp[(h, tau)])
                                with nc.allow_low_precision(reason="recip feeds bf16 broadcast anyway"):
                                    rc = v.reciprocal(rrow_bf[:], dn_ps[g % 2])
                                inc(rc, "DV")
                                v.wait_ge(SEM["AC"], ev_rbe[(h, tau)])
                                ce = v.tensor_tensor(
                                    A[:, h * T + tau * TW:h * T + (tau + 1) * TW],
                                    cx_ps[g % 2], rb_sb[:], op=OP.mult)
                                inc(ce, "DV")
                assert cnt["PE1"] == ev_scores[(NH - 1, 3, 15)]
                assert cnt["PE2"] == ev_rb[(NH - 1, 3)]
                assert cnt["DV"] == ev_cxe[(NH - 1, 3)]
                assert cnt["AC"] == ev_rbe[(NH - 1, 3)]

        def phase_xhome_init():
            with nc.Block() as blk:
                @blk.sync
                def _(s):
                    for i in range(NCH):
                        dma(s, "SST", x_home[i], AC_(i))

        def phase_s8x(oc):
            d0, s0 = cnt["DV"], cnt["SLD"]

            def ep(v, i, n, ps4):
                v.wait_ge(SEM["SLD"], s0 + (i + 1) * 16)
                last = None
                for ti in range(4):
                    sl = slice(ti * TW, (ti + 1) * TW)
                    last = v.tensor_tensor(B_xold[i % 3][:, sl], ps4[ti][:],
                                           B_xold[i % 3][:, sl], op=OP.add)
                inc(last, "DV")

            def sst(s, i, n, d0_):
                dma(s, "SLD", B_xold[i % 3][:], x_home[i])
                s.wait_ge(SEM["DV"], d0 + i + 1)
                dma(s, "SST", x_home[i], B_xold[i % 3][:])

            gemm_T(_un("s8x"), oc, list(range(NCH)), B_band, ep, dv_per_n=1, sst_hook=sst)

        def phase_ffn(final):
            with ExitStack() as ph:
                wsf = ph.enter_context(nc.sbuf_tensor(_un("ff_wsf"), [1, 4096], BF16))
                if final:
                    stgf = [ph.enter_context(nc.sbuf_tensor(_un(f"f3_sg{i}"), [128, 512], F32))
                            for i in range(2)]
                    stgo = B_stgo3
                else:
                    stgf = B_stgf2
                    stgo = B_stgo2
                pb = [PS8[0], PS8[1]]
                g0 = cnt["GLD"]
                with nc.Block() as blk:
                    @blk.gpsimd
                    def _(g):
                        for n in range(NCH):
                            dma(g, "GLD", biascols[:, n:n + 1],
                                fbias[0:1, n * 128:(n + 1) * 128]
                                .rearrange("o (p f) -> (o p) f", f=1))
                        dma(g, "GLD", wsf[:], wsums[0:1, 6144:10240])
                p0, d0 = cnt["PE1"], cnt["DV"]
                with nc.Block() as blk:
                    @blk.tensor
                    def _(t):
                        for ti in range(4):
                            if ti >= 2:
                                t.wait_ge(SEM["DV"], d0 + ti - 1)
                            mm = t.matmul(pb[ti % 2][:], ones_row[:],
                                          C_rstdrow[:, ti * TW:(ti + 1) * TW], start=True, stop=True)
                            inc(mm, "PE1")
                    @blk.vector
                    def _(v):
                        for ti in range(4):
                            v.wait_ge(SEM["PE1"], p0 + ti + 1)
                            tc = v.tensor_copy(C_rstdb[:, ti * TW:(ti + 1) * TW], pb[ti % 2][:])
                            inc(tc, "DV")
                st0 = cnt["SST"]

                def rank1(t, i, n, psu, ti):
                    return t.matmul(psu[:], wsf[0:1, n * 128:(n + 1) * 128],
                                    C_meanrow[:, ti * TW:(ti + 1) * TW], start=False, stop=True)

                def ep(v, i, n, ps4):
                    v.wait_ge(SEM["GLD"], g0 + 33 * 16)
                    if i >= 2:
                        v.wait_ge(SEM["SST"], st0 + (i - 1) * 16)
                    last = None
                    for ti in range(4):
                        sl = slice(ti * TW, (ti + 1) * TW)
                        v.tensor_tensor(stgf[ti % 2][:], ps4[ti][:], C_rstdb[:, sl], op=OP.mult)
                        v.tensor_scalar(stgf[ti % 2][:], stgf[ti % 2][:],
                                        biascols[:, n:n + 1], None, op0=OP.add)
                        last = v.tensor_tensor(stgo[i % 2][:, sl], stgf[ti % 2][:],
                                               AC_(n, ti * TW, TW), op=OP.add)
                    inc(last, "DV")

                d2 = cnt["DV"]

                def sst2(s, i, n, d0_):
                    s.wait_ge(SEM["DV"], d2 + i + 1)
                    if final:
                        dma(s, "SST", y_out[n * 128:(n + 1) * 128, :], stgo[i % 2][:])
                    else:
                        dma(s, "SST", x_home[n], stgo[i % 2][:])

                gemm_T("ff" + ("3" if final else "2"), fcache, list(range(NCH)),
                       B_band, ep, dv_per_n=1, sst_hook=sst2, rank1=rank1)

        def dump_fm(name):
            d = mkdbg(name, [D, T])
            with nc.Block() as blk:
                @blk.gpsimd
                def _(g):
                    for qd in range(4):
                        dma(g, "GLD",
                            d[qd * 1024:(qd + 1) * 1024, :].rearrange("(c p) t -> p c t", p=128),
                            A[:, qd * 8 * T:(qd + 1) * 8 * T].rearrange("p (c t) -> p c t", t=T))

        def dump_sbuf(name, ap, shape):
            d = mkdbg(name, list(shape))
            with nc.Block() as blk:
                @blk.gpsimd
                def _(g):
                    dma(g, "GLD", d[:], ap)

        def dump_dram(name, src, shape):
            d = mkdbg(name, list(shape), BF16)
            with nc.Block() as blk:
                @blk.sync
                def _(s):
                    s.wait_ge(SEM["SST"], cnt["SST"])
                    dma(s, "SST", d[:], src[:])

        def finish():
            with nc.Block() as blk:
                @blk.sync
                def _(s):
                    s.wait_ge(SEM["SST"], cnt["SST"])
                    s.wait_ge(SEM["GLD"], cnt["GLD"])
                    s.wait_ge(SEM["SLD"], cnt["SLD"])
                    s.wait_ge(SEM["ALD"], cnt["ALD"])

        p0_attn_a()
        p0_ffn()
        p0_vb()
        p0_colsum(lambda nb: bcache[nb * 128:(nb + 1) * 128, :], 48, 10240)
        p0_colsum(lambda nb: qcache[nb], 32, 0)
        p0_colsum(lambda nb: kcache[nb], 8, 4096)
        if stop_after == "p0":
            if "caches" in taps:
                dump_dram("ws", wsums, [1, 16384])
                dump_dram("oc", ocacheT, [NCH, 128, D])
                dump_dram("qc", qcache, [NH, 128, D])
                dump_dram("kc", kcache, [NKV, 128, D])
                dump_dram("vc", vcache, [128, 2, NCH, 512])
                dump_dram("fc", fcache, [NCH, 128, D])
                dump_dram("vbc", vbcache, [128, 2, NCH, 512])
            finish()
            return nc, dbg

        WSOFF = {0: (0, 4096, 5120), 1: (10240, 14336, 15360)}

        def attn_layer(layer):
            with ExitStack() as lh:
                oq, ok, ov = WSOFF[layer]
                phase_stats(first=(layer == 0), ffn=False)
                if layer == 0:
                    phase_xhome_init()
                with ExitStack() as lh2:
                    wsq = lh2.enter_context(nc.sbuf_tensor(_un(f"ws_q{layer}"), [1, 4096], BF16))
                    wsk = lh2.enter_context(nc.sbuf_tensor(_un(f"ws_k{layer}"), [1, 1024], BF16))
                    phase_tables([(wsq[:], wsums[0:1, oq:oq + 4096]),
                                  (wsk[:], wsums[0:1, ok:ok + 1024])])
                    phase_s4_qk(layer, wsq, wsk)
                phase_s4_v(layer, ov)
                if layer == 0 and stop_after == "qkv":
                    return True
                phase_s6(layer)
                if layer == 0 and stop_after == "ctx":
                    return True
                phase_s8x(ocacheT if layer == 0 else
                          ocache.rearrange("(n p) e -> n p e", p=128))
            return False

        stop = attn_layer(0)
        if stop_after == "qkv":
            if "qkv" in taps:
                dump_dram("q", qspill, [NH, 128, T])
                dump_sbuf("k", B[:], [128, 8 * T])
                dump_sbuf("v", C[:], [128, 16 * 1024])
            finish()
            return nc, dbg
        if stop_after == "ctx":
            if "ctx" in taps:
                dump_fm("ctx")
            finish()
            return nc, dbg
        if stop_after == "sub1":
            if "x1" in taps:
                dump_dram("x1", x_home, [NCH, 128, T])
            finish()
            return nc, dbg

        phase_stats(first=False, ffn=True)
        phase_ffn(final=False)
        if stop_after == "sub2":
            if "x2" in taps:
                dump_dram("x2", x_home, [NCH, 128, T])
            finish()
            return nc, dbg

        attn_layer(1)
        if stop_after == "sub3":
            if "x3" in taps:
                dump_dram("x3", x_home, [NCH, 128, T])
            finish()
            return nc, dbg

        phase_stats(first=False, ffn=True)
        phase_ffn(final=True)
        finish()
        return nc, dbg


from concourse.bass_utils import run_bass_kernel_spmd


def kernel(**inputs):
    x = np.asarray(inputs["L0_attn_a_hidden_states"])
    consts = host_constants()
    shared = {
        "qw_q": np.ascontiguousarray(np.asarray(inputs["L0_attn_a_qw_q"], dtype=np.int32)),
        "qw_k": np.ascontiguousarray(np.asarray(inputs["L0_attn_a_qw_k"], dtype=np.int32)),
        "qw_v": np.ascontiguousarray(np.asarray(inputs["L0_attn_a_qw_v"], dtype=np.int32)),
        "qw_o": np.ascontiguousarray(np.asarray(inputs["L0_attn_a_qw_o"], dtype=np.int32)),
        "sc_q": np.ascontiguousarray(np.asarray(inputs["L0_attn_a_scales_q"], dtype=np.float32)),
        "sc_k": np.ascontiguousarray(np.asarray(inputs["L0_attn_a_scales_k"], dtype=np.float32)),
        "sc_v": np.ascontiguousarray(np.asarray(inputs["L0_attn_a_scales_v"], dtype=np.float32)),
        "sc_o": np.ascontiguousarray(np.asarray(inputs["L0_attn_a_scales_o"], dtype=np.float32)),
        "w_in": np.ascontiguousarray(np.asarray(inputs["L0_attn_b_w_in"], dtype=np.float32)),
        "w_out": np.ascontiguousarray(np.asarray(inputs["L0_attn_b_w_out"], dtype=np.float32)),
        "fq": np.ascontiguousarray(np.asarray(inputs["L0_ffn_qweight"], dtype=np.int32).reshape(D, D)),
        "fsc": np.ascontiguousarray(np.asarray(inputs["L0_ffn_scales"], dtype=np.float32).reshape(D, 32)),
        "fzp": np.ascontiguousarray(np.asarray(inputs["L0_ffn_scaled_zeros"], dtype=np.float32).reshape(D, 32)),
        "fbias": np.ascontiguousarray(np.asarray(inputs["L0_ffn_bias"], dtype=np.float32).reshape(1, D)),
        "cosbase": consts["cosbase"],
        "sinbase": consts["sinbase"],
        "maskm": consts["maskm"],
        "ident": consts["ident"],
    }
    nc, _dbg = build_program()
    in_maps = []
    for b in range(8):
        m = dict(shared)
        m["x_fm"] = np.ascontiguousarray(np.asarray(x[b], dtype=np.float32).T)
        in_maps.append(m)
    res = run_bass_kernel_spmd(nc, in_maps, core_ids=list(range(8)))
    out = np.stack([np.ascontiguousarray(np.asarray(r["y"]).T) for r in res.results], axis=0)
    return out


# version 04 (final)
# speedup vs baseline: 1.2068x; 1.0011x over previous
import math
import numpy as np
import concourse.bass as bass
import concourse.mybir as mybir
from concourse.alu_op_type import AluOpType as OP
from contextlib import ExitStack

F32, BF16, I32 = mybir.dt.float32, mybir.dt.bfloat16, mybir.dt.int32
AF = mybir.ActivationFunctionType

D, T, NCH, TW = 4096, 2048, 32, 512
NH, NKV, HD = 32, 8, 128
EXPC = 10.0
SM_SCALE = 1.0 / math.sqrt(HD)
EPS = 1e-5


def host_constants():
    import ml_dtypes
    inv_freq = 1.0 / (10000.0 ** (np.arange(0, HD, 2, dtype=np.float64) / HD))
    ang = np.arange(T, dtype=np.float64)[:, None] * inv_freq[None, :]
    cos, sin = np.cos(ang).T, np.sin(ang).T
    kk = np.arange(128)[:, None]
    jj = np.arange(896)[None, :]
    return {
        "cosbase": np.concatenate([cos, cos], 0).astype(ml_dtypes.bfloat16),
        "sinbase": np.concatenate([-sin, sin], 0).astype(ml_dtypes.bfloat16),
        "maskm": (kk + 384 <= jj).astype(np.float32).astype(ml_dtypes.bfloat16),
        "ident": np.eye(128, dtype=np.float32).astype(ml_dtypes.bfloat16),
    }


_unctr = {}
def _un(nm):
    _unctr[nm] = _unctr.get(nm, 0) + 1
    return f"{nm}_{_unctr[nm]}"


def build_program(stop_after="all", debug_taps=()):
    _unctr.clear()
    nc = bass.Bass("TRN2", target_bir_lowering=False)
    taps = set(debug_taps)

    x_in = nc.dram_tensor("x_fm", [D, T], F32, kind="ExternalInput").ap()
    qw = {
        "q": nc.dram_tensor("qw_q", [D, 4096], I32, kind="ExternalInput").ap(),
        "k": nc.dram_tensor("qw_k", [D, 1024], I32, kind="ExternalInput").ap(),
        "v": nc.dram_tensor("qw_v", [D, 1024], I32, kind="ExternalInput").ap(),
        "o": nc.dram_tensor("qw_o", [4096, D], I32, kind="ExternalInput").ap(),
    }
    sc = {
        "q": nc.dram_tensor("sc_q", [32, 4096], F32, kind="ExternalInput").ap(),
        "k": nc.dram_tensor("sc_k", [32, 1024], F32, kind="ExternalInput").ap(),
        "v": nc.dram_tensor("sc_v", [32, 1024], F32, kind="ExternalInput").ap(),
        "o": nc.dram_tensor("sc_o", [32, 4096], F32, kind="ExternalInput").ap(),
    }
    w_in = nc.dram_tensor("w_in", [6144, D], F32, kind="ExternalInput").ap()
    w_out = nc.dram_tensor("w_out", [D, D], F32, kind="ExternalInput").ap()
    fq = nc.dram_tensor("fq", [D, D], I32, kind="ExternalInput").ap()
    fsc = nc.dram_tensor("fsc", [D, 32], F32, kind="ExternalInput").ap()
    fzp = nc.dram_tensor("fzp", [D, 32], F32, kind="ExternalInput").ap()
    fbias = nc.dram_tensor("fbias", [1, D], F32, kind="ExternalInput").ap()
    cosb_h = nc.dram_tensor("cosbase", [128, T], BF16, kind="ExternalInput").ap()
    sinb_h = nc.dram_tensor("sinbase", [128, T], BF16, kind="ExternalInput").ap()
    maskm = nc.dram_tensor("maskm", [128, 896], BF16, kind="ExternalInput").ap()
    ident_h = nc.dram_tensor("ident", [128, 128], BF16, kind="ExternalInput").ap()
    y_out = nc.dram_tensor("y", [D, T], F32, kind="ExternalOutput").ap()

    x_home = nc.dram_tensor("x_home", [NCH, 128, T], BF16, kind="Internal").ap()
    qspill = nc.dram_tensor("qspill", [NH, 128, T], BF16, kind="Internal").ap()
    qcache = nc.dram_tensor("qcache", [NH, 128, D], BF16, kind="Internal").ap()
    kcache = nc.dram_tensor("kcache", [NKV, 128, D], BF16, kind="Internal").ap()
    vcache = nc.dram_tensor("vcache", [128, 2, NCH, 512], BF16, kind="Internal").ap()
    fcache = nc.dram_tensor("fcache", [NCH, 128, D], BF16, kind="Internal").ap()
    bcache = nc.dram_tensor("bcache", [6144, D], BF16, kind="Internal").ap()
    ocache = nc.dram_tensor("ocache", [D, D], BF16, kind="Internal").ap()
    vbcache = nc.dram_tensor("vbcache", [128, 2, NCH, 512], BF16, kind="Internal").ap()
    wsums = nc.dram_tensor("wsums", [1, 16384], BF16, kind="Internal").ap()
    ocacheT = nc.dram_tensor("ocacheT", [NCH, 128, D], BF16, kind="Internal").ap()

    dbg = {}

    def mkdbg(name, shape, dtype=F32):
        if name not in dbg:
            dbg[name] = nc.dram_tensor("dbg_" + name, shape, dtype, kind="ExternalOutput").ap()
        return dbg[name]

    es = ExitStack()
    with es:
        sbt = lambda name, shape, dtype: es.enter_context(nc.sbuf_tensor(name, shape, dtype))
        A = sbt("A", [128, NCH * T], BF16)
        B = sbt("B", [128, 16384], BF16)
        C = sbt("C", [128, 16384], BF16)
        ones_row = sbt("ones_row", [1, 128], BF16)
        ones_col = sbt("ones_col", [128, 1], BF16)
        cb_negC = sbt("cb_negC", [128, 1], F32)
        cb_eps = sbt("cb_eps", [128, 1], F32)
        ident = sbt("ident_s", [128, 128], BF16)
        ident32 = sbt("ident32", [32, 32], F32)
        rstdcols = sbt("rstdcols", [128, 16], F32)
        biascols = sbt("biascols", [128, 32], F32)

        PS8 = [es.enter_context(nc.psum_tensor(f"gps{i}", [128, 512], F32)) for i in range(8)]
        PSB = [p[:].bitcast(BF16) for p in PS8]
        SEM = {}
        cnt = {}
        for k in ["GLD", "SLD", "ALD", "SST", "PE1", "PE2", "DV", "AC",
                  "XLD", "SC0", "SC1", "BD0", "BD1", "QH0", "QH1", "QS0", "QS1",
                  "STG", "FF0", "FF1", "WCC"]:
            SEM[k] = es.enter_context(nc.semaphore(k))
            cnt[k] = 0

        def dma(engs, key, out, in_):
            engs.dma_start(out, in_).then_inc(SEM[key], 16)
            cnt[key] += 16
            return cnt[key]

        def dma_t(engs, key, out, in_):
            engs.dma_start_transpose(out, in_).then_inc(SEM[key], 16)
            cnt[key] += 16
            return cnt[key]

        def inc(instr, key, n=1):
            instr.then_inc(SEM[key], n)
            cnt[key] += n
            return cnt[key]

        def AC_(c, t0=0, ww=T):
            return A[:, c * T + t0:c * T + t0 + ww]

        C_cos = C[:, 0:2048]
        C_sin = C[:, 2048:4096]
        C_qsm = [C[:, 4096:4608], C[:, 4608:5120]]
        C_rt1 = C[:, 5120:5632]
        C_rt2 = C[:, 5632:6144]
        C_band = [C[:, 6144 + i * 4096:6144 + (i + 1) * 4096]
                  .rearrange("p (c f) -> p c f", f=128) for i in range(2)]
        C_strow = C[0:1, 0:8192].bitcast(F32)
        C_sq = C[:, 8192:10240]
        C_rstdrow = C[0:1, 10240:12288]
        C_meanrow = C[0:1, 14336:16384]
        C_deq = C.rearrange("p (c f) -> p c f", f=512)
        C_rstdb = C[:, 2048:4096]
        B_stage = B.rearrange("p (c f) -> p c f", f=512)
        B_stage4 = B.rearrange("p (j f) -> p j f", f=D)
        B_band = [B[:, i * 4096:(i + 1) * 4096].rearrange("p (c f) -> p c f", f=128)
                  for i in range(2)]
        B_xold = [B[:, 8192 + i * 2048:8192 + (i + 1) * 2048] for i in range(3)]
        B_stgo2 = [B[:, 8192 + i * 2048:8192 + (i + 1) * 2048] for i in range(2)]
        B_stgf2 = [B[:, 12288 + i * 1024:12288 + (i + 1) * 1024].bitcast(F32) for i in range(2)]
        B_stgo3 = [B[:, 8192 + i * 4096:8192 + (i + 1) * 4096].bitcast(F32) for i in range(2)]

        with nc.Block() as blk:
            @blk.vector
            def _(v):
                v.memset(ones_row[:], 1.0)
                v.memset(ones_col[:], 1.0)
                v.memset(cb_negC[:], -EXPC)
                v.memset(cb_eps[:], EPS)
                v.wait_ge(SEM["SLD"], 16)
                v.tensor_copy(ident32[:], ident[0:32, 0:32])
            @blk.sync
            def _(s):
                dma(s, "SLD", ident[:], ident_h[:])
            @blk.gpsimd
            def _(g):
                for qd in range(4):
                    dma(g, "XLD",
                        A[:, qd * 8 * T:(qd + 1) * 8 * T].rearrange("p (c t) -> p c t", t=T),
                        x_in[qd * 1024:(qd + 1) * 1024, :].rearrange("(c p) t -> p c t", p=128))
                dma(g, "WCC", bcache[:], w_in[:])
                dma(g, "WCC", ocache[:], w_out[:])
        GLD_X = 4 * 16
        GLD_PRE = cnt["GLD"]
        SLD_CONST = cnt["SLD"]

        def p0_attn_a():
            with ExitStack() as ph:
                gath = ph.enter_context(nc.sbuf_tensor("p0_gath", [128, D], BF16))
                scratch = ph.enter_context(nc.sbuf_tensor("p0_scr", [128, 1024], F32))
                scl = scratch[0:32, 0:512]
                scbc = [scratch[:, 0:512], scratch[:, 512:1024]]
                scT = [ph.enter_context(nc.sbuf_tensor(f"p0_scT{i}", [128, 32], F32)) for i in range(4)]
                n8T = [ph.enter_context(nc.sbuf_tensor(f"p0_n8T{i}", [128, 32], F32)) for i in range(4)]
                wsrow = ph.enter_context(nc.sbuf_tensor("p0_wsr", [1, 512], BF16))
                pst = [PSB[0][:, 0:128], PSB[1][:, 0:128]]
                psS = PS8[2][:, 0:32]
                ws_ps = PS8[3][0:1, :]
                bands = [("q", b, "t") for b in range(8)] + \
                        [("k", b, "t") for b in range(2)] + \
                        [("o", b, "t") for b in range(8)] + \
                        [("v", b, "v") for b in range(2)]
                ws_off = {"q": 0, "k": 4096, "v": 5120}
                dstmap = {"q": qcache, "k": kcache, "o": ocacheT}
                tband = [x for x in bands if x[2] == "t"]
                ev_stage = {}
                ev_scl = {}
                with nc.Block() as blk:
                    @blk.gpsimd
                    def _(g):
                        wname, b, _k = tband[0]
                        for cq in range(4):
                            ev_stage[0] = dma(g, "GLD", B_stage[:, cq * 8:(cq + 1) * 8, :],
                                              qw[wname][cq * 1024:(cq + 1) * 1024,
                                                        b * 512:b * 512 + 512]
                                              .rearrange("(c p) f -> p c f", p=128))
                    @blk.sync
                    def _(s):
                        s.wait_ge(SEM["SLD"], 16)
                        wname, b, _k = tband[0]
                        ev_scl[0] = dma(s, "SLD", scl, sc[wname][:, b * 512:b * 512 + 512])
                for bi, (wname, b, kind) in enumerate(tband):
                    col = b * 512
                    d0, p0, a0, s0 = cnt["DV"], cnt["PE1"], cnt["AC"], cnt["SST"]
                    with nc.Block() as blk:
                        @blk.gpsimd
                        def _(g):
                            if bi + 1 < len(tband):
                                wn2, b2, _k2 = tband[bi + 1]
                                g.wait_ge(SEM["PE1"], p0 + 4 + 128)
                                for cq in range(4):
                                    ev_stage[bi + 1] = dma(
                                        g, "GLD", B_stage[:, cq * 8:(cq + 1) * 8, :],
                                        qw[wn2][cq * 1024:(cq + 1) * 1024,
                                                b2 * 512:b2 * 512 + 512]
                                        .rearrange("(c p) f -> p c f", p=128))
                        @blk.sync
                        def _(s):
                            if bi + 1 < len(tband):
                                wn2, b2, _k2 = tband[bi + 1]
                                s.wait_ge(SEM["PE1"], p0 + 4)
                                ev_scl[bi + 1] = dma(s, "SLD", scl,
                                                     sc[wn2][:, b2 * 512:b2 * 512 + 512])
                            for nl in range(4):
                                s.wait_ge(SEM["AC"], a0 + (nl + 1) * 32)
                                dma(s, "SST", dstmap[wname][4 * b + nl], gath[:])
                        @blk.tensor
                        def _(t):
                            t.wait_ge(SEM["SLD"], ev_scl[bi])
                            t.wait_ge(SEM["GLD"], ev_stage[bi])
                            for nl in range(4):
                                if nl >= 1:
                                    t.wait_ge(SEM["DV"], d0 + 2 * nl)
                                tp = t.transpose(psS, scl[:, nl * 128:(nl + 1) * 128],
                                                 ident32[:])
                                inc(tp, "PE1")
                            for nl in range(4):
                                for c in range(NCH):
                                    kk = nl * 32 + c
                                    if kk >= 2:
                                        t.wait_ge(SEM["AC"], a0 + kk - 1)
                                    tp = t.transpose(pst[c % 2],
                                                     B_stage[:, c, nl * 128:(nl + 1) * 128],
                                                     ident[:])
                                    inc(tp, "PE1")
                        @blk.vector
                        def _(v):
                            for nl in range(4):
                                v.wait_ge(SEM["PE1"], p0 + nl + 1)
                                tc = v.tensor_copy(scT[nl][:], psS)
                                inc(tc, "DV")
                                ts = v.tensor_scalar(n8T[nl][:], psS, -8.0, None,
                                                     op0=OP.mult)
                                inc(ts, "DV")
                        @blk.scalar
                        def _(a):
                            for nl in range(4):
                                a.wait_ge(SEM["DV"], d0 + 2 * nl + 2)
                                a.wait_ge(SEM["SST"], s0 + nl * 16)
                                for c in range(NCH):
                                    a.wait_ge(SEM["PE1"], p0 + 4 + nl * 32 + c + 1)
                                    cp = a.activation(gath[:, c * 128:(c + 1) * 128],
                                                      pst[c % 2], AF.Identity,
                                                      bias=n8T[nl][:, c:c + 1],
                                                      scale=scT[nl][:, c:c + 1])
                                    inc(cp, "AC")
                for bi, (wname, b, kind) in enumerate([x for x in bands if x[2] == "v"]):
                    col = b * 512
                    if True:
                        with nc.Block() as blk:
                            @blk.gpsimd
                            def _(g):
                                g.wait_ge(SEM["PE1"], cnt["PE1"])
                                g.wait_ge(SEM["DV"], cnt["DV"])
                                for cq in range(4):
                                    dma(g, "GLD", B_stage[:, cq * 8:(cq + 1) * 8, :],
                                        qw[wname][cq * 1024:(cq + 1) * 1024, col:col + 512]
                                        .rearrange("(c p) f -> p c f", p=128))
                            @blk.sync
                            def _(s):
                                s.wait_ge(SEM["GLD"], cnt["GLD"])
                                s.wait_ge(SEM["SST"], cnt["SST"])
                        d0, p0, a0, s0 = cnt["DV"], cnt["PE1"], cnt["AC"], cnt["SST"]
                        ev_sc = {}
                        with nc.Block() as blk:
                            @blk.sync
                            def _(s):
                                for c in range(NCH):
                                    if c >= 2:
                                        s.wait_ge(SEM["DV"], d0 + c - 1)
                                    ev_sc[c] = dma(s, "SC%d" % (c % 2), scbc[c % 2][:],
                                                   sc[wname][c:c + 1, col:col + 512]
                                                   .partition_broadcast(128).squeeze(1))
                                for c in range(NCH):
                                    s.wait_ge(SEM["DV"], d0 + c + 1)
                                    dma(s, "SST", vcache[:, b, c, :], C_deq[:, c, :])
                            @blk.vector
                            def _(v):
                                for c in range(NCH):
                                    v.wait_ge(SEM["SC%d" % (c % 2)], ev_sc[c])
                                    st = v.scalar_tensor_tensor(C_deq[:, c, :], B_stage[:, c, :],
                                                                -8.0, scbc[c % 2][:],
                                                                op0=OP.add, op1=OP.mult)
                                    inc(st, "DV")
                        d1, p1 = cnt["DV"], cnt["PE1"]
                        with nc.Block() as blk:
                            @blk.tensor
                            def _(t):
                                mm = None
                                for c in range(NCH):
                                    mm = t.matmul(ws_ps, ones_col[:], C_deq[:, c, :],
                                                  start=(c == 0), stop=(c == NCH - 1))
                                inc(mm, "PE1")
                            @blk.vector
                            def _(v):
                                v.wait_ge(SEM["PE1"], p1 + 1)
                                ws = v.tensor_copy(wsrow[:], ws_ps)
                                inc(ws, "DV")
                            @blk.sync
                            def _(s):
                                s.wait_ge(SEM["DV"], cnt["DV"])
                                dma(s, "SST", wsums[0:1, 5120 + b * 512:5120 + (b + 1) * 512],
                                    wsrow[:])

        def p0_ffn():
            with ExitStack() as ph:
                fscb = [ph.enter_context(nc.sbuf_tensor(f"p0_fs{i}", [128, 32], F32)) for i in range(2)]
                fzpb = [ph.enter_context(nc.sbuf_tensor(f"p0_fz{i}", [128, 32], F32)) for i in range(2)]
                wcol32 = ph.enter_context(nc.sbuf_tensor("p0_wc32", [128, 1], F32))
                wcol = ph.enter_context(nc.sbuf_tensor("p0_wc", [128, 2], BF16))
                wrow = ph.enter_context(nc.sbuf_tensor("p0_wr", [1, 128], BF16))
                ws_ps = PS8[2][0:2, 0:128]
                Bs = [B[:, 0:4096], B[:, 4096:8192]]
                Cd = [C[:, 0:4096], C[:, 4096:8192]]
                with nc.Block() as blk:
                    @blk.sync
                    def _(s):
                        s.wait_ge(SEM["SST"], cnt["SST"])
                g0, d0, s0 = cnt["GLD"], cnt["DV"], cnt["SST"]
                p00, a00, sst00 = cnt["PE1"], cnt["AC"], cnt["SST"]
                ev_ff = {}
                with nc.Block() as blk:
                    @blk.gpsimd
                    def _(g):
                        for n in range(NCH):
                            if n >= 2:
                                g.wait_ge(SEM["DV"], d0 + 2 * (n - 2) + 1)
                            dma(g, "FF%d" % (n % 2), Bs[n % 2], fq[n * 128:(n + 1) * 128, :])
                            dma(g, "FF%d" % (n % 2), fscb[n % 2][:], fsc[n * 128:(n + 1) * 128, :])
                            ev_ff[n] = dma(g, "FF%d" % (n % 2), fzpb[n % 2][:], fzp[n * 128:(n + 1) * 128, :])
                    @blk.vector
                    def _(v):
                        for n in range(NCH):
                            v.wait_ge(SEM["FF%d" % (n % 2)], ev_ff[n])
                            if n >= 2:
                                v.wait_ge(SEM["SST"], s0 + (n - 1) * 32)
                            ts = None
                            for gg in range(32):
                                ts = v.tensor_scalar(Cd[n % 2][:, gg * 128:(gg + 1) * 128],
                                                     Bs[n % 2][:, gg * 128:(gg + 1) * 128],
                                                     fscb[n % 2][:, gg:gg + 1], fzpb[n % 2][:, gg:gg + 1],
                                                     op0=OP.mult, op1=OP.add)
                            inc(ts, "DV")
                            if n >= 1:
                                v.wait_ge(SEM["AC"], a00 + 3 * (n - 1) + 1)
                            rd = v.tensor_reduce(wcol32[:], Cd[n % 2], axis=mybir.AxisListType.X,
                                                 op=OP.add)
                            inc(rd, "DV")
                    @blk.scalar
                    def _(a):
                        for n in range(NCH):
                            a.wait_ge(SEM["DV"], d0 + 2 * n + 2)
                            if n >= 1:
                                a.wait_ge(SEM["PE1"], p00 + n)
                            cp = a.copy(wcol[:, 0:1], wcol32[:])
                            inc(cp, "AC")
                            z = a.activation(wcol[:, 1:2], wcol32[:], AF.Copy, bias=0.0, scale=0.0)
                            inc(z, "AC")
                            a.wait_ge(SEM["PE1"], p00 + n + 1)
                            if n >= 1:
                                a.wait_ge(SEM["SST"], sst00 + n * 32)
                            cp2 = a.copy(wrow[:], ws_ps[0:1, :])
                            inc(cp2, "AC")
                    @blk.tensor
                    def _(t):
                        for n in range(NCH):
                            t.wait_ge(SEM["AC"], a00 + 3 * n + 2)
                            mm = t.matmul(ws_ps, wcol[:], ident[:], start=True, stop=True)
                            inc(mm, "PE1")
                    @blk.sync
                    def _(s):
                        for n in range(NCH):
                            s.wait_ge(SEM["DV"], d0 + 2 * n + 1)
                            dma(s, "SST", fcache[n], Cd[n % 2])
                            s.wait_ge(SEM["AC"], a00 + 3 * n + 3)
                            dma(s, "SST", wsums[0:1, 6144 + n * 128:6144 + (n + 1) * 128], wrow[:])
                if "p0ffn" in taps:
                    dcol = mkdbg("ffncol", [128, 1])
                    dcol2 = mkdbg("ffncol2", [128, 2])
                    drow = mkdbg("ffnrow", [1, 128])
                    with nc.Block() as blk2:
                        @blk2.gpsimd
                        def _(g):
                            dma(g, "GLD", dcol[:], wcol32[:])
                            dma(g, "GLD", dcol2[:], wcol[:])
                            dma(g, "GLD", drow[:], wrow[:])

        def p0_vb():
            with ExitStack() as ph:
                pst = [PSB[0][:, 0:128], PSB[1][:, 0:128]]
                for dvh in range(2):
                    with nc.Block() as blk:
                        @blk.sync
                        def _(s):
                            s.wait_ge(SEM["GLD"], cnt["GLD"])
                            s.wait_ge(SEM["SST"], cnt["SST"])
                            dma(s, "SLD", B_stage4[:, :, :],
                                bcache[5120 + dvh * 512:5120 + (dvh + 1) * 512, :]
                                .rearrange("(j p) e -> p j e", p=128))
                    p0, a0, s0 = cnt["PE1"], cnt["AC"], cnt["SLD"]
                    with nc.Block() as blk:
                        @blk.tensor
                        def _(t):
                            t.wait_ge(SEM["SLD"], s0)
                            for jj in range(4):
                                for c in range(NCH):
                                    kk = jj * 32 + c
                                    if kk >= 2:
                                        t.wait_ge(SEM["AC"], a0 + kk - 1)
                                    tp = t.transpose(pst[c % 2],
                                                     B_stage4[:, jj, c * 128:(c + 1) * 128], ident[:])
                                    inc(tp, "PE1")
                        @blk.scalar
                        def _(a):
                            for jj in range(4):
                                for c in range(NCH):
                                    a.wait_ge(SEM["PE1"], p0 + jj * 32 + c + 1)
                                    cp = a.copy(C[:, c * 512 + jj * 128:c * 512 + (jj + 1) * 128],
                                                pst[c % 2])
                                    inc(cp, "AC")
                        @blk.sync
                        def _(s):
                            s.wait_ge(SEM["AC"], a0 + 128)
                            dma(s, "SST", vbcache[:, dvh, :, :], C_deq[:, :, :])

        _cs_ctr = [0]
        def p0_colsum(srcrows, NB, ws_base):
            _cs_ctr[0] += 1
            with ExitStack() as ph:
                wcol32 = ph.enter_context(nc.sbuf_tensor(f"p0d_wc32_{_cs_ctr[0]}", [128, 1], F32))
                wcol = ph.enter_context(nc.sbuf_tensor(f"p0d_wc_{_cs_ctr[0]}", [128, 2], BF16))
                wrow = ph.enter_context(nc.sbuf_tensor(f"p0d_wr_{_cs_ctr[0]}", [1, 128], BF16))
                ws_ps = PS8[2][0:2, 0:128]
                stg2 = [B[:, 0:4096], B[:, 4096:8192]]
                slotn = ["STG", "FF0"]
                d0, p0, a0 = cnt["DV"], cnt["PE1"], cnt["AC"]
                sst_b0 = cnt["SST"]
                ev_st = {}
                with nc.Block() as blk:
                    @blk.sync
                    def _(s):
                        s.wait_ge(SEM["SST"], cnt["SST"])
                        s.wait_ge(SEM["WCC"], 32)
                        for nb in range(NB):
                            if nb >= 2:
                                s.wait_ge(SEM["DV"], d0 + nb - 1)
                            ev_st[nb] = dma(s, slotn[nb % 2], stg2[nb % 2], srcrows(nb))
                            if nb >= 1:
                                s.wait_ge(SEM["AC"], a0 + 3 * nb)
                                dma(s, "SST",
                                    wsums[0:1, ws_base + (nb - 1) * 128:ws_base + nb * 128],
                                    wrow[:])
                        s.wait_ge(SEM["AC"], a0 + 3 * NB)
                        dma(s, "SST", wsums[0:1, ws_base + (NB - 1) * 128:ws_base + NB * 128],
                            wrow[:])
                    @blk.vector
                    def _(v):
                        for nb in range(NB):
                            v.wait_ge(SEM[slotn[nb % 2]], ev_st[nb])
                            if nb >= 1:
                                v.wait_ge(SEM["AC"], a0 + 3 * (nb - 1) + 1)
                            rd = v.tensor_reduce(wcol32[:], stg2[nb % 2], axis=mybir.AxisListType.X,
                                                 op=OP.add)
                            inc(rd, "DV")
                    @blk.scalar
                    def _(a):
                        for nb in range(NB):
                            a.wait_ge(SEM["DV"], d0 + nb + 1)
                            if nb >= 1:
                                a.wait_ge(SEM["PE1"], p0 + nb)
                            cp = a.copy(wcol[:, 0:1], wcol32[:])
                            inc(cp, "AC")
                            z = a.activation(wcol[:, 1:2], wcol32[:], AF.Copy, bias=0.0, scale=0.0)
                            inc(z, "AC")
                            a.wait_ge(SEM["PE1"], p0 + nb + 1)
                            if nb >= 1:
                                a.wait_ge(SEM["SST"], sst_b0 + nb * 16)
                            cp2 = a.copy(wrow[:], ws_ps[0:1, :])
                            inc(cp2, "AC")
                    @blk.tensor
                    def _(t):
                        for nb in range(NB):
                            t.wait_ge(SEM["AC"], a0 + 3 * nb + 2)
                            mm = t.matmul(ws_ps, wcol[:], ident[:], start=True, stop=True)
                            inc(mm, "PE1")

        def phase_stats(first, ffn):
            with ExitStack() as ph:
                sums = [PS8[i][0:1, :] for i in range(4)]
                sqs = [PS8[4 + i][0:1, :] for i in range(4)]
                sqb = [C[:, 8192:10240], C[:, 10240:12288]]
                if not first:
                    with nc.Block() as blk:
                        @blk.sync
                        def _(s):
                            s.wait_ge(SEM["SST"], cnt["SST"])
                            for qd in range(4):
                                dma(s, "SLD",
                                    A[:, qd * 8 * T:(qd + 1) * 8 * T].rearrange("p (c t) -> p c t", t=T),
                                    x_home[qd * 8:(qd + 1) * 8].rearrange("c p t -> p c t"))
                d0, p0 = cnt["DV"], cnt["PE1"]
                sld_x = cnt["SLD"]
                with nc.Block() as blk:
                    @blk.vector
                    def _(v):
                        if first:
                            v.wait_ge(SEM["XLD"], GLD_X)
                        else:
                            v.wait_ge(SEM["SLD"], sld_x)
                        for c in range(NCH):
                            if c >= 2:
                                v.wait_ge(SEM["PE1"], p0 + c - 1)
                            tt = v.tensor_tensor(sqb[c % 2][:], AC_(c), AC_(c), op=OP.mult)
                            inc(tt, "DV")
                    @blk.tensor
                    def _(t):
                        for c in range(NCH):
                            t.wait_ge(SEM["DV"], d0 + c + 1)
                            mm = None
                            for ti in range(4):
                                t.matmul(sums[ti], ones_col[:], AC_(c, ti * TW, TW),
                                         start=(c == 0), stop=(c == NCH - 1))
                                mm = t.matmul(sqs[ti], ones_col[:], sqb[c % 2][:, ti * TW:(ti + 1) * TW],
                                              start=(c == 0), stop=(c == NCH - 1))
                            inc(mm, "PE1")
                pe_end = cnt["PE1"]
                d1 = cnt["DV"]
                with nc.Block() as blk:
                    @blk.vector
                    def _(v):
                        v.wait_ge(SEM["PE1"], pe_end)
                        for ti in range(4):
                            sl = slice(ti * TW, (ti + 1) * TW)
                            r_mean = C_strow[:, ti * 512:ti * 512 + 512]
                            r_var = C_strow[:, 2048 + ti * 512:2048 + ti * 512 + 512]
                            r_tmp = C[0:1, 8192:9216].bitcast(F32)
                            v.tensor_scalar(r_mean, sums[ti], 1.0 / D, None, op0=OP.mult)
                            v.tensor_scalar(r_var, sqs[ti], 1.0 / D, None, op0=OP.mult)
                            v.scalar_tensor_tensor(r_tmp, r_mean, -1.0, r_mean, op0=OP.mult, op1=OP.mult)
                            v.tensor_tensor(r_var, r_var, r_tmp, op=OP.add)
                            tc = v.tensor_scalar(C_meanrow[:, sl], r_mean, -1.0, None, op0=OP.mult)
                            inc(tc, "DV")
                    @blk.scalar
                    def _(a):
                        for ti in range(4):
                            sl = slice(ti * TW, (ti + 1) * TW)
                            a.wait_ge(SEM["DV"], d1 + ti + 1)
                            r_var = C_strow[:, 2048 + ti * 512:2048 + ti * 512 + 512]
                            a.activation(r_var, r_var, AF.Ln, bias=cb_eps[0:1, :], scale=1.0)
                            ac = a.activation(C_rstdrow[:, sl], r_var, AF.Exp, bias=0.0, scale=-0.5)
                            inc(ac, "AC")
                a1 = cnt["AC"]
                with nc.Block() as blk:
                    if not ffn:
                        pcol = None
                        p1 = cnt["PE1"]
                        pcol_ps = PS8[0][:, 0:16]
                        @blk.tensor
                        def _(t):
                            t.wait_ge(SEM["AC"], a1)
                            mm = None
                            for tau in range(16):
                                mm = t.matmul(pcol_ps[:, tau:tau + 1],
                                              C_rstdrow[:, tau * 128:(tau + 1) * 128],
                                              ones_row[0:1, 0:1], start=True, stop=True)
                            inc(mm, "PE1")
                        @blk.vector
                        def _(v):
                            v.wait_ge(SEM["PE1"], p1 + 1)
                            tc = v.tensor_copy(rstdcols[:], pcol_ps)
                            inc(tc, "DV")
                    else:
                        pass


        def phase_tables(ws_loads):
            with ExitStack() as ph:
                pb = [PS8[0], PS8[1]]
                s0 = cnt["SLD"]
                with nc.Block() as blk:
                    @blk.sync
                    def _(s):
                        dma(s, "SLD", C_cos, cosb_h[:])
                        dma(s, "SLD", C_sin, sinb_h[:])
                        for dst, srcsl in ws_loads:
                            dma(s, "SLD", dst, srcsl)
                sld_all = cnt["SLD"]
                p1, d0 = cnt["PE1"], cnt["DV"]
                with nc.Block() as blk:
                    @blk.tensor
                    def _(t):
                        for i in range(8):
                            ti = i % 4
                            if i >= 2:
                                t.wait_ge(SEM["DV"], d0 + i - 1)
                            mm = t.matmul(pb[i % 2][:], ones_row[:],
                                          C_rstdrow[:, ti * TW:(ti + 1) * TW],
                                          start=True, stop=True)
                            inc(mm, "PE1")
                    @blk.vector
                    def _(v):
                        v.wait_ge(SEM["SLD"], sld_all)
                        for i in range(8):
                            ti = i % 4
                            v.wait_ge(SEM["PE1"], p1 + i + 1)
                            if i < 4:
                                tgt = C_cos[:, ti * TW:(ti + 1) * TW]
                            else:
                                tgt = C_sin[:, ti * TW:(ti + 1) * TW]
                            tt = v.tensor_tensor(tgt, pb[i % 2][:], tgt, op=OP.mult)
                            inc(tt, "DV")

        def gemm_T(name, cache_view, n_list, band_bufs, epilogue, dv_per_n,
                   sst_hook=None, rank1=None, pre_dv_wait=None):
            NL = len(n_list)
            p0, d0 = cnt["PE1"], cnt["DV"]
            sst_pre = cnt["SST"]
            ev_bd = {}
            with nc.Block() as blk:
                @blk.scalar
                def _(a):
                    a.wait_ge(SEM["SST"], sst_pre)
                    a.wait_ge(SEM["WCC"], 32)
                    for i in range(NL):
                        if i >= 2:
                            a.wait_ge(SEM["PE1"], p0 + i - 1)
                        ev_bd[i] = dma_t(a, "BD%d" % (i % 2), band_bufs[i % 2][:], cache_view[n_list[i]])
                @blk.tensor
                def _(t):
                    for i, n in enumerate(n_list):
                        t.wait_ge(SEM["BD%d" % (i % 2)], ev_bd[i])
                        if i >= 2:
                            t.wait_ge(SEM["DV"], d0 + dv_per_n * (i - 1))
                        ps4 = PS8[(i % 2) * 4:(i % 2) * 4 + 4]
                        mm = None
                        for c in range(NCH):
                            for ti in range(4):
                                mm = t.matmul(ps4[ti][:], band_bufs[i % 2][:, c, :],
                                              AC_(c, ti * TW, TW),
                                              start=(c == 0), stop=(c == NCH - 1 and rank1 is None))
                        if rank1 is not None:
                            for ti in range(4):
                                mm = rank1(t, i, n, ps4[ti], ti)
                        inc(mm, "PE1")
                @blk.vector
                def _(v):
                    for i, n in enumerate(n_list):
                        v.wait_ge(SEM["PE1"], p0 + i + 1)
                        if pre_dv_wait is not None:
                            pre_dv_wait(v, i, n)
                        ps4 = PS8[(i % 2) * 4:(i % 2) * 4 + 4]
                        epilogue(v, i, n, ps4)
                if sst_hook is not None:
                    @blk.sync
                    def _(s):
                        for i, n in enumerate(n_list):
                            sst_hook(s, i, n, d0)

        def phase_s4_qk(layer, wsq_row, wsk_row):
            if layer == 0:
                qcv, kcv = qcache, kcache
                q_nl, k_nl = list(range(NH)), list(range(NKV))
            else:
                bv = bcache.rearrange("(n p) e -> n p e", p=128)
                qcv, kcv = bv, bv
                q_nl, k_nl = list(range(NH)), list(range(32, 32 + NKV))

            def rope_ep(dst_of, small):
                def ep(v, i, n, ps4):
                    for ti in range(4):
                        sl = slice(ti * TW, (ti + 1) * TW)
                        dst = dst_of(i, ti)
                        v.tensor_tensor(C_rt2[:], ps4[ti][:], C_cos[:, sl], op=OP.mult)
                        v.tensor_tensor(C_rt1[0:64, :], ps4[ti][64:128, :], C_sin[0:64, sl], op=OP.mult)
                        v.tensor_tensor(C_rt1[64:128, :], ps4[ti][0:64, :], C_sin[64:128, sl], op=OP.mult)
                        tt = v.tensor_tensor(dst, C_rt2[:], C_rt1[:], op=OP.add)
                        inc(tt, "DV")
                return ep

            qs0 = {0: cnt["QS0"], 1: cnt["QS1"]}

            def q_ep(v, i, n, ps4):
                for ti in range(4):
                    gidx = 4 * i + ti
                    sl = slice(ti * TW, (ti + 1) * TW)
                    if gidx >= 2:
                        v.wait_ge(SEM["QS%d" % (gidx % 2)],
                                  qs0[gidx % 2] + ((gidx - 2) // 2 + 1) * 16)
                    v.tensor_tensor(C_rt2[:], ps4[ti][:], C_cos[:, sl], op=OP.mult)
                    v.tensor_tensor(C_rt1[0:64, :], ps4[ti][64:128, :], C_sin[0:64, sl], op=OP.mult)
                    v.tensor_tensor(C_rt1[64:128, :], ps4[ti][0:64, :], C_sin[64:128, sl], op=OP.mult)
                    tt = v.tensor_tensor(C_qsm[gidx % 2], C_rt2[:], C_rt1[:], op=OP.add)
                    inc(tt, "DV")

            def q_sst(s, i, n, d0):
                for ti in range(4):
                    gidx = 4 * i + ti
                    s.wait_ge(SEM["DV"], d0 + gidx + 1)
                    dma(s, "QS%d" % (gidx % 2), qspill[n_q_list[i]][:, ti * TW:(ti + 1) * TW],
                        C_qsm[gidx % 2])

            n_q_list = q_nl

            def q_rank1(t, i, n, psu, ti):
                return t.matmul(psu[:], wsq_row[0:1, i * 128:(i + 1) * 128],
                                C_meanrow[:, ti * TW:(ti + 1) * TW], start=False, stop=True)

            gemm_T(f"s4q{layer}", qcv, q_nl, C_band, q_ep, dv_per_n=4, sst_hook=q_sst,
                   rank1=q_rank1)

            def k_ep(v, i, n, ps4):
                for ti in range(4):
                    sl = slice(ti * TW, (ti + 1) * TW)
                    v.tensor_tensor(C_rt2[:], ps4[ti][:], C_cos[:, sl], op=OP.mult)
                    v.tensor_tensor(C_rt1[0:64, :], ps4[ti][64:128, :], C_sin[0:64, sl], op=OP.mult)
                    v.tensor_tensor(C_rt1[64:128, :], ps4[ti][0:64, :], C_sin[64:128, sl], op=OP.mult)
                    tt = v.tensor_tensor(B[:, i * T + ti * TW:i * T + (ti + 1) * TW],
                                         C_rt2[:], C_rt1[:], op=OP.add)
                    inc(tt, "DV")
            def k_rank1(t, i, n, psu, ti):
                return t.matmul(psu[:], wsk_row[0:1, i * 128:(i + 1) * 128],
                                C_meanrow[:, ti * TW:(ti + 1) * TW], start=False, stop=True)

            gemm_T(f"s4k{layer}", kcv, k_nl, C_band, k_ep, dv_per_n=4, rank1=k_rank1)

        def phase_s4_v(layer, ws_off_v):
            vc = vcache if layer == 0 else vbcache
            with ExitStack() as ph:
                wsv_row = ph.enter_context(nc.sbuf_tensor(_un(f"ws_v{layer}"), [1, 1024], BF16))
                nmean = ph.enter_context(nc.sbuf_tensor(_un("s4v_nm"), [1, 2048], BF16))
                vband = ph.enter_context(nc.sbuf_tensor(_un("s4v_bd"), [128, 8, 512], BF16))
                s0, p0, d0 = cnt["SLD"], cnt["PE1"], cnt["DV"]
                qs_tot0, qs_tot1 = cnt["QS0"], cnt["QS1"]
                with nc.Block() as blk:
                    @blk.sync
                    def _(s):
                        dma(s, "SLD", wsv_row[:], wsums[0:1, ws_off_v:ws_off_v + 1024])
                        kq = 0
                        for dvh in range(2):
                            for tg in range(2):
                                grp = dvh * 2 + tg
                                for iq in range(4):
                                    if kq >= 1:
                                        pv = p0 + (kq // 4) * 5 + (kq % 4)
                                        s.wait_ge(SEM["PE1"], pv)
                                    dma(s, "SLD", vband[:], vc[:, dvh, iq * 8:(iq + 1) * 8, :])
                                    kq += 1
                    @blk.tensor
                    def _(t):
                        kq = 0
                        for dvh in range(2):
                            for tg in range(2):
                                grp = dvh * 2 + tg
                                for iq in range(4):
                                    t.wait_ge(SEM["SLD"], s0 + (kq + 2) * 16)
                                    if iq == 0 and grp >= 1:
                                        t.wait_ge(SEM["DV"], d0 + 1 + grp * 8)
                                    mm = None
                                    for c8 in range(8):
                                        c = iq * 8 + c8
                                        for tau in range(tg * 8, tg * 8 + 8):
                                            mm = t.matmul(PS8[tau % 8][:],
                                                          AC_(c, tau * 128, 128),
                                                          vband[:, c8, :],
                                                          start=(iq == 0 and c8 == 0),
                                                          stop=False)
                                    inc(mm, "PE1")
                                    kq += 1
                                if dvh == 0 and tg == 0:
                                    t.wait_ge(SEM["DV"], d0 + 1)
                                for tau in range(tg * 8, tg * 8 + 8):
                                    mm = t.matmul(PS8[tau % 8][:],
                                                  nmean[:, tau * 128:(tau + 1) * 128],
                                                  wsv_row[0:1, dvh * 512:(dvh + 1) * 512],
                                                  start=False, stop=True)
                                inc(mm, "PE1")
                    @blk.vector
                    def _(v):
                        v.wait_ge(SEM["QS0"], qs_tot0)
                        v.wait_ge(SEM["QS1"], qs_tot1)
                        nm = v.tensor_copy(nmean[:], C_meanrow[:])
                        inc(nm, "DV")
                        for dvh in range(2):
                            for tg in range(2):
                                grp = dvh * 2 + tg
                                v.wait_ge(SEM["PE1"], p0 + (grp + 1) * 5)
                                for tau in range(tg * 8, tg * 8 + 8):
                                    ts = v.tensor_scalar(
                                        C[:, tau * 1024 + dvh * 512:tau * 1024 + dvh * 512 + 512],
                                        PS8[tau % 8][:], rstdcols[:, tau:tau + 1], None, op0=OP.mult)
                                    inc(ts, "DV")

        def phase_s6(layer):
            with ExitStack() as ph:
                qh = [ph.enter_context(nc.sbuf_tensor(_un(f"s6_q{i}"), [128, T], BF16)) for i in range(2)]
                pt = [ph.enter_context(nc.sbuf_tensor(_un(f"s6_p{i}"), [128, 512], BF16)) for i in range(2)]
                rb_sb = ph.enter_context(nc.sbuf_tensor(_un("s6_rb"), [128, 512], BF16))
                rrow_bf = ph.enter_context(nc.sbuf_tensor(_un("s6_rrb"), [1, 512], BF16))
                maskt = ph.enter_context(nc.sbuf_tensor(_un("s6_mask"), [128, 896], BF16))
                sc_ps = [PS8[0][:], PS8[1][:]]
                cx_ps = [PS8[2][:], PS8[3][:]]
                dn_ps = [PS8[4][0:1, :], PS8[5][0:1, :]]
                rb_ps = PS8[6][:]

                ev_scores, ev_exp, ev_mask, ev_pt = {}, {}, {}, {}
                ev_grp, ev_recip, ev_rb, ev_rbe, ev_cxe, ev_qld = {}, {}, {}, {}, {}, {}
                pe1, pe2, dv, ac, sld = cnt["PE1"], cnt["PE2"], cnt["DV"], cnt["AC"], cnt["SLD"]
                for h in range(NH):
                    sld += 16
                    ev_qld[h] = sld
                    for tau in range(4):
                        nk = 4 * tau + 4
                        for kc in range(nk):
                            pe1 += 1; ev_scores[(h, tau, kc)] = pe1
                            ac += 1; ev_exp[(h, tau, kc)] = ac
                            if kc >= 4 * tau:
                                dv += 1; ev_mask[(h, tau, kc)] = dv
                            pe2 += 1; ev_pt[(h, tau, kc)] = pe2
                        ev_grp[(h, tau)] = pe2
                        dv += 1; ev_recip[(h, tau)] = dv
                        pe2 += 1; ev_rb[(h, tau)] = pe2
                        ac += 1; ev_rbe[(h, tau)] = ac
                        dv += 1; ev_cxe[(h, tau)] = dv
                GV = {h * 4 + tau: (h, tau) for h in range(NH) for tau in range(4)}
                qs_tot0, qs_tot1 = cnt["QS0"], cnt["QS1"]
                ev_qh = {}
                ev_mask_ld = [0]
                with nc.Block() as blk:
                    @blk.sync
                    def _(s):
                        s.wait_ge(SEM["QS0"], qs_tot0)
                        s.wait_ge(SEM["QS1"], qs_tot1)
                        ev_mask_ld[0] = dma(s, "SLD", maskt[:], maskm[:])
                        for h in range(NH):
                            if h >= 2:
                                s.wait_ge(SEM["PE1"], ev_scores[(h - 2, 3, 15)])
                            ev_qh[h] = dma(s, "QH%d" % (h % 2), qh[h % 2][:], qspill[h])
                    @blk.tensor
                    def _(t):
                        for h in range(NH):
                            j = h // 4
                            t.wait_ge(SEM["QH%d" % (h % 2)], ev_qh[h])
                            for tau in range(4):
                                g = h * 4 + tau
                                nk = 4 * tau + 4
                                if g >= 2:
                                    t.wait_ge(SEM["DV"], ev_recip[GV[g - 2]])
                                    t.wait_ge(SEM["DV"], ev_cxe[GV[g - 2]])
                                if g >= 1:
                                    t.wait_ge(SEM["AC"], ev_rbe[GV[g - 1]])

                                def mm_scores(kc):
                                    if (h, tau, kc - 2) in ev_exp:
                                        t.wait_ge(SEM["AC"], ev_exp[(h, tau, kc - 2)])
                                    mm = t.matmul(sc_ps[kc % 2],
                                                  B[:, j * T + kc * 128:j * T + kc * 128 + 128],
                                                  qh[h % 2][:, tau * TW:(tau + 1) * TW],
                                                  start=True, stop=True)
                                    inc(mm, "PE1")

                                def mm_ctxden(kc):
                                    t.wait_ge(SEM["AC"], ev_exp[(h, tau, kc)])
                                    if (h, tau, kc) in ev_mask:
                                        t.wait_ge(SEM["DV"], ev_mask[(h, tau, kc)])
                                    t.matmul(cx_ps[g % 2],
                                             C[:, kc * 1024 + j * 128:kc * 1024 + (j + 1) * 128],
                                             pt[kc % 2][:],
                                             start=(kc == 0), stop=(kc == nk - 1))
                                    mm = t.matmul(dn_ps[g % 2], ones_col[:], pt[kc % 2][:],
                                                  start=(kc == 0), stop=(kc == nk - 1))
                                    inc(mm, "PE2")

                                mm_scores(0)
                                for kc in range(nk):
                                    if kc + 1 < nk:
                                        mm_scores(kc + 1)
                                    mm_ctxden(kc)
                                t.wait_ge(SEM["DV"], ev_recip[(h, tau)])
                                mm = t.matmul(rb_ps, ones_row[:], rrow_bf[:], start=True, stop=True)
                                inc(mm, "PE2")
                    @blk.scalar
                    def _(a):
                        for h in range(NH):
                            for tau in range(4):
                                g = h * 4 + tau
                                nk = 4 * tau + 4
                                for kc in range(nk):
                                    a.wait_ge(SEM["PE1"], ev_scores[(h, tau, kc)])
                                    if (h, tau, kc - 2) in ev_pt:
                                        a.wait_ge(SEM["PE2"], ev_pt[(h, tau, kc - 2)])
                                    ex = a.activation(pt[kc % 2][:], sc_ps[kc % 2], AF.Exp,
                                                      bias=cb_negC[:], scale=SM_SCALE)
                                    inc(ex, "AC")
                                a.wait_ge(SEM["PE2"], ev_rb[(h, tau)])
                                if g >= 1:
                                    a.wait_ge(SEM["DV"], ev_cxe[GV[g - 1]])
                                cp = a.copy(rb_sb[:], rb_ps)
                                inc(cp, "AC")
                    @blk.vector
                    def _(v):
                        v.wait_ge(SEM["SLD"], ev_mask_ld[0])
                        for h in range(NH):
                            for tau in range(4):
                                g = h * 4 + tau
                                nk = 4 * tau + 4
                                for kc in range(4 * tau, nk):
                                    off = kc * 128 - tau * 512
                                    v.wait_ge(SEM["AC"], ev_exp[(h, tau, kc)])
                                    mk = v.tensor_tensor(pt[kc % 2][:], pt[kc % 2][:],
                                                         maskt[:, 384 - off:384 - off + 512],
                                                         op=OP.mult)
                                    inc(mk, "DV")
                                v.wait_ge(SEM["PE2"], ev_grp[(h, tau)])
                                with nc.allow_low_precision(reason="recip feeds bf16 broadcast anyway"):
                                    rc = v.reciprocal(rrow_bf[:], dn_ps[g % 2])
                                inc(rc, "DV")
                                v.wait_ge(SEM["AC"], ev_rbe[(h, tau)])
                                ce = v.tensor_tensor(
                                    A[:, h * T + tau * TW:h * T + (tau + 1) * TW],
                                    cx_ps[g % 2], rb_sb[:], op=OP.mult)
                                inc(ce, "DV")
                assert cnt["PE1"] == ev_scores[(NH - 1, 3, 15)]
                assert cnt["PE2"] == ev_rb[(NH - 1, 3)]
                assert cnt["DV"] == ev_cxe[(NH - 1, 3)]
                assert cnt["AC"] == ev_rbe[(NH - 1, 3)]

        def phase_xhome_init():
            with nc.Block() as blk:
                @blk.sync
                def _(s):
                    for i in range(NCH):
                        dma(s, "SST", x_home[i], AC_(i))

        def phase_s8x(oc):
            d0, s0 = cnt["DV"], cnt["SLD"]

            def ep(v, i, n, ps4):
                v.wait_ge(SEM["SLD"], s0 + (i + 1) * 16)
                last = None
                for ti in range(4):
                    sl = slice(ti * TW, (ti + 1) * TW)
                    last = v.tensor_tensor(B_xold[i % 3][:, sl], ps4[ti][:],
                                           B_xold[i % 3][:, sl], op=OP.add)
                inc(last, "DV")

            def sst(s, i, n, d0_):
                dma(s, "SLD", B_xold[i % 3][:], x_home[i])
                s.wait_ge(SEM["DV"], d0 + i + 1)
                dma(s, "SST", x_home[i], B_xold[i % 3][:])

            gemm_T(_un("s8x"), oc, list(range(NCH)), B_band, ep, dv_per_n=1, sst_hook=sst)

        def phase_ffn(final):
            with ExitStack() as ph:
                wsf = ph.enter_context(nc.sbuf_tensor(_un("ff_wsf"), [1, 4096], BF16))
                if final:
                    stgf = [ph.enter_context(nc.sbuf_tensor(_un(f"f3_sg{i}"), [128, 512], F32))
                            for i in range(2)]
                    stgo = B_stgo3
                else:
                    stgf = B_stgf2
                    stgo = B_stgo2
                pb = [PS8[0], PS8[1]]
                g0 = cnt["GLD"]
                with nc.Block() as blk:
                    @blk.gpsimd
                    def _(g):
                        for n in range(NCH):
                            dma(g, "GLD", biascols[:, n:n + 1],
                                fbias[0:1, n * 128:(n + 1) * 128]
                                .rearrange("o (p f) -> (o p) f", f=1))
                        dma(g, "GLD", wsf[:], wsums[0:1, 6144:10240])
                p0, d0 = cnt["PE1"], cnt["DV"]
                with nc.Block() as blk:
                    @blk.tensor
                    def _(t):
                        for ti in range(4):
                            if ti >= 2:
                                t.wait_ge(SEM["DV"], d0 + ti - 1)
                            mm = t.matmul(pb[ti % 2][:], ones_row[:],
                                          C_rstdrow[:, ti * TW:(ti + 1) * TW], start=True, stop=True)
                            inc(mm, "PE1")
                    @blk.vector
                    def _(v):
                        for ti in range(4):
                            v.wait_ge(SEM["PE1"], p0 + ti + 1)
                            tc = v.tensor_copy(C_rstdb[:, ti * TW:(ti + 1) * TW], pb[ti % 2][:])
                            inc(tc, "DV")
                st0 = cnt["SST"]

                def rank1(t, i, n, psu, ti):
                    return t.matmul(psu[:], wsf[0:1, n * 128:(n + 1) * 128],
                                    C_meanrow[:, ti * TW:(ti + 1) * TW], start=False, stop=True)

                def ep(v, i, n, ps4):
                    v.wait_ge(SEM["GLD"], g0 + 33 * 16)
                    if i >= 2:
                        v.wait_ge(SEM["SST"], st0 + (i - 1) * 16)
                    last = None
                    for ti in range(4):
                        sl = slice(ti * TW, (ti + 1) * TW)
                        v.tensor_tensor(stgf[ti % 2][:], ps4[ti][:], C_rstdb[:, sl], op=OP.mult)
                        v.tensor_scalar(stgf[ti % 2][:], stgf[ti % 2][:],
                                        biascols[:, n:n + 1], None, op0=OP.add)
                        last = v.tensor_tensor(stgo[i % 2][:, sl], stgf[ti % 2][:],
                                               AC_(n, ti * TW, TW), op=OP.add)
                    inc(last, "DV")

                d2 = cnt["DV"]

                def sst2(s, i, n, d0_):
                    s.wait_ge(SEM["DV"], d2 + i + 1)
                    if final:
                        dma(s, "SST", y_out[n * 128:(n + 1) * 128, :], stgo[i % 2][:])
                    else:
                        dma(s, "SST", x_home[n], stgo[i % 2][:])

                gemm_T("ff" + ("3" if final else "2"), fcache, list(range(NCH)),
                       B_band, ep, dv_per_n=1, sst_hook=sst2, rank1=rank1)

        def dump_fm(name):
            d = mkdbg(name, [D, T])
            with nc.Block() as blk:
                @blk.gpsimd
                def _(g):
                    for qd in range(4):
                        dma(g, "GLD",
                            d[qd * 1024:(qd + 1) * 1024, :].rearrange("(c p) t -> p c t", p=128),
                            A[:, qd * 8 * T:(qd + 1) * 8 * T].rearrange("p (c t) -> p c t", t=T))

        def dump_sbuf(name, ap, shape):
            d = mkdbg(name, list(shape))
            with nc.Block() as blk:
                @blk.gpsimd
                def _(g):
                    dma(g, "GLD", d[:], ap)

        def dump_dram(name, src, shape):
            d = mkdbg(name, list(shape), BF16)
            with nc.Block() as blk:
                @blk.sync
                def _(s):
                    s.wait_ge(SEM["SST"], cnt["SST"])
                    dma(s, "SST", d[:], src[:])

        def finish():
            with nc.Block() as blk:
                @blk.sync
                def _(s):
                    s.wait_ge(SEM["SST"], cnt["SST"])
                    s.wait_ge(SEM["GLD"], cnt["GLD"])
                    s.wait_ge(SEM["SLD"], cnt["SLD"])
                    s.wait_ge(SEM["ALD"], cnt["ALD"])

        p0_attn_a()
        p0_ffn()
        p0_vb()
        p0_colsum(lambda nb: bcache[nb * 128:(nb + 1) * 128, :], 48, 10240)
        p0_colsum(lambda nb: qcache[nb], 32, 0)
        p0_colsum(lambda nb: kcache[nb], 8, 4096)
        if stop_after == "p0":
            if "caches" in taps:
                dump_dram("ws", wsums, [1, 16384])
                dump_dram("oc", ocacheT, [NCH, 128, D])
                dump_dram("qc", qcache, [NH, 128, D])
                dump_dram("kc", kcache, [NKV, 128, D])
                dump_dram("vc", vcache, [128, 2, NCH, 512])
                dump_dram("fc", fcache, [NCH, 128, D])
                dump_dram("vbc", vbcache, [128, 2, NCH, 512])
            finish()
            return nc, dbg

        WSOFF = {0: (0, 4096, 5120), 1: (10240, 14336, 15360)}

        def attn_layer(layer):
            with ExitStack() as lh:
                oq, ok, ov = WSOFF[layer]
                phase_stats(first=(layer == 0), ffn=False)
                if layer == 0:
                    phase_xhome_init()
                with ExitStack() as lh2:
                    wsq = lh2.enter_context(nc.sbuf_tensor(_un(f"ws_q{layer}"), [1, 4096], BF16))
                    wsk = lh2.enter_context(nc.sbuf_tensor(_un(f"ws_k{layer}"), [1, 1024], BF16))
                    phase_tables([(wsq[:], wsums[0:1, oq:oq + 4096]),
                                  (wsk[:], wsums[0:1, ok:ok + 1024])])
                    phase_s4_qk(layer, wsq, wsk)
                phase_s4_v(layer, ov)
                if layer == 0 and stop_after == "qkv":
                    return True
                phase_s6(layer)
                if layer == 0 and stop_after == "ctx":
                    return True
                phase_s8x(ocacheT if layer == 0 else
                          ocache.rearrange("(n p) e -> n p e", p=128))
            return False

        stop = attn_layer(0)
        if stop_after == "qkv":
            if "qkv" in taps:
                dump_dram("q", qspill, [NH, 128, T])
                dump_sbuf("k", B[:], [128, 8 * T])
                dump_sbuf("v", C[:], [128, 16 * 1024])
            finish()
            return nc, dbg
        if stop_after == "ctx":
            if "ctx" in taps:
                dump_fm("ctx")
            finish()
            return nc, dbg
        if stop_after == "sub1":
            if "x1" in taps:
                dump_dram("x1", x_home, [NCH, 128, T])
            finish()
            return nc, dbg

        phase_stats(first=False, ffn=True)
        phase_ffn(final=False)
        if stop_after == "sub2":
            if "x2" in taps:
                dump_dram("x2", x_home, [NCH, 128, T])
            finish()
            return nc, dbg

        attn_layer(1)
        if stop_after == "sub3":
            if "x3" in taps:
                dump_dram("x3", x_home, [NCH, 128, T])
            finish()
            return nc, dbg

        phase_stats(first=False, ffn=True)
        phase_ffn(final=True)
        finish()
        return nc, dbg


from concourse.bass_utils import run_bass_kernel_spmd


def kernel(**inputs):
    x = np.asarray(inputs["L0_attn_a_hidden_states"])
    consts = host_constants()
    shared = {
        "qw_q": np.ascontiguousarray(np.asarray(inputs["L0_attn_a_qw_q"], dtype=np.int32)),
        "qw_k": np.ascontiguousarray(np.asarray(inputs["L0_attn_a_qw_k"], dtype=np.int32)),
        "qw_v": np.ascontiguousarray(np.asarray(inputs["L0_attn_a_qw_v"], dtype=np.int32)),
        "qw_o": np.ascontiguousarray(np.asarray(inputs["L0_attn_a_qw_o"], dtype=np.int32)),
        "sc_q": np.ascontiguousarray(np.asarray(inputs["L0_attn_a_scales_q"], dtype=np.float32)),
        "sc_k": np.ascontiguousarray(np.asarray(inputs["L0_attn_a_scales_k"], dtype=np.float32)),
        "sc_v": np.ascontiguousarray(np.asarray(inputs["L0_attn_a_scales_v"], dtype=np.float32)),
        "sc_o": np.ascontiguousarray(np.asarray(inputs["L0_attn_a_scales_o"], dtype=np.float32)),
        "w_in": np.ascontiguousarray(np.asarray(inputs["L0_attn_b_w_in"], dtype=np.float32)),
        "w_out": np.ascontiguousarray(np.asarray(inputs["L0_attn_b_w_out"], dtype=np.float32)),
        "fq": np.ascontiguousarray(np.asarray(inputs["L0_ffn_qweight"], dtype=np.int32).reshape(D, D)),
        "fsc": np.ascontiguousarray(np.asarray(inputs["L0_ffn_scales"], dtype=np.float32).reshape(D, 32)),
        "fzp": np.ascontiguousarray(np.asarray(inputs["L0_ffn_scaled_zeros"], dtype=np.float32).reshape(D, 32)),
        "fbias": np.ascontiguousarray(np.asarray(inputs["L0_ffn_bias"], dtype=np.float32).reshape(1, D)),
        "cosbase": consts["cosbase"],
        "sinbase": consts["sinbase"],
        "maskm": consts["maskm"],
        "ident": consts["ident"],
    }
    nc, _dbg = build_program()
    in_maps = []
    for b in range(8):
        m = dict(shared)
        m["x_fm"] = np.ascontiguousarray(np.asarray(x[b], dtype=np.float32).T)
        in_maps.append(m)
    res = run_bass_kernel_spmd(nc, in_maps, core_ids=list(range(8)))
    out = np.stack([np.ascontiguousarray(np.asarray(r["y"]).T) for r in res.results], axis=0)
    return out
